# version 08
# baseline (speedup 1.0000x reference)
import numpy as np
import ml_dtypes
import concourse.bass as bass
import concourse.mybir as mybir
from concourse.bass_utils import run_bass_kernel_spmd

F32 = mybir.dt.float32
BF16 = mybir.dt.bfloat16
AF = mybir.ActivationFunctionType
ALU = mybir.AluOpType
AX = mybir.AxisListType
BFNP = ml_dtypes.bfloat16

T = 4096
DM = 1024
NF = 8
DIP = 4256
DI = 2048
NCV = 17
NH = 32
NPAIR = 16
FFN = 4096
ST = 256
NST = 16
Q = 128
NE = 8
MI = 1408
NMI = 11
GU2 = 2816
NGU = 22
SI = 2816
NSI = 22
QT = 1024
NQ = 4
TT = 512
EPS = 1e-5

SB_SZ = 229376

DEBUG = False
PHASES = "full"
PREFIX = 0


class Buf:
    __slots__ = ("w", "rs", "name")

    def __init__(self, name=""):
        self.w = None
        self.rs = []
        self.name = name


class Bld:
    def __init__(self):
        self.ops = []

    def emit(self, eng, fn, reads=(), writes=(), deps=(), dma=False, tag=""):
        i = len(self.ops)
        dl = list(deps)
        for b in reads:
            if b.w is not None:
                dl.append((b.w, True))
        for b in writes:
            if b.w is not None:
                dl.append((b.w, False))
            for t in b.rs:
                dl.append((t, False))
        self.ops.append((eng, fn, dl, dma, tag))
        for b in reads:
            b.rs.append(i)
        for b in writes:
            b.w = i
            b.rs = []
        if not hasattr(self, "last"):
            self.last = {}
        self.last[eng] = i
        return i

    def all_tokens(self):
        return [(i, False) for i in getattr(self, "last", {}).values()]

    def finalize(self):
        semof = {"tensor": "pe", "vector": "ve", "scalar": "sc", "gpsimd": "gp"}
        n = len(self.ops)
        eng_of = [o[0] for o in self.ops]
        is_dma = [o[3] for o in self.ops]
        need = [None] * n
        depended = set()
        for i in range(n):
            eng = eng_of[i]
            nd = []
            for (j, raw) in self.ops[i][2]:
                if j == i:
                    continue
                same = eng_of[j] == eng and not is_dma[j] and not is_dma[i]
                if same and eng == "tensor":
                    continue
                if same and not raw:
                    continue
                nd.append(j)
                depended.add(j)
            need[i] = nd
        G = 4
        val = {}
        cnt = {}
        qidx = {}
        self.fence = {}
        qgrp_end = {}
        per_q_ops = {}
        for i in range(n):
            if is_dma[i]:
                q = eng_of[i]
                per_q_ops.setdefault(q, []).append(i)
        for q, ops in per_q_ops.items():
            g = -1
            gsz = 0
            prev = -10
            for i in ops:
                if i != prev + 1 or gsz >= G:
                    g += 1
                    gsz = 0
                gsz += 1
                prev = i
                s = f"dq_{q}_{g % 2}"
                cnt[s] = cnt.get(s, 0) + 16
                qidx[i] = (q, g, s)
            acc = {}
            for i in ops:
                q_, g, s = qidx[i]
                acc[s] = acc.get(s, 0) + 16
                qgrp_end[(q, g)] = (s, acc[s])
            seen = set()
            for i in ops:
                q_, g, s = qidx[i]
                if g not in seen:
                    seen.add(g)
                    if g >= 2:
                        self.fence[i] = qgrp_end[(q, g - 2)]
        for i in range(n):
            if is_dma[i]:
                q, g, s = qidx[i]
                val[i] = (s, qgrp_end[(q, g)][1], 16)
            elif i in depended:
                s = semof[eng_of[i]]
                cnt[s] = cnt.get(s, 0) + 1
                val[i] = (s, cnt[s], 1)
        wm = {}
        waits = [None] * n
        for i in range(n):
            m = wm.setdefault(eng_of[i], {})
            wl = []
            for j in sorted(set(need[i])):
                s, v, a = val[j]
                if m.get(s, 0) < v:
                    wl.append((s, v))
                    m[s] = v
            waits[i] = wl
        self.waits = waits
        self.val = val
        self.cnt = cnt

    def run_engine(self, eng_name, e, sems):
        for i, (eng, fn, deps, dma, tag) in enumerate(self.ops):
            if eng != eng_name:
                continue
            for (s, v) in self.waits[i]:
                e.wait_ge(sems[s], v)
            if dma and i in self.fence:
                fs, fv = self.fence[i]
                e.wait_ge(sems[fs], fv)
            inst = fn(e)
            if i in self.val:
                s, v, a = self.val[i]
                inst.then_inc(sems[s], 16 if dma else a)


def build_program():
    nc = bass.Bass("TRN2", target_bir_lowering=False, debug=False)
    dt = nc.dram_tensor

    xfm = dt("xfm", [DM, T], F32, kind="ExternalInput").ap()
    ipT = dt("ipT", [DM, DIP], F32, kind="ExternalInput").ap()
    opT = dt("opT", [DI, DM], F32, kind="ExternalInput").ap()
    f1T = dt("f1T", [DM, FFN], F32, kind="ExternalInput").ap()
    f2T = dt("f2T", [FFN, DM], F32, kind="ExternalInput").ap()
    guT = dt("guT", [NE, DM, GU2], F32, kind="ExternalInput").ap()
    dnT = dt("dnT", [NE, MI, DM], F32, kind="ExternalInput").ap()
    sgT = dt("sgT", [DM, SI], F32, kind="ExternalInput").ap()
    suT = dt("suT", [DM, SI], F32, kind="ExternalInput").ap()
    sdT = dt("sdT", [SI, DM], F32, kind="ExternalInput").ap()
    gTc = dt("gTc", [DM, NE], F32, kind="ExternalInput").ap()
    segc = dt("segc", [128, 8], F32, kind="ExternalInput").ap()
    lnc = dt("lnc", [128, 48], F32, kind="ExternalInput").ap()
    cwc = dt("cwc", [128, 68], F32, kind="ExternalInput").ap()
    cbc = dt("cbc", [128, 17], F32, kind="ExternalInput").ap()
    b1c = dt("b1c", [128, 32], F32, kind="ExternalInput").ap()
    b2c = dt("b2c", [128, 8], F32, kind="ExternalInput").ap()
    nwp = dt("nwp", [128, 16], F32, kind="ExternalInput").ap()
    dpr = dt("dpr", [128, 16], F32, kind="ExternalInput").ap()
    dtb = dt("dtb", [32, 1], F32, kind="ExternalInput").ap()
    alg = dt("alg", [32, 1], F32, kind="ExternalInput").ap()
    c_lt_b = dt("c_lt_b", [128, 128], BF16, kind="ExternalInput").ap()
    c_ut_b = dt("c_ut_b", [128, 128], BF16, kind="ExternalInput").ap()
    c_id_b = dt("c_id_b", [128, 128], BF16, kind="ExternalInput").ap()
    c_id_f = dt("c_id_f", [128, 128], F32, kind="ExternalInput").ap()
    c_sel = dt("c_sel", [32, 2048], BF16, kind="ExternalInput").ap()
    c_o1f = dt("c_o1f", [1, 128], F32, kind="ExternalInput").ap()
    c_ocb = dt("c_ocb", [128, 1], BF16, kind="ExternalInput").ap()
    c_oh8 = dt("c_oh8", [8, 1024], F32, kind="ExternalInput").ap()
    c_fc = dt("c_fc", [128, 8], F32, kind="ExternalInput").ap()

    dbgk = "ExternalOutput" if DEBUG else "Internal"
    d_xn = dt("d_xn", [DM, T], BF16, kind=dbgk).ap()
    d_xb = dt("d_xb", [DM, T], F32, kind=dbgk).ap()
    d_xc = dt("d_xc", [DM, T], BF16, kind=dbgk).ap()
    d_w = dt("d_w", [8, T], F32, kind=dbgk).ap()
    d_s = dt("d_s", [DM, T], F32, kind=dbgk).ap()
    d_acc = dt("d_acc", [DM, T], F32, kind=dbgk).ap()
    d_f1 = dt("d_f1", [DM, FFN], BF16, kind="Internal").ap()
    d_f2 = dt("d_f2", [FFN, DM], BF16, kind="Internal").ap()
    d_gu = dt("d_gu", [NE, DM, GU2], BF16, kind="Internal").ap()
    d_dn = dt("d_dn", [NE, MI, DM], BF16, kind="Internal").ap()
    d_sg = dt("d_sg", [DM, SI], BF16, kind="Internal").ap()
    d_su = dt("d_su", [DM, SI], BF16, kind="Internal").ap()
    d_sd = dt("d_sd", [SI, DM], BF16, kind="Internal").ap()
    ofm = dt("ofm", [DM, T], F32, kind="ExternalOutput").ap()
    d_dbg = dt("d_dbg", [8, ST], F32, kind=dbgk).ap()
    d_dbg2 = dt("d_dbg2", [128, ST], F32, kind=dbgk).ap()
    d_dbg3 = dt("d_dbg3", [128, ST], F32, kind=dbgk).ap()

    nalloc = [0]
    off_ = [0]

    def at(shape, dtype, off):
        nalloc[0] += 1
        per = int(np.prod(shape[1:])) * mybir.dt.size(dtype)
        assert off + per <= SB_SZ, f"sbuf overflow {off}+{per}"
        h = nc.alloc_sbuf_tensor_at(f"sb{nalloc[0]}", list(shape), dtype,
                                    offset=off, align_bytes=32)
        return h.ap()

    def take(shape, dtype, n=1):
        per = int(np.prod(shape[1:])) * mybir.dt.size(dtype)
        per = -(-per // 32) * 32
        r = []
        for _ in range(n):
            r.append(at(shape, dtype, off_[0]))
            off_[0] += per
        return r if n > 1 else r[0]

    cursor = SB_SZ - nc.sbuf_bytes_remaining
    off_[0] = 17408
    assert cursor <= off_[0], ("bass internal allocs reach", cursor)
    s_lt_b = take([128, 128], BF16)
    s_ut_b = take([128, 128], BF16)
    s_id_b = take([128, 128], BF16)
    s_id_f = take([128, 128], F32)
    s_sel = take([32, 2048], BF16)
    s_o1f = take([1, 128], F32)
    s_ocb = take([128, 1], BF16)
    s_oh8 = take([8, 1024], F32)
    s_lnc = take([128, 48], F32)
    s_cwc = take([128, 68], F32)
    s_cbc = take([128, 17], F32)
    s_b1c = take([128, 32], F32)
    s_b2c = take([128, 8], F32)
    s_nwp = take([128, 16], F32)
    s_dpr = take([128, 16], F32)
    s_dtb = take([32, 1], F32)
    s_alg = take([32, 1], F32)
    s_acol = take([32, 1], F32)
    s_gtc = take([128, 8], F32, 8)
    s_seg = take([128, 8], F32)
    s_segb = take([128, 8], BF16)
    s_fc = take([128, 8], F32)
    assert off_[0] <= 30208, off_[0]

    WB = 30208
    W_SZ = 135168
    ipW = [at([128, DIP], BF16, WB + k * 8512) for k in range(NF)]
    opW = [at([128, DM], BF16, WB + 68096 + k * 2048) for k in range(16)]
    f1W = [at([128, FFN], BF16, WB + k * 8192) for k in range(NF)]
    f2W = [at([128, DM], BF16, WB + 65536 + k * 2048) for k in range(32)]
    guW = [at([128, GU2], BF16, WB + k * 5632) for k in range(NF)]
    dnW = [[at([128, DM], BF16, WB + 45056 + (p * NMI + k) * 2048)
            for k in range(NMI)] for p in range(2)]
    sgW = [at([128, SI], BF16, WB + k * 5632) for k in range(NF)]
    suW = [at([128, SI], BF16, WB + 45056 + k * 5632) for k in range(NF)]
    sdW = [at([128, DM], BF16, WB + 90112 + k * 2048) for k in range(NSI)]
    f1Wm = at([128, FFN * NF], BF16, WB)
    f2Wm = at([128, DM * 32], BF16, WB + 65536)
    guWm = at([128, GU2 * NF], BF16, WB)
    dnWm = [at([128, DM * NMI], BF16, WB + 45056 + p * NMI * 2048) for p in range(2)]
    sgWm = at([128, SI * NF], BF16, WB)
    suWm = at([128, SI * NF], BF16, WB + 45056)
    sdWm = at([128, DM * NSI], BF16, WB + 90112)
    p1bst = [at([128, 512], F32, WB + 131072 + i * 2048) for i in range(2)]

    AB = WB + W_SZ
    off_[0] = AB
    xavP = take([128, ST * NF], F32, 2)
    xaP = [[xavP[par][:, k * ST:(k + 1) * ST] for k in range(NF)] for par in range(2)]
    xmbv = take([128, ST * NF], BF16)
    xmb = [xmbv[:, k * ST:(k + 1) * ST] for k in range(NF)]
    xsc = take([128, ST], BF16, 2)
    xsq = take([128, ST], BF16, 2)
    xtmp = take([128, ST], F32, 2)
    zt = take([128, ST], BF16, NPAIR)
    xBCr = take([128, ST + 4], BF16, NCV)
    xst = take([128, ST], BF16, NPAIR)
    bct = take([128, ST], BF16)
    sCf = take([64, ST], BF16)
    HT = take([64, DI], BF16, 2)
    xsT = take([128, 128], BF16, NPAIR)
    assert off_[0] <= SB_SZ, off_[0]
    save_off = off_[0]
    off_[0] = WB + 100864
    sBT = take([128, 64], BF16)
    sBh = take([128, 64], BF16, 2)
    sGm = take([128, 128], BF16)
    sL2 = take([128, 128], BF16, 4)
    sM2 = take([128, 128], BF16, 4)
    sRL = take([128, 128], BF16, 4)
    sEb = take([128, 128], BF16, 2)
    sT1 = take([128, 128], BF16, 2)
    sE2 = take([128, 128], F32, 2)
    sdtT = take([128, 32], F32)
    slnd = take([128, 32], F32)
    saT = take([128, 32], F32)
    saTb = take([128, 32], BF16)
    swT = take([128, 32], F32)
    swd = take([128, 32], F32)
    sEf = take([32, 128], BF16)
    sLb = take([64, 32], F32)
    sLr = take([1, 32], F32)
    sLr2 = take([1, 32], F32)
    r_m, r_ms, r_t, r_v, r_r1, r_mr = take([1, ST], F32, 6)
    r_r2 = r_r1
    r_rr = r_r1
    dtraw = take([32, ST], F32)
    dtf = take([32, ST], F32)
    dtt = take([32, ST], F32)
    afm = dtraw
    g_lf = take([8, ST], F32)
    g_lt = take([128, 8], F32, 2)
    g_d = take([128, 8], F32)
    g_m1 = take([128, 1], F32)
    g_sp = take([128, 8], F32)
    g_ex = take([128, 8], F32)
    g_ss = take([128, 1], F32)
    g_w = take([128, 8], F32, 2)
    g_wf = g_lf
    Ast = take([128, 1024], F32, 2)
    AstB = take([128, 1024], BF16, 2)
    xns = take([128, ST * NF], BF16)
    assert off_[0] <= WB + W_SZ, off_[0]
    off_[0] = save_off

    STB = 512
    off_[0] = AB
    xcbv = take([128, STB * NF], BF16)
    xcb = [xcbv[:, k * STB:(k + 1) * STB] for k in range(NF)]
    g1b = take([128, STB], BF16, 32)
    xbfv = take([128, STB * NF], F32)
    xbf = [xbfv[:, k * STB:(k + 1) * STB] for k in range(NF)]
    assert off_[0] <= SB_SZ

    off_[0] = WB + 90112
    p2accv = take([128, QT * NF], F32)
    p2acc = [p2accv[:, k * QT:(k + 1) * QT] for k in range(NF)]
    assert off_[0] <= WB + W_SZ, off_[0]
    off_[0] = AB
    p2xnv = take([128, TT * NF], BF16)
    p2xn = [p2xnv[:, k * TT:(k + 1) * TT] for k in range(NF)]
    p2h = take([128, TT], BF16, NMI)
    p2wbc = take([128, TT], F32)
    p2tmp = take([128, TT], F32)
    p2w = take([8, QT], F32)
    p2st = take([128, 1408], F32, 2)
    assert off_[0] <= SB_SZ

    off_[0] = AB
    p3hs = take([128, ST], BF16, NSI)
    p3xnv = take([128, ST * NF], BF16)
    p3xn = [p3xnv[:, k * ST:(k + 1) * ST] for k in range(NF)]
    p3sv = take([128, ST * NF], F32)
    p3s = [p3sv[:, k * ST:(k + 1) * ST] for k in range(NF)]
    p3xv = take([128, ST * NF], F32)
    p3x = [p3xv[:, k * ST:(k + 1) * ST] for k in range(NF)]
    p3acv = take([128, ST * NF], F32)
    p3ac = [p3acv[:, k * ST:(k + 1) * ST] for k in range(NF)]
    p3gv = take([128, ST * NF], F32)
    p3g = [p3gv[:, k * ST:(k + 1) * ST] for k in range(NF)]
    p3sw = take([1, ST], F32)
    p3swb = take([128, ST], F32)
    p3st = take([128, 1408], F32, 2)
    assert off_[0] <= SB_SZ

    import contextlib
    stack = contextlib.ExitStack()
    pf = [stack.enter_context(nc.psum_tensor(f"pf{i}", [128, 512], F32)).ap()
          for i in range(6)]
    pb = [stack.enter_context(nc.psum_tensor(f"pb{i}", [128, 1024], BF16)).ap()
          for i in range(2)]
    sem_names = ["pe", "ve", "sc", "gp",
                 "dq_sync_0", "dq_sync_1", "dq_scalar_0", "dq_scalar_1",
                 "dq_gpsimd_0", "dq_gpsimd_1"]
    sems = {s: stack.enter_context(nc.semaphore(s)) for s in sem_names}

    B = Bld()
    bufs = {}

    def bf(x):
        if isinstance(x, Buf):
            return x
        if x not in bufs:
            bufs[x] = Buf(x)
        return bufs[x]

    PSB = [bf(f"ps{i}") for i in range(8)]

    FCVALS = [1e-5, 1.0, -0.5, -1.0, 0.5, 1.0 / 2048.0, 1.0 / 1024.0, 2.0]
    for fi, fv in enumerate(FCVALS):
        nc.const_aps.register(F32, fv, s_fc[:, fi:fi + 1])

    def E(eng, fn, r=(), w=(), deps=(), dma=False, tag=""):
        return B.emit(eng, fn, [bf(x) for x in r], [bf(x) for x in w], deps, dma, tag)

    def mm(out, lhsT, rhs, start, stop, rb=(), wb=(), tp=None, tag="mm"):
        def f(e, o=out, l=lhsT, rr=rhs, s=start, p=stop, t=tp):
            if t is None:
                return e.matmul(out=o, lhsT=l, rhs=rr, start=s, stop=p)
            return e.matmul(out=o, lhsT=l, rhs=rr, start=s, stop=p, tile_position=t)
        return E("tensor", f, r=rb, w=wb, tag=tag)

    def tpose(out, in_, ident, rb, wb, tag="tp"):
        def f(e, o=out, i_=in_, d=ident):
            return e.transpose(out=o, in_=i_, identity=d)
        return E("tensor", f, r=rb, w=wb, tag=tag)

    cl = [(s_lt_b, c_lt_b, "c_lt"), (s_ut_b, c_ut_b, "c_ut"), (s_id_b, c_id_b, "c_idb"),
          (s_id_f, c_id_f, "c_idf"), (s_sel, c_sel, "c_sel"), (s_o1f, c_o1f, "c_o1f"),
          (s_ocb, c_ocb, "c_ocb"), (s_oh8, c_oh8, "c_oh8"), (s_lnc, lnc, "c_lnc"),
          (s_cwc, cwc, "c_cwc"), (s_cbc, cbc, "c_cbc"), (s_b1c, b1c, "c_b1c"),
          (s_b2c, b2c, "c_b2c"), (s_nwp, nwp, "c_nwp"), (s_dpr, dpr, "c_dpr"),
          (s_dtb, dtb, "c_dtb"), (s_alg, alg, "c_alg"), (s_seg, segc, "c_seg"),
          (s_fc, c_fc, "c_fc")]
    for (dst, src, nm) in cl:
        if PREFIX and nm not in ("c_ocb", "c_fc", "c_alg", "c_dtb", "c_lt"):
            continue
        E("sync", lambda e, d=dst, s=src: e.dma_start(out=d, in_=s),
          w=[nm], dma=True, tag="constld")
    for k in range(8):
        if PREFIX:
            break
        E("sync", lambda e, d=s_gtc[k], s=gTc[k * 128:(k + 1) * 128, :]:
          e.dma_start(out=d, in_=s), w=[f"gtc{k}"], dma=True, tag="gtcld")
    E("scalar", lambda e: e.activation(out=s_acol, in_=s_alg, func=AF.Exp),
      r=["c_alg"], w=["acol"], tag="acol")
    E("vector", lambda e: e.tensor_scalar(out=s_acol, in0=s_acol, scalar1=-1.0,
                                          scalar2=None, op0=ALU.mult),
      r=["acol"], w=["acol"], tag="acol2")
    E("vector", lambda e: e.tensor_copy(out=s_segb, in_=s_seg),
      r=["c_seg"], w=["segb"], tag="segb")
    E("vector", lambda e: e.memset(HT[0][:], 0.0), w=["HT0"], tag="ht0")
    E("vector", lambda e: e.memset(HT[1][:], 0.0), w=["HT1"], tag="ht1")

    si = [0]

    def stage_conv(src, dstw, dstname, cols, eng="vector"):
        j = si[0] % 2
        sb = f"Ast{j}"
        E("sync", lambda e, d=Ast[j][:, 0:cols], s=src: e.dma_start(out=d, in_=s),
          w=[sb], dma=True, tag="wld")
        E(eng, lambda e, d=dstw, s=Ast[j][:, 0:cols]: e.tensor_copy(out=d, in_=s),
          r=[sb, "warena"], w=[dstname], tag="wcv")
        si[0] += 1

    if not PREFIX:
        for k in range(NF):
            for c4 in range(5):
                c0, c1 = c4 * 1024, min(DIP, (c4 + 1) * 1024)
                stage_conv(ipT[k * 128:(k + 1) * 128, c0:c1], ipW[k][:, c0:c1],
                           f"ipW{k}", c1 - c0)
        for k in range(16):
            stage_conv(opT[k * 128:(k + 1) * 128, :], opW[k][:, 0:1024], f"opW{k}", 1024)

    def w_stream(srcs, cve="vector"):
        si2 = [0]

        def one(srcap, dstap, cols, dstname, stg, spfx):
            j = si2[0] % 2
            E("sync", lambda e, d=stg[j][:, 0:cols], s=srcap: e.dma_start(out=d, in_=s),
              w=[f"{spfx}{j}"], dma=True, tag="wsl")
            E(cve, lambda e, d=dstap, s=stg[j][:, 0:cols]: e.tensor_copy(out=d, in_=s),
              r=[f"{spfx}{j}", "warena"], w=[dstname], tag="wsc")
            si2[0] += 1
        for args in srcs:
            one(*args)

    def ln_rows(denom, tagp):
        E("vector", lambda e, p=pf[4][0:1, 0:ST]:
          e.tensor_scalar(out=r_m, in0=p, scalar1=1.0 / denom, scalar2=None, op0=ALU.mult),
          r=[PSB[4]], w=["r_m"], tag=tagp + "m")
        E("vector", lambda e, p=pf[5][0:1, 0:ST]:
          e.tensor_scalar(out=r_ms, in0=p, scalar1=1.0 / denom, scalar2=None, op0=ALU.mult),
          r=[PSB[5]], w=["r_ms"], tag=tagp + "ms")
        E("vector", lambda e: e.tensor_tensor(out=r_t, in0=r_m, in1=r_m, op=ALU.mult),
          r=["r_m"], w=["r_t"], tag=tagp + "t")
        E("vector", lambda e: e.tensor_tensor(out=r_v, in0=r_ms, in1=r_t, op=ALU.subtract),
          r=["r_ms", "r_t"], w=["r_v"], tag=tagp + "v")
        E("scalar", lambda e: e.activation(out=r_t, in_=r_v, func=AF.Ln, bias=EPS),
          r=["r_v", "c_fc"], w=["r_t"], tag=tagp + "ln")
        E("scalar", lambda e: e.activation(out=r_r1, in_=r_t, func=AF.Exp, scale=-0.5),
          r=["r_t", "c_fc"], w=["r_r1"], tag=tagp + "ex")
        E("vector", lambda e: e.tensor_tensor(out=r_mr, in0=r_m, in1=r_r1, op=ALU.mult),
          r=["r_m", "r_r1"], w=["r_mr"], tag=tagp + "mr")

    def bcast_row(row, rowname, bank, tagp):
        return mm(pf[bank][:, 0:ST], s_o1f, row, True, True,
                  rb=["c_o1f", rowname], wb=[PSB[bank]], tag=tagp)

    def stats_pass(srcs, srcnames, tagp):
        nsrc = len(srcs)
        for k in range(nsrc):
            sc_, sq_ = xsc[k % 2], xsq[k % 2]
            E("vector", lambda e, d=sc_, s=srcs[k]: e.tensor_copy(out=d, in_=s),
              r=[srcnames[k]], w=[f"xsc{k % 2}"], tag=tagp + "c")
            mm(pf[4][0:1, 0:ST], s_ocb, sc_, k == 0, k == nsrc - 1,
               rb=["c_ocb", f"xsc{k % 2}"], wb=[PSB[4]], tag=tagp + "m")
            E("scalar", lambda e, d=sq_, s=srcs[k]: e.activation(out=d, in_=s, func=AF.Square),
              r=[srcnames[k]], w=[f"xsq{k % 2}"], tag=tagp + "q")
            mm(pf[5][0:1, 0:ST], s_ocb, sq_, k == 0, k == nsrc - 1,
               rb=["c_ocb", f"xsq{k % 2}"], wb=[PSB[5]], tag=tagp + "qm")

    def ssd_chunk(st, qq, seq):
        qs = slice(qq * Q, (qq + 1) * Q)
        for j in range(NPAIR):
            bk = 6 + (j % 2)
            sl = slice(((j // 2) % 4) * 128, ((j // 2) % 4) * 128 + 128)
            tpose(pb[j % 2][:, sl], xst[j][:, qs], s_id_b,
                  rb=[f"xst{j}", "c_idb"], wb=[PSB[bk]], tag="xsT")
            if j % 4 < 2:
                E("vector", lambda e, d=xsT[j], s=pb[j % 2][:, sl]:
                  e.tensor_copy(out=d, in_=s), r=[PSB[bk]], w=[f"xsT{j}"], tag="xsTe")
            else:
                E("scalar", lambda e, d=xsT[j], s=pb[j % 2][:, sl]:
                  e.activation(out=d, in_=s, func=AF.Copy),
                  r=[PSB[bk]], w=[f"xsT{j}"], tag="xsTe")
        tpose(pb[0][:, 512:576], bct[0:64, qs], s_id_b[0:64, 0:64],
              rb=["bct", "c_idb"], wb=[PSB[6]], tag="BT")
        E("vector", lambda e, s=pb[0][:, 512:576]: e.tensor_copy(out=sBT, in_=s),
          r=[PSB[6]], w=["sBT"], tag="BTe")
        mm(pf[5][0:128, 0:128], bct[0:64, qs], sCf[:, qs], True, True,
           rb=["bct", "sCf"], wb=[PSB[5]], tag="G")
        E("vector", lambda e, p=pf[5][0:128, 0:128]:
          e.tensor_tensor(out=sGm, in0=p, in1=s_lt_b, op=ALU.mult),
          r=[PSB[5], "c_lt"], w=["sGm"], tag="Gm")
        tpose(pf[5][:, 128:160], dtf[:, qs], s_id_f[0:32, 0:32],
              rb=["dtf", "c_idf"], wb=[PSB[5]], tag="dtT")
        E("vector", lambda e, s=pf[5][:, 128:160]: e.tensor_copy(out=sdtT, in_=s),
          r=[PSB[5]], w=["sdtT"], tag="dtTe")
        E("scalar", lambda e, s=pf[5][:, 128:160]: e.activation(out=slnd, in_=s, func=AF.Ln),
          r=[PSB[5]], w=["slnd"], tag="lnd")
        tpose(pf[5][:, 160:192], afm[:, qs], s_id_f[0:32, 0:32],
              rb=["dtraw", "c_idf"], wb=[PSB[5]], tag="aT")
        E("vector", lambda e, s=pf[5][:, 160:192]: e.tensor_copy(out=saT, in_=s),
          r=[PSB[5]], w=["saT"], tag="aTe")
        E("vector", lambda e: e.tensor_copy(out=saTb, in_=saT),
          r=["saT"], w=["saTb"], tag="aTb")
        mm(pf[5][0:32, 192:320], saTb, s_lt_b, True, True,
           rb=["saTb", "c_lt"], wb=[PSB[5]], tag="sfm")
        E("scalar", lambda e, p=pf[5][0:32, 192:320]:
          e.activation(out=sEf, in_=p, func=AF.Exp), r=[PSB[5]], w=["sEf"], tag="Ef")
        mm(pf[5][:, 320:352], s_ut_b, saTb, True, True,
           rb=["c_ut", "saTb"], wb=[PSB[5]], tag="wtm")
        E("scalar", lambda e, p=pf[5][:, 320:352]:
          e.activation(out=swT, in_=p, func=AF.Exp), r=[PSB[5]], w=["swT"], tag="wte")
        E("vector", lambda e: e.tensor_tensor(out=swd, in0=swT, in1=sdtT, op=ALU.mult),
          r=["swT", "sdtT"], w=["swd"], tag="wtd")
        E("scalar", lambda e: e.activation(out=sLr, in_=saT[0:1, :], func=AF.Exp),
          r=["saT"], w=["sLr"], tag="La")
        E("vector", lambda e: e.tensor_tensor(out=sLr2, in0=sLr, in1=swT[0:1, :], op=ALU.mult),
          r=["sLr", "swT"], w=["sLr2"], tag="Lb")
        mm(pf[5][0:64, 352:384], s_o1f[:, 0:64], sLr2, True, True,
           rb=["c_o1f", "sLr2"], wb=[PSB[5]], tag="Lm")
        E("vector", lambda e, p=pf[5][0:64, 352:384]: e.tensor_copy(out=sLb, in_=p),
          r=[PSB[5]], w=["sLb"], tag="Le")
        for w in range(8):
            hs4 = [4 * w + i for i in range(4)]
            prs = [2 * w, 2 * w + 1]
            for h in hs4:
                E("vector", lambda e, d=sRL[h % 4], sc1=saT[:, h:h + 1]:
                  e.tensor_scalar(out=d, in0=s_lt_b, scalar1=sc1, scalar2=None,
                                  op0=ALU.mult),
                  r=["c_lt", "saT"], w=[f"sRL{h % 4}"], tag="rl")
            for h in hs4:
                sl = slice((h % 4 // 2) * 128, (h % 4 // 2) * 128 + 128)
                mm(pf[h % 2][:, sl], s_ut_b, sRL[h % 4], True, True,
                   rb=["c_ut", f"sRL{h % 4}"], wb=[PSB[h % 2]], tag="dif")
            for h in hs4:
                sl = slice((h % 4 // 2) * 128, (h % 4 // 2) * 128 + 128)
                E("scalar", lambda e, d=sL2[h % 4], pp=pf[h % 2][:, sl],
                  b=slnd[:, h:h + 1]:
                  e.activation(out=d, in_=pp, func=AF.Exp, bias=b),
                  r=[PSB[h % 2], "slnd"], w=[f"sL2{h % 4}"], tag="l2")
            for h in hs4:
                E("gpsimd", lambda e, d=sM2[h % 4], s=sL2[h % 4]:
                  e.tensor_tensor(out=d, in0=s, in1=sGm, op=ALU.mult),
                  r=[f"sL2{h % 4}", "sGm"], w=[f"sM2{h % 4}"], tag="m2")
            for p in prs:
                bI = 4 + p % 2
                mm(pf[bI][:, 128:256], s_sel[:, p * 128:(p + 1) * 128], sEf, True, True,
                   rb=["c_sel", "sEf"], wb=[PSB[bI]], tag="Ebm")
                mm(pf[bI][:, 0:128], HT[seq][:, p * 128:(p + 1) * 128], sCf[:, qs],
                   True, True, rb=[f"HT{seq}", "sCf"], wb=[PSB[bI]], tag="Yi")
                E("scalar", lambda e, d=sEb[p % 2], pp=pf[bI][:, 128:256]:
                  e.activation(out=d, in_=pp, func=AF.Copy),
                  r=[PSB[bI]], w=[f"sEb{p % 2}"], tag="Ebe")
                E("vector", lambda e, d=sT1[p % 2], pp=pf[bI][:, 0:128], eb=sEb[p % 2]:
                  e.tensor_tensor(out=d, in0=pp, in1=eb, op=ALU.mult),
                  r=[PSB[bI], f"sEb{p % 2}"], w=[f"sT1{p % 2}"], tag="t1")
            for h in hs4:
                p = h // 2
                hi = h % 2
                bP = 2 + p % 2
                mm(pf[bP][hi * 64:(hi + 1) * 64, 0:128],
                   xsT[p][:, hi * 64:(hi + 1) * 64], sM2[h % 4], True, True,
                   rb=[f"xsT{p}", f"sM2{h % 4}"], wb=[PSB[bP]],
                   tp=(0, hi * 64), tag="Ym")
            for h in hs4:
                p = h // 2
                hi = h % 2
                bP = 2 + p % 2
                E("vector", lambda e, d=sBh[h % 2], sc1=swd[:, h:h + 1]:
                  e.tensor_scalar(out=d, in0=sBT, scalar1=sc1, scalar2=None,
                                  op0=ALU.mult),
                  r=["sBT", "swd"], w=[f"sBh{h % 2}"], tag="bh")
                mm(pf[bP][0:64, 128 + hi * 64:128 + (hi + 1) * 64], sBh[h % 2],
                   xsT[p][:, hi * 64:(hi + 1) * 64], True, True,
                   rb=[f"sBh{h % 2}", f"xsT{p}"], wb=[PSB[bP]], tag="dH")
                E("vector", lambda e, d=HT[seq][:, h * 64:(h + 1) * 64],
                  sc1=sLb[:, h:h + 1],
                  pp=pf[bP][0:64, 128 + hi * 64:128 + (hi + 1) * 64]:
                  e.scalar_tensor_tensor(out=d, in0=d, scalar=sc1, in1=pp,
                                         op0=ALU.mult, op1=ALU.add),
                  r=[f"HT{seq}", "sLb", PSB[bP]], w=[f"HT{seq}"], tag="Hup")
            for p in prs:
                bP = 2 + p % 2
                E("vector", lambda e, d=sE2[p % 2], t1=sT1[p % 2], pp=pf[bP][:, 0:128]:
                  e.tensor_tensor(out=d, in0=t1, in1=pp, op=ALU.add),
                  r=[f"sT1{p % 2}", PSB[bP]], w=[f"sE2{p % 2}"], tag="e2")
                E("vector", lambda e, d=xst[p][:, qs], sc1=s_dpr[:, p:p + 1],
                  e2=sE2[p % 2]:
                  e.scalar_tensor_tensor(out=d, in0=d, scalar=sc1, in1=e2,
                                         op0=ALU.mult, op1=ALU.add),
                  r=[f"xst{p}", "c_dpr", f"sE2{p % 2}"], w=[f"xst{p}"], tag="e3")

    def emit_xld(st):
        par = st % 2
        cols = slice(st * ST, (st + 1) * ST)
        E("sync", lambda e, d=xavP[par].rearrange("p (k t) -> p k t", k=NF),
          s=xfm[:, cols].rearrange("(k p) t -> p k t", p=128):
          e.dma_start(out=d, in_=s), r=["aarena"],
          w=[f"xa{par}_{k}" for k in range(NF)], dma=True, tag="xld")

    def p1a_supertile(st):
        par = st % 2
        xa = xaP[par]
        cols = slice(st * ST, (st + 1) * ST)
        seq = st // 8
        if st == 0:
            emit_xld(0)
        if st + 1 < NST:
            emit_xld(st + 1)
        stats_pass(xa, [f"xa{par}_{k}" for k in range(NF)], "s1")
        ln_rows(DM, "r1")
        if DEBUG and st == 0:
            for di, (rr, rn) in enumerate([(r_m, "r_m"), (r_ms, "r_ms"), (r_v, "r_v"),
                                           (r_r1, "r_r1"), (r_mr, "r_mr"), (r_t, "r_t")]):
                E("gpsimd", lambda e, d=d_dbg[di:di + 1, :], s=rr:
                  e.dma_start(out=d, in_=s), r=[rn], w=["d_dbg"], dma=True, tag="dbg")
            E("vector", lambda e: e.tensor_copy(out=xtmp[0], in_=xsq[1]),
              r=["xsq1"], w=["xtmp0"], tag="dbgc")
            E("gpsimd", lambda e: e.dma_start(out=d_dbg2, in_=xtmp[0]),
              r=["xtmp0"], w=["d_dbg2"], dma=True, tag="dbg2")
            E("vector", lambda e: e.tensor_copy(out=xtmp[1], in_=xsq[0]),
              r=["xsq0"], w=["xtmp1"], tag="dbgc2")
            E("gpsimd", lambda e: e.dma_start(out=d_dbg3, in_=xtmp[1]),
              r=["xtmp1"], w=["d_dbg3"], dma=True, tag="dbg3")
        if PREFIX:
            E("gpsimd", lambda e: e.dma_start(out=d_dbg2, in_=xa[7]),
              r=["xa7"], w=["d_dbg2"], dma=True, tag="dbgxa")
            E("vector", lambda e: e.tensor_copy(out=xtmp[0], in_=xsq[1]),
              r=["xsq1"], w=["xtmp0"], tag="dbq")
            E("gpsimd", lambda e: e.dma_start(out=d_dbg3, in_=xtmp[0]),
              r=["xtmp0"], w=["d_dbg3"], dma=True, tag="dbq2")
            return
        bcast_row(r_r1, "r_r1", 4, "bc_r1")
        bcast_row(r_mr, "r_mr", 5, "bc_mr1")
        for k in range(NF):
            E("vector", lambda e, d=xa[k], p=pf[4][:, 0:ST]:
              e.tensor_tensor(out=d, in0=d, in1=p, op=ALU.mult),
              r=[f"xa{par}_{k}", PSB[4]], w=[f"xa{par}_{k}"], tag="xn1")
            E("vector", lambda e, d=xa[k], p=pf[5][:, 0:ST]:
              e.tensor_tensor(out=d, in0=d, in1=p, op=ALU.subtract),
              r=[f"xa{par}_{k}", PSB[5]], w=[f"xa{par}_{k}"], tag="xn2")
        for k in range(NF):
            mm(pf[3][0:8, 0:ST], s_gtc[k], xa[k], k == 0, k == NF - 1,
               rb=[f"gtc{k}", f"xa{par}_{k}"], wb=[PSB[3]], tag="gl")
        E("vector", lambda e, p=pf[3][0:8, 0:ST]: e.tensor_copy(out=g_lf, in_=p),
          r=[PSB[3]], w=["g_lf"], tag="glev")
        for k in range(NF):
            E("vector", lambda e, d=xns[:, k * ST:(k + 1) * ST], s=xa[k]:
              e.tensor_copy(out=d, in_=s), r=[f"xa{par}_{k}"], w=[f"xns{k}"], tag="xnb")
        E("gpsimd", lambda e, d=d_xn[:, cols].rearrange("(k p) t -> p k t", p=128),
          s=xns.rearrange("p (k t) -> p k t", k=NF): e.dma_start(out=d, in_=s),
          r=[f"xns{k}" for k in range(NF)],
          w=[f"dxn{st}_{k}" for k in range(NF)], dma=True, tag="xnsp")
        E("vector", lambda e: e.tensor_tensor(out=r_t, in0=r_r1, in1=r_r1, op=ALU.mult),
          r=["r_r1"], w=["r_t"], tag="r2a")
        E("vector", lambda e: e.tensor_tensor(out=r_v, in0=r_v, in1=r_t, op=ALU.mult),
          r=["r_v", "r_t"], w=["r_v"], tag="r2b")
        E("scalar", lambda e: e.activation(out=r_t, in_=r_v, func=AF.Ln, bias=EPS),
          r=["r_v", "c_fc"], w=["r_t"], tag="r2c")
        E("scalar", lambda e: e.activation(out=r_r2, in_=r_t, func=AF.Exp, scale=-0.5),
          r=["r_t", "c_fc"], w=["r_r1"], tag="r2d")
        bcast_row(r_r2, "r_r1", 4, "bc_r2")
        for k in range(NF):
            E("vector", lambda e, d=xa[k], p=pf[4][:, 0:ST]:
              e.tensor_tensor(out=d, in0=d, in1=p, op=ALU.mult),
              r=[f"xa{par}_{k}", PSB[4]], w=[f"xa{par}_{k}"], tag="l0a")
            E("vector", lambda e, d=xa[k], w1=s_lnc[:, k:k + 1], bb=s_lnc[:, 8 + k:9 + k]:
              e.tensor_scalar(out=d, in0=d, scalar1=w1, scalar2=bb,
                              op0=ALU.mult, op1=ALU.add),
              r=[f"xa{par}_{k}", "c_lnc"], w=[f"xa{par}_{k}"], tag="l0b")
        for h2 in range(2):
            sl = slice(h2 * 128, (h2 + 1) * 128)
            tpose(pf[5][:, 384:392], g_lf[:, sl], s_id_f[0:8, 0:8],
                  rb=["g_lf", "c_idf"], wb=[PSB[5]], tag="glt")
            E("vector", lambda e, d=g_lt[h2], p=pf[5][:, 384:392]:
              e.tensor_copy(out=d, in_=p), r=[PSB[5]], w=[f"glt{h2}"], tag="glte")
        for h2 in range(2):
            lt = g_lt[h2]
            lb = f"glt{h2}"
            E("vector", lambda e, s=lt: e.tensor_reduce(out=g_m1, in_=s, axis=AX.X,
                                                        op=ALU.max),
              r=[lb], w=["gm1"], tag="gm1")
            E("vector", lambda e, s=lt: e.tensor_scalar(out=g_d, in0=s, scalar1=g_m1,
                                                        scalar2=None, op0=ALU.subtract),
              r=[lb, "gm1"], w=["gd"], tag="gd")
            E("vector", lambda e: e.tensor_scalar(out=g_sp, in0=g_d, scalar1=0.0,
                                                  scalar2=-1e30, op0=ALU.is_ge,
                                                  op1=ALU.mult),
              r=["gd"], w=["gsp"], tag="gsp")
            E("vector", lambda e, s=lt: e.tensor_tensor(out=g_sp, in0=g_sp, in1=s,
                                                        op=ALU.add),
              r=["gsp", lb], w=["gsp"], tag="gsp2")
            E("vector", lambda e: e.tensor_reduce(out=g_m1, in_=g_sp, axis=AX.X,
                                                  op=ALU.max),
              r=["gsp"], w=["gm1"], tag="gm2")
            E("vector", lambda e, s=lt: e.tensor_scalar(out=g_sp, in0=s, scalar1=g_m1,
                                                        scalar2=None, op0=ALU.subtract),
              r=[lb, "gm1"], w=["gsp"], tag="gd2")
            E("vector", lambda e: e.tensor_scalar(out=g_sp, in0=g_sp, scalar1=0.0,
                                                  scalar2=None, op0=ALU.is_ge),
              r=["gsp"], w=["gsp"], tag="gmask")
            E("scalar", lambda e: e.activation(out=g_ex, in_=g_d, func=AF.Exp),
              r=["gd"], w=["gex"], tag="gex")
            E("vector", lambda e: e.tensor_reduce(out=g_ss, in_=g_ex, axis=AX.X,
                                                  op=ALU.add),
              r=["gex"], w=["gss"], tag="gss")
            E("vector", lambda e: e.reciprocal(out=g_ss, in_=g_ss),
              r=["gss"], w=["gss"], tag="grc")
            E("vector", lambda e: e.tensor_scalar(out=g_ex, in0=g_ex, scalar1=g_ss,
                                                  scalar2=None, op0=ALU.mult),
              r=["gex", "gss"], w=["gex"], tag="gsc")
            E("vector", lambda e, d=g_w[h2]: e.tensor_tensor(out=d, in0=g_ex, in1=g_sp,
                                                             op=ALU.mult),
              r=["gex", "gsp"], w=[f"gw{h2}"], tag="gw")
            tpose(pf[5][0:8, 384:512], g_w[h2], s_id_f,
                  rb=[f"gw{h2}", "c_idf"], wb=[PSB[5]], tag="gwt")
            E("vector", lambda e, d=g_wf[:, h2 * 128:(h2 + 1) * 128], p=pf[5][0:8, 384:512]:
              e.tensor_copy(out=d, in_=p), r=[PSB[5]], w=["g_lf"], tag="gwfe")
        E("gpsimd", lambda e, d=d_w[:, cols], s=g_wf: e.dma_start(out=d, in_=s),
          r=["g_lf"], w=[f"dw{st}"], dma=True, tag="wsp")
        stats_pass(xa, [f"xa{par}_{k}" for k in range(NF)], "s3")
        ln_rows(DM, "r3")
        bcast_row(r_r1, "r_r1", 4, "bc_r3")
        bcast_row(r_mr, "r_mr", 5, "bc_mr3")
        for k in range(NF):
            tm = xtmp[k % 2]
            E("vector", lambda e, d=tm, s=xa[k], p=pf[4][:, 0:ST]:
              e.tensor_tensor(out=d, in0=s, in1=p, op=ALU.mult),
              r=[f"xa{par}_{k}", PSB[4]], w=[f"xtmp{k % 2}"], tag="l1a")
            E("vector", lambda e, d=tm, p=pf[5][:, 0:ST]:
              e.tensor_tensor(out=d, in0=d, in1=p, op=ALU.subtract),
              r=[f"xtmp{k % 2}", PSB[5]], w=[f"xtmp{k % 2}"], tag="l1b")
            E("vector", lambda e, d=xmb[k], s=tm, w1=s_lnc[:, 16 + k:17 + k],
              bb=s_lnc[:, 24 + k:25 + k]:
              e.tensor_scalar(out=d, in0=s, scalar1=w1, scalar2=bb,
                              op0=ALU.mult, op1=ALU.add),
              r=[f"xtmp{k % 2}", "c_lnc"], w=[f"xmb{k}"], tag="l1c")
        for o in range(34):
            o0 = o * 128
            o1 = min(DIP, o0 + 128)
            ow = o1 - o0
            bank = o % 3
            for k in range(NF):
                mm(pf[bank][0:ow, 0:ST], ipW[k][:, o0:o1], xmb[k], k == 0, k == NF - 1,
                   rb=[f"ipW{k}", f"xmb{k}", "warena"], wb=[PSB[bank]], tag="ipm")
            if o < 16:
                E("scalar", lambda e, d=zt[o], p=pf[bank][0:128, 0:ST]:
                  e.activation(out=d, in_=p, func=AF.Silu),
                  r=[PSB[bank]], w=[f"zt{o}"], tag="zev")
            elif o < 33:
                j = o - 16
                E("scalar", lambda e, d=xBCr[j][:, 3:3 + ST], p=pf[bank][0:128, 0:ST]:
                  e.activation(out=d, in_=p, func=AF.Copy),
                  r=[PSB[bank]], w=[f"xBC{j}"], tag="xbcev")
            else:
                E("scalar", lambda e, p=pf[bank][0:32, 0:ST]:
                  e.activation(out=dtraw, in_=p, func=AF.Copy),
                  r=[PSB[bank]], w=["dtraw"], tag="dtev")
        if st == 0 or st == 8:
            for j in range(NCV):
                E("vector", lambda e, d=xBCr[j][:, 0:3]: e.memset(d, 0.0),
                  w=[f"xBC{j}"], tag="cvz")
        for j in range(NCV):
            dst = xst[j] if j < 16 else bct
            dn = f"xst{j}" if j < 16 else "bct"
            cs = xtmp[j % 2]
            E("vector", lambda e, d=cs, s=xBCr[j][:, 0:ST], w1=s_cwc[:, 4 * j:4 * j + 1]:
              e.tensor_scalar(out=d, in0=s, scalar1=w1, scalar2=None, op0=ALU.mult),
              r=[f"xBC{j}", "c_cwc"], w=[f"xtmp{j % 2}"], tag="cv0")
            for tap in (1, 2, 3):
                E("vector", lambda e, d=cs, s=xBCr[j][:, tap:tap + ST],
                  w1=s_cwc[:, 4 * j + tap:4 * j + tap + 1]:
                  e.scalar_tensor_tensor(out=d, in0=s, scalar=w1, in1=d,
                                         op0=ALU.mult, op1=ALU.add),
                  r=[f"xBC{j}", f"xtmp{j % 2}", "c_cwc"], w=[f"xtmp{j % 2}"], tag="cvt")
            E("scalar", lambda e, d=dst, s=cs, bb=s_cbc[:, j:j + 1]:
              e.activation(out=d, in_=s, func=AF.Silu, bias=bb),
              r=[f"xtmp{j % 2}", "c_cbc"], w=[dn], tag="cvs")
            E("gpsimd", lambda e, d=xBCr[j][:, 0:3], s=xBCr[j][:, ST:ST + 3]:
              e.tensor_copy(out=d, in_=s), r=[f"xBC{j}"], w=[f"xBC{j}"], tag="cvh")
        E("sync", lambda e: e.dma_start(out=sCf, in_=bct[64:128, :]),
          r=["bct"], w=["sCf"], dma=True, tag="crel")
        E("scalar", lambda e: e.activation(out=dtt, in_=dtraw, func=AF.Abs, bias=s_dtb),
          r=["dtraw", "c_dtb"], w=["dtt"], tag="dt1")
        E("scalar", lambda e: e.activation(out=dtt, in_=dtt, func=AF.Exp, scale=-1.0),
          r=["dtt", "c_fc"], w=["dtt"], tag="dt2")
        E("scalar", lambda e: e.activation(out=dtt, in_=dtt, func=AF.Ln, bias=1.0),
          r=["dtt", "c_fc"], w=["dtt"], tag="dt3")
        E("scalar", lambda e: e.activation(out=dtf, in_=dtraw, func=AF.Relu, bias=s_dtb),
          r=["dtraw", "c_dtb"], w=["dtf"], tag="dt4")
        E("vector", lambda e: e.tensor_tensor(out=dtf, in0=dtf, in1=dtt, op=ALU.add),
          r=["dtf", "dtt"], w=["dtf"], tag="dt5")
        E("vector", lambda e: e.tensor_scalar(out=afm, in0=dtf, scalar1=s_acol,
                                              scalar2=None, op0=ALU.mult),
          r=["dtf", "acol"], w=["dtraw"], tag="dta")
        for qq in range(2):
            ssd_chunk(st, qq, seq)
        for j in range(NPAIR):
            E("vector", lambda e, d=xst[j], s=zt[j]:
              e.tensor_tensor(out=d, in0=d, in1=s, op=ALU.mult),
              r=[f"xst{j}", f"zt{j}"], w=[f"xst{j}"], tag="gat")
        for j in range(NPAIR):
            sq_ = xsq[j % 2]
            E("scalar", lambda e, d=sq_, s=xst[j]: e.activation(out=d, in_=s,
                                                                func=AF.Square),
              r=[f"xst{j}"], w=[f"xsq{j % 2}"], tag="rmq")
            mm(pf[4][0:1, 0:ST], s_ocb, sq_, j == 0, j == NPAIR - 1,
               rb=["c_ocb", f"xsq{j % 2}"], wb=[PSB[4]], tag="rmm")
        E("scalar", lambda e, p=pf[4][0:1, 0:ST]:
          e.activation(out=r_t, in_=p, func=AF.Ln, scale=1.0 / DI, bias=EPS),
          r=[PSB[4], "c_fc"], w=["r_t"], tag="rml")
        E("scalar", lambda e: e.activation(out=r_rr, in_=r_t, func=AF.Exp, scale=-0.5),
          r=["r_t", "c_fc"], w=["r_r1"], tag="rme")
        bcast_row(r_rr, "r_r1", 5, "bc_rms")
        for j in range(NPAIR):
            E("vector", lambda e, d=zt[j], s=xst[j], w1=s_nwp[:, j:j + 1], p=pf[5][:, 0:ST]:
              e.scalar_tensor_tensor(out=d, in0=s, scalar=w1, in1=p,
                                     op0=ALU.mult, op1=ALU.mult),
              r=[f"xst{j}", "c_nwp", PSB[5]], w=[f"zt{j}"], tag="rma")
        for o in range(NF):
            bank = o % 3
            for k in range(16):
                mm(pf[bank][:, 0:ST], opW[k][:, o * 128:(o + 1) * 128], zt[k],
                   k == 0, k == 15, rb=[f"opW{k}", f"zt{k}", "warena"],
                   wb=[PSB[bank]], tag="opm")
            E("vector", lambda e, d=xa[o], p=pf[bank][:, 0:ST]:
              e.tensor_tensor(out=d, in0=d, in1=p, op=ALU.add),
              r=[f"xa{par}_{o}", PSB[bank]], w=[f"xa{par}_{o}"], tag="opev")
        E("gpsimd", lambda e, d=d_xb[:, cols].rearrange("(k p) t -> p k t", p=128),
          s=xavP[par].rearrange("p (k t) -> p k t", k=NF): e.dma_start(out=d, in_=s),
          r=[f"xa{par}_{k}" for k in range(NF)],
          w=[f"dxb{st}_{k}" for k in range(NF)], dma=True, tag="xbsp")
        stats_pass(xa, [f"xa{par}_{k}" for k in range(NF)], "s4")
        ln_rows(DM, "r4")
        bcast_row(r_r1, "r_r1", 4, "bc_r4")
        bcast_row(r_mr, "r_mr", 5, "bc_mr4")
        for k in range(NF):
            tm = xtmp[k % 2]
            E("vector", lambda e, d=tm, s=xa[k], p=pf[4][:, 0:ST]:
              e.tensor_tensor(out=d, in0=s, in1=p, op=ALU.mult),
              r=[f"xa{par}_{k}", PSB[4]], w=[f"xtmp{k % 2}"], tag="l2a")
            E("vector", lambda e, d=tm, p=pf[5][:, 0:ST]:
              e.tensor_tensor(out=d, in0=d, in1=p, op=ALU.subtract),
              r=[f"xtmp{k % 2}", PSB[5]], w=[f"xtmp{k % 2}"], tag="l2b")
            E("vector", lambda e, d=xmb[k], s=tm, w1=s_lnc[:, 32 + k:33 + k],
              bb=s_lnc[:, 40 + k:41 + k]:
              e.tensor_scalar(out=d, in0=s, scalar1=w1, scalar2=bb,
                              op0=ALU.mult, op1=ALU.add),
              r=[f"xtmp{k % 2}", "c_lnc"], w=[f"xmb{k}"], tag="l2c")
        E("gpsimd", lambda e, d=d_xc[:, cols].rearrange("(k p) t -> p k t", p=128),
          s=xmbv.rearrange("p (k t) -> p k t", k=NF): e.dma_start(out=d, in_=s),
          r=[f"xmb{k}" for k in range(NF)],
          w=[f"dxc{st}_{k}" for k in range(NF)], dma=True, tag="xcsp")

    def p1b():
        STB = 512
        E("vector", lambda e: e.engine_nop(), w=["warena"], deps=B.all_tokens(), tag="wgate1b")
        E("vector", lambda e: e.engine_nop(), w=["aarena"], deps=B.all_tokens(), tag="agate1b")
        jobs = []
        for k in range(NF):
            for c4 in range(8):
                c0, c1 = c4 * 512, (c4 + 1) * 512
                jobs.append((f1T[k * 128:(k + 1) * 128, c0:c1], f1W[k][:, c0:c1],
                             512, f"f1W{k}", p1bst, "p1bst"))
        for k in range(32):
            for c4 in range(2):
                c0, c1 = c4 * 512, (c4 + 1) * 512
                jobs.append((f2T[k * 128:(k + 1) * 128, c0:c1], f2W[k][:, c0:c1],
                             512, f"f2W{k}", p1bst, "p1bst"))
        w_stream(jobs)
        for st in range(8):
            cols = slice(st * STB, (st + 1) * STB)
            sts = (2 * st, 2 * st + 1)
            E("sync", lambda e, d=xcbv.rearrange("p (k t) -> p k t", k=NF),
              s=d_xc[:, cols].rearrange("(k p) t -> p k t", p=128):
              e.dma_start(out=d, in_=s),
              r=["aarena"] + [f"dxc{s_}_{k}" for s_ in sts for k in range(NF)],
              w=[f"xcb{k}" for k in range(NF)], dma=True, tag="xcld")
            E("sync", lambda e, d=xbfv.rearrange("p (k t) -> p k t", k=NF),
              s=d_xb[:, cols].rearrange("(k p) t -> p k t", p=128):
              e.dma_start(out=d, in_=s),
              r=["aarena"] + [f"dxb{s_}_{k}" for s_ in sts for k in range(NF)],
              w=[f"xbf{k}" for k in range(NF)], dma=True, tag="xbld")
            for o in range(32):
                bank = o % 3
                for k in range(NF):
                    mm(pf[bank][:, 0:STB], f1W[k][:, o * 128:(o + 1) * 128], xcb[k],
                       k == 0, k == NF - 1,
                       rb=[f"f1W{k}", f"xcb{k}", "warena"], wb=[PSB[bank]], tag="f1m")
                E("scalar", lambda e, d=g1b[o], p=pf[bank][:, 0:STB],
                  bb=s_b1c[:, o:o + 1]:
                  e.activation(out=d, in_=p, func=AF.Gelu, bias=bb),
                  r=[PSB[bank], "c_b1c"], w=[f"g1b{o}"], tag="gelu")
            for o in range(NF):
                bank = o % 3
                for k in range(32):
                    mm(pf[bank][:, 0:STB], f2W[k][:, o * 128:(o + 1) * 128], g1b[k],
                       k == 0, k == 31,
                       rb=[f"f2W{k}", f"g1b{k}", "warena"], wb=[PSB[bank]], tag="f2m")
                E("vector", lambda e, d=xbf[o], p=pf[bank][:, 0:STB],
                  bb=s_b2c[:, o:o + 1]:
                  e.scalar_tensor_tensor(out=d, in0=p, scalar=bb, in1=d,
                                         op0=ALU.add, op1=ALU.add),
                  r=[PSB[bank], "c_b2c", f"xbf{o}"], w=[f"xbf{o}"], tag="f2ev")
            E("gpsimd", lambda e, d=d_s[:, cols].rearrange("(k p) t -> p k t", p=128),
              s=xbfv.rearrange("p (k t) -> p k t", k=NF): e.dma_start(out=d, in_=s),
              r=[f"xbf{k}" for k in range(NF)],
              w=[f"ds{s_}_{k}" for s_ in sts for k in range(NF)], dma=True, tag="ssp")

    def p2():
        E("vector", lambda e: e.engine_nop(), w=["warena"], deps=B.all_tokens(), tag="wgate2")
        E("vector", lambda e: e.engine_nop(), w=["aarena"], deps=B.all_tokens(), tag="agate2")

        def load_gu(ee):
            jobs = []
            for k in range(NF):
                for c4 in range(2):
                    c0, c1 = c4 * 1408, (c4 + 1) * 1408
                    jobs.append((guT[ee, k * 128:(k + 1) * 128, c0:c1],
                                 guW[k][:, c0:c1], 1408, f"guW{k}", p2st, "p2st"))
            w_stream(jobs)

        def load_dn(ee, pp):
            jobs = []
            for k in range(NMI):
                jobs.append((dnT[ee, k * 128:(k + 1) * 128, :],
                             dnW[pp][k][:, 0:1024], 1024, f"dnW{pp}_{k}", p2st, "p2st"))
            w_stream(jobs)

        for qt in range(NQ):
            qcols = slice(qt * QT, (qt + 1) * QT)
            wsts = [qt * 4 + i for i in range(4)]
            E("sync", lambda e, d=p2w, s=d_w[:, qcols]: e.dma_start(out=d, in_=s),
              r=["aarena"] + [f"dw{s_}" for s_ in wsts], w=["p2w"], dma=True, tag="p2wld")
            for ee in range(NE):
                if qt == 0 and ee == 0:
                    load_gu(0)
                    load_dn(0, 0)
                for tt2 in range(2):
                    tcols = slice(qt * QT + tt2 * TT, qt * QT + (tt2 + 1) * TT)
                    xsts = [(qt * QT + tt2 * TT) // ST, (qt * QT + tt2 * TT) // ST + 1]
                    E("sync", lambda e, d=p2xnv.rearrange("p (k t) -> p k t", k=NF),
                      s=d_xn[:, tcols].rearrange("(k p) t -> p k t", p=128):
                      e.dma_start(out=d, in_=s),
                      r=["aarena"] + [f"dxn{s_}_{k}" for s_ in xsts for k in range(NF)],
                      w=[f"p2xn{k}" for k in range(NF)], dma=True, tag="p2xld")
                    for j in range(NMI):
                        for k in range(NF):
                            mm(pf[0][:, 0:TT], guW[k][:, j * 128:(j + 1) * 128], p2xn[k],
                               k == 0, k == NF - 1,
                               rb=[f"guW{k}", f"p2xn{k}", "warena"], wb=[PSB[0]],
                               tag="gum")
                        for k in range(NF):
                            mm(pf[1][:, 0:TT], guW[k][:, MI + j * 128:MI + (j + 1) * 128],
                               p2xn[k], k == 0, k == NF - 1,
                               rb=[f"guW{k}", f"p2xn{k}", "warena"], wb=[PSB[1]],
                               tag="guu")
                        E("scalar", lambda e, d=p2h[j], p=pf[0][:, 0:TT]:
                          e.activation(out=d, in_=p, func=AF.Silu),
                          r=[PSB[0]], w=[f"p2h{j}"], tag="hsl")
                        E("vector", lambda e, d=p2h[j], p=pf[1][:, 0:TT]:
                          e.tensor_tensor(out=d, in0=d, in1=p, op=ALU.mult),
                          r=[f"p2h{j}", PSB[1]], w=[f"p2h{j}"], tag="hml")
                    mm(pf[4][:, 0:TT], s_oh8[:, ee * 128:(ee + 1) * 128],
                       p2w[:, tt2 * TT:(tt2 + 1) * TT], True, True,
                       rb=["c_oh8", "p2w"], wb=[PSB[4]], tag="wbm")
                    E("vector", lambda e, p=pf[4][:, 0:TT]: e.tensor_copy(out=p2wbc, in_=p),
                      r=[PSB[4]], w=["p2wbc"], tag="wbe")
                    pp_ = ee % 2
                    for o in range(NF):
                        bank = 2 + o % 2
                        for k in range(NMI):
                            mm(pf[bank][:, 0:TT], dnW[pp_][k][:, o * 128:(o + 1) * 128],
                               p2h[k], k == 0, k == NMI - 1,
                               rb=[f"dnW{pp_}_{k}", f"p2h{k}", "warena"],
                               wb=[PSB[bank]], tag="dnm")
                        asl = slice(tt2 * TT, (tt2 + 1) * TT)
                        if ee == 0:
                            E("vector", lambda e, d=p2acc[o][:, asl], p=pf[bank][:, 0:TT]:
                              e.tensor_tensor(out=d, in0=p, in1=p2wbc, op=ALU.mult),
                              r=[PSB[bank], "p2wbc"], w=[f"p2acc{o}"], tag="cb0")
                        else:
                            E("vector", lambda e, p=pf[bank][:, 0:TT]:
                              e.tensor_tensor(out=p2tmp, in0=p, in1=p2wbc, op=ALU.mult),
                              r=[PSB[bank], "p2wbc"], w=["p2tmp"], tag="cbm")
                            E("gpsimd", lambda e, d=p2acc[o][:, asl]:
                              e.tensor_tensor(out=d, in0=d, in1=p2tmp, op=ALU.add),
                              r=[f"p2acc{o}", "p2tmp"], w=[f"p2acc{o}"], tag="cba")
                ne_, nq_ = (ee + 1, qt) if ee < NE - 1 else (0, qt + 1)
                if nq_ < NQ:
                    load_gu(ne_)
                    load_dn(ne_, ne_ % 2)
            E("gpsimd", lambda e, d=d_acc[:, qcols].rearrange("(k p) t -> p k t", p=128),
              s=p2accv.rearrange("p (k t) -> p k t", k=NF): e.dma_start(out=d, in_=s),
              r=[f"p2acc{k}" for k in range(NF)],
              w=[f"dacc{qt}_{k}" for k in range(NF)], dma=True, tag="accsp")

    def p3():
        E("vector", lambda e: e.engine_nop(), w=["warena"], deps=B.all_tokens(), tag="wgate3")
        E("vector", lambda e: e.engine_nop(), w=["aarena"], deps=B.all_tokens(), tag="agate3")
        jobs = []
        for (srcT, dstL, nm) in [(sgT, sgW, "sgW"), (suT, suW, "suW")]:
            for k in range(NF):
                for c4 in range(2):
                    c0, c1 = c4 * 1408, (c4 + 1) * 1408
                    jobs.append((srcT[k * 128:(k + 1) * 128, c0:c1],
                                 dstL[k][:, c0:c1], 1408, f"{nm}{k}", p3st, "p3st"))
        for k in range(NSI):
            jobs.append((sdT[k * 128:(k + 1) * 128, :], sdW[k][:, 0:1024],
                         1024, f"sdW{k}", p3st, "p3st"))
        w_stream(jobs)
        for st in range(NST):
            cols = slice(st * ST, (st + 1) * ST)
            qt = st // 4
            for (dst, srct, rl, wpfx, tg) in [
                (p3xnv, d_xn, [f"dxn{st}_{k}" for k in range(NF)], "p3xn", "p3xnld"),
                (p3sv, d_s, [f"ds{st}_{k}" for k in range(NF)], "p3s", "p3sld"),
                (p3xv, xfm, [], "p3x", "p3xld"),
                (p3acv, d_acc, [f"dacc{qt}_{k}" for k in range(NF)], "p3ac", "p3acld"),
            ]:
                E("sync", lambda e, d=dst.rearrange("p (k t) -> p k t", k=NF),
                  s=srct[:, cols].rearrange("(k p) t -> p k t", p=128):
                  e.dma_start(out=d, in_=s), r=["aarena"] + rl,
                  w=[f"{wpfx}{k}" for k in range(NF)], dma=True, tag=tg)
            for j in range(NSI):
                b0 = (j % 2) * 2
                b1_ = b0 + 1
                for k in range(NF):
                    mm(pf[b0][:, 0:ST], sgW[k][:, j * 128:(j + 1) * 128], p3xn[k],
                       k == 0, k == NF - 1,
                       rb=[f"sgW{k}", f"p3xn{k}", "warena"], wb=[PSB[b0]], tag="shg")
                for k in range(NF):
                    mm(pf[b1_][:, 0:ST], suW[k][:, j * 128:(j + 1) * 128], p3xn[k],
                       k == 0, k == NF - 1,
                       rb=[f"suW{k}", f"p3xn{k}", "warena"], wb=[PSB[b1_]], tag="shu")
                E("scalar", lambda e, d=p3hs[j], p=pf[b0][:, 0:ST]:
                  e.activation(out=d, in_=p, func=AF.Silu),
                  r=[PSB[b0]], w=[f"p3hs{j}"], tag="shs")
                E("vector", lambda e, d=p3hs[j], p=pf[b1_][:, 0:ST]:
                  e.tensor_tensor(out=d, in0=d, in1=p, op=ALU.mult),
                  r=[f"p3hs{j}", PSB[b1_]], w=[f"p3hs{j}"], tag="shm")
            for k in range(NF):
                mm(pf[4][0:1, 0:ST], s_segb[:, k:k + 1], p3xn[k], k == 0, k == NF - 1,
                   rb=["segb", f"p3xn{k}"], wb=[PSB[4]], tag="swm")
            E("scalar", lambda e, p=pf[4][0:1, 0:ST]:
              e.activation(out=p3sw, in_=p, func=AF.Tanh, scale=0.5),
              r=[PSB[4], "c_fc"], w=["p3sw"], tag="swt")
            E("vector", lambda e: e.tensor_scalar(out=p3sw, in0=p3sw, scalar1=1.0,
                                                  scalar2=0.5, op0=ALU.add, op1=ALU.mult),
              r=["p3sw"], w=["p3sw"], tag="sws")
            mm(pf[5][:, 0:ST], s_o1f, p3sw, True, True,
               rb=["c_o1f", "p3sw"], wb=[PSB[5]], tag="swb")
            E("vector", lambda e, p=pf[5][:, 0:ST]: e.tensor_copy(out=p3swb, in_=p),
              r=[PSB[5]], w=["p3swb"], tag="swbe")
            for o in range(NF):
                bank = o % 2
                for k in range(NSI):
                    mm(pf[bank][:, 0:ST], sdW[k][:, o * 128:(o + 1) * 128], p3hs[k],
                       k == 0, k == NSI - 1,
                       rb=[f"sdW{k}", f"p3hs{k}", "warena"], wb=[PSB[bank]], tag="sdm")
                E("vector", lambda e, d=p3g[o], p=pf[bank][:, 0:ST]:
                  e.tensor_tensor(out=d, in0=p, in1=p3swb, op=ALU.mult),
                  r=[PSB[bank], "p3swb"], w=[f"p3g{o}"], tag="sde")
                E("vector", lambda e, d=p3g[o], s=p3ac[o]:
                  e.tensor_tensor(out=d, in0=d, in1=s, op=ALU.add),
                  r=[f"p3g{o}", f"p3ac{o}"], w=[f"p3g{o}"], tag="sda")
                E("scalar", lambda e, d=p3g[o]: e.activation(out=d, in_=d, func=AF.Silu),
                  r=[f"p3g{o}"], w=[f"p3g{o}"], tag="gsl")
                E("vector", lambda e, d=p3g[o], s=p3s[o]:
                  e.tensor_tensor(out=d, in0=d, in1=s, op=ALU.mult),
                  r=[f"p3g{o}", f"p3s{o}"], w=[f"p3g{o}"], tag="gms")
                E("vector", lambda e, d=p3g[o], s=p3x[o]:
                  e.tensor_tensor(out=d, in0=d, in1=s, op=ALU.add),
                  r=[f"p3g{o}", f"p3x{o}"], w=[f"p3g{o}"], tag="gax")
            E("gpsimd", lambda e, d=ofm[:, cols].rearrange("(k p) t -> p k t", p=128),
              s=p3gv.rearrange("p (k t) -> p k t", k=NF): e.dma_start(out=d, in_=s),
              r=[f"p3g{k}" for k in range(NF)], w=["ofm"], dma=True, tag="outsp")


    for st in range(PREFIX if PREFIX else NST):
        p1a_supertile(st)
    if not PREFIX:
        if PHASES in ("1b", "2", "full"):
            p1b()
        if PHASES in ("2", "full"):
            p2()
        if PHASES == "full":
            p3()

    B.finalize()

    with nc.Block() as block:
        @block.sync
        def _(e):
            B.run_engine("sync", e, sems)

        @block.tensor
        def _(e):
            B.run_engine("tensor", e, sems)

        @block.vector
        def _(e):
            B.run_engine("vector", e, sems)

        @block.scalar
        def _(e):
            B.run_engine("scalar", e, sems)

        @block.gpsimd
        def _(e):
            B.run_engine("gpsimd", e, sems)
            for s in ("dq_sync_0", "dq_sync_1", "dq_scalar_0", "dq_scalar_1",
                      "dq_gpsimd_0", "dq_gpsimd_1"):
                if B.cnt.get(s):
                    e.wait_ge(sems[s], B.cnt[s])

    stack.close()
    return nc


def _host_prep(inputs):
    x = np.asarray(inputs["L0_attn_x"])
    f = np.ascontiguousarray
    d = {}
    d["ipT"] = f(np.asarray(inputs["L0_attn_in_proj_w"]).T)
    d["opT"] = f(np.asarray(inputs["L0_attn_out_proj_w"]).T)
    d["f1T"] = f(np.asarray(inputs["L0_attn_ffn_w1"]).T)
    d["f2T"] = f(np.asarray(inputs["L0_attn_ffn_w2"]).T)
    d["guT"] = f(np.asarray(inputs["L0_ffn_experts_gate_up"]).transpose(0, 2, 1))
    d["dnT"] = f(np.asarray(inputs["L0_ffn_experts_down"]).transpose(0, 2, 1))
    d["sgT"] = f(np.asarray(inputs["L0_ffn_shared_gate_w"]).T)
    d["suT"] = f(np.asarray(inputs["L0_ffn_shared_up_w"]).T)
    d["sdT"] = f(np.asarray(inputs["L0_ffn_shared_down_w"]).T)
    d["gTc"] = f(np.asarray(inputs["L0_ffn_gate_w"]).T)
    d["segc"] = f(np.asarray(inputs["L0_ffn_shared_expert_gate_w"]).reshape(8, 128).T)
    ln = np.zeros((128, 48), np.float32)
    for i, nm in enumerate(["ln0_w", "ln0_b", "ln1_w", "ln1_b", "ln2_w", "ln2_b"]):
        ln[:, i * 8:(i + 1) * 8] = np.asarray(inputs["L0_attn_" + nm]).reshape(8, 128).T
    d["lnc"] = ln
    d["cwc"] = f(np.asarray(inputs["L0_attn_conv_w"]).reshape(17, 128, 4)
                 .transpose(1, 0, 2).reshape(128, 68))
    d["cbc"] = f(np.asarray(inputs["L0_attn_conv_b"]).reshape(17, 128).T)
    d["b1c"] = f(np.asarray(inputs["L0_attn_ffn_b1"]).reshape(32, 128).T)
    d["b2c"] = f(np.asarray(inputs["L0_attn_ffn_b2"]).reshape(8, 128).T)
    d["nwp"] = f(np.asarray(inputs["L0_attn_norm_w"]).reshape(16, 128).T)
    D = np.asarray(inputs["L0_attn_D_param"]).astype(np.float32)
    dp = np.zeros((128, 16), np.float32)
    for c in range(16):
        dp[0:64, c] = D[2 * c]
        dp[64:128, c] = D[2 * c + 1]
    d["dpr"] = dp
    d["dtb"] = f(np.asarray(inputs["L0_attn_dt_bias"]).reshape(32, 1))
    d["alg"] = f(np.asarray(inputs["L0_attn_A_log"]).reshape(32, 1))
    idx = np.arange(128)
    d["c_lt_b"] = (idx[:, None] <= idx[None, :]).astype(BFNP)
    d["c_ut_b"] = (idx[:, None] > idx[None, :]).astype(BFNP)
    d["c_id_b"] = np.eye(128, dtype=BFNP)
    d["c_id_f"] = np.eye(128, dtype=np.float32)
    sel = np.zeros((32, 2048), BFNP)
    for c in range(16):
        for p_ in range(128):
            sel[2 * c + (p_ >= 64), c * 128 + p_] = 1
    d["c_sel"] = sel
    d["c_o1f"] = np.ones((1, 128), np.float32)
    d["c_ocb"] = np.ones((128, 1), BFNP)
    oh = np.zeros((8, 1024), np.float32)
    for e_ in range(8):
        oh[e_, e_ * 128:(e_ + 1) * 128] = 1
    d["c_oh8"] = oh
    fc = np.zeros((128, 8), np.float32)
    for fi, fv in enumerate([1e-5, 1.0, -0.5, -1.0, 0.5, 1.0 / 2048.0, 1.0 / 1024.0, 2.0]):
        fc[:, fi] = fv
    d["c_fc"] = fc
    xcore = []
    for i in range(8):
        xf = np.empty((1024, 4096), np.float32)
        xf[:, :2048] = x[2 * i].T
        xf[:, 2048:] = x[2 * i + 1].T
        xcore.append(xf)
    return d, xcore


_LAST = {}


def kernel(**inputs):
    shared, xcore = _host_prep(inputs)
    nc = build_program()
    in_maps = []
    for i in range(8):
        m = dict(shared)
        m["xfm"] = xcore[i]
        in_maps.append(m)
    res = run_bass_kernel_spmd(nc, in_maps, core_ids=list(range(8)))
    _LAST["res"] = res
    out = np.empty((16, 2048, 1024), np.float32)
    for i in range(8):
        o = res.results[i]["ofm"]
        out[2 * i] = o[:, :2048].T
        out[2 * i + 1] = o[:, 2048:].T
    return out


# version 09
# speedup vs baseline: 1.0360x; 1.0360x over previous
import numpy as np
import ml_dtypes
import concourse.bass as bass
import concourse.mybir as mybir
from concourse.bass_utils import run_bass_kernel_spmd

F32 = mybir.dt.float32
BF16 = mybir.dt.bfloat16
AF = mybir.ActivationFunctionType
ALU = mybir.AluOpType
AX = mybir.AxisListType
BFNP = ml_dtypes.bfloat16

T = 4096
DM = 1024
NF = 8
DIP = 4256
DI = 2048
NCV = 17
NH = 32
NPAIR = 16
FFN = 4096
ST = 256
NST = 16
Q = 128
NE = 8
MI = 1408
NMI = 11
GU2 = 2816
NGU = 22
SI = 2816
NSI = 22
QT = 1024
NQ = 4
TT = 512
EPS = 1e-5

SB_SZ = 229376

DEBUG = False
PHASES = "full"
PREFIX = 0


class Buf:
    __slots__ = ("w", "rs", "name")

    def __init__(self, name=""):
        self.w = None
        self.rs = []
        self.name = name


class Bld:
    def __init__(self):
        self.ops = []

    def emit(self, eng, fn, reads=(), writes=(), deps=(), dma=False, tag=""):
        i = len(self.ops)
        dl = list(deps)
        for b in reads:
            if b.w is not None:
                dl.append((b.w, True))
        for b in writes:
            if b.w is not None:
                dl.append((b.w, False))
            for t in b.rs:
                dl.append((t, False))
        self.ops.append((eng, fn, dl, dma, tag))
        for b in reads:
            b.rs.append(i)
        for b in writes:
            b.w = i
            b.rs = []
        if not hasattr(self, "last"):
            self.last = {}
        self.last[eng] = i
        return i

    def all_tokens(self):
        return [(i, False) for i in getattr(self, "last", {}).values()]

    def finalize(self):
        semof = {"tensor": "pe", "vector": "ve", "scalar": "sc", "gpsimd": "gp"}
        n = len(self.ops)
        eng_of = [o[0] for o in self.ops]
        is_dma = [o[3] for o in self.ops]
        need = [None] * n
        depended = set()
        for i in range(n):
            eng = eng_of[i]
            nd = []
            for (j, raw) in self.ops[i][2]:
                if j == i:
                    continue
                same = eng_of[j] == eng and not is_dma[j] and not is_dma[i]
                if same and eng == "tensor":
                    continue
                if same and not raw:
                    continue
                nd.append(j)
                depended.add(j)
            need[i] = nd
        G = 4
        val = {}
        cnt = {}
        qidx = {}
        self.fence = {}
        qgrp_end = {}
        per_q_ops = {}
        for i in range(n):
            if is_dma[i]:
                q = eng_of[i]
                per_q_ops.setdefault(q, []).append(i)
        for q, ops in per_q_ops.items():
            g = -1
            gsz = 0
            prev = -10
            for i in ops:
                if i != prev + 1 or gsz >= G:
                    g += 1
                    gsz = 0
                gsz += 1
                prev = i
                s = f"dq_{q}_{g % 2}"
                cnt[s] = cnt.get(s, 0) + 16
                qidx[i] = (q, g, s)
            acc = {}
            for i in ops:
                q_, g, s = qidx[i]
                acc[s] = acc.get(s, 0) + 16
                qgrp_end[(q, g)] = (s, acc[s])
            seen = set()
            for i in ops:
                q_, g, s = qidx[i]
                if g not in seen:
                    seen.add(g)
                    if g >= 2:
                        self.fence[i] = qgrp_end[(q, g - 2)]
        for i in range(n):
            if is_dma[i]:
                q, g, s = qidx[i]
                val[i] = (s, qgrp_end[(q, g)][1], 16)
            elif i in depended:
                s = semof[eng_of[i]]
                cnt[s] = cnt.get(s, 0) + 1
                val[i] = (s, cnt[s], 1)
        wm = {}
        waits = [None] * n
        for i in range(n):
            m = wm.setdefault(eng_of[i], {})
            wl = []
            for j in sorted(set(need[i])):
                s, v, a = val[j]
                if m.get(s, 0) < v:
                    wl.append((s, v))
                    m[s] = v
            waits[i] = wl
        self.waits = waits
        self.val = val
        self.cnt = cnt

    def run_engine(self, eng_name, e, sems):
        for i, (eng, fn, deps, dma, tag) in enumerate(self.ops):
            if eng != eng_name:
                continue
            for (s, v) in self.waits[i]:
                e.wait_ge(sems[s], v)
            if dma and i in self.fence:
                fs, fv = self.fence[i]
                e.wait_ge(sems[fs], fv)
            inst = fn(e)
            if i in self.val:
                s, v, a = self.val[i]
                inst.then_inc(sems[s], 16 if dma else a)


def build_program():
    nc = bass.Bass("TRN2", target_bir_lowering=False, debug=False)
    dt = nc.dram_tensor

    xfm = dt("xfm", [DM, T], F32, kind="ExternalInput").ap()
    ipT = dt("ipT", [DM, DIP], F32, kind="ExternalInput").ap()
    opT = dt("opT", [DI, DM], F32, kind="ExternalInput").ap()
    f1T = dt("f1T", [DM, FFN], F32, kind="ExternalInput").ap()
    f2T = dt("f2T", [FFN, DM], F32, kind="ExternalInput").ap()
    guT = dt("guT", [NE, DM, GU2], F32, kind="ExternalInput").ap()
    dnT = dt("dnT", [NE, MI, DM], F32, kind="ExternalInput").ap()
    sgT = dt("sgT", [DM, SI], F32, kind="ExternalInput").ap()
    suT = dt("suT", [DM, SI], F32, kind="ExternalInput").ap()
    sdT = dt("sdT", [SI, DM], F32, kind="ExternalInput").ap()
    gTc = dt("gTc", [DM, NE], F32, kind="ExternalInput").ap()
    segc = dt("segc", [128, 8], F32, kind="ExternalInput").ap()
    lnc = dt("lnc", [128, 48], F32, kind="ExternalInput").ap()
    cwc = dt("cwc", [128, 68], F32, kind="ExternalInput").ap()
    cbc = dt("cbc", [128, 17], F32, kind="ExternalInput").ap()
    b1c = dt("b1c", [128, 32], F32, kind="ExternalInput").ap()
    b2c = dt("b2c", [128, 8], F32, kind="ExternalInput").ap()
    nwp = dt("nwp", [128, 16], F32, kind="ExternalInput").ap()
    dpr = dt("dpr", [128, 16], F32, kind="ExternalInput").ap()
    dtb = dt("dtb", [32, 1], F32, kind="ExternalInput").ap()
    alg = dt("alg", [32, 1], F32, kind="ExternalInput").ap()
    c_lt_b = dt("c_lt_b", [128, 128], BF16, kind="ExternalInput").ap()
    c_ut_b = dt("c_ut_b", [128, 128], BF16, kind="ExternalInput").ap()
    c_id_b = dt("c_id_b", [128, 128], BF16, kind="ExternalInput").ap()
    c_id_f = dt("c_id_f", [128, 128], F32, kind="ExternalInput").ap()
    c_sel = dt("c_sel", [32, 2048], BF16, kind="ExternalInput").ap()
    c_o1f = dt("c_o1f", [1, 128], F32, kind="ExternalInput").ap()
    c_ocb = dt("c_ocb", [128, 1], BF16, kind="ExternalInput").ap()
    c_oh8 = dt("c_oh8", [8, 1024], F32, kind="ExternalInput").ap()
    c_fc = dt("c_fc", [128, 8], F32, kind="ExternalInput").ap()

    dbgk = "ExternalOutput" if DEBUG else "Internal"
    d_xn = dt("d_xn", [DM, T], BF16, kind=dbgk).ap()
    d_xb = dt("d_xb", [DM, T], F32, kind=dbgk).ap()
    d_xc = dt("d_xc", [DM, T], BF16, kind=dbgk).ap()
    d_w = dt("d_w", [8, T], F32, kind=dbgk).ap()
    d_s = dt("d_s", [DM, T], F32, kind=dbgk).ap()
    d_acc = dt("d_acc", [DM, T], F32, kind=dbgk).ap()
    d_f1 = dt("d_f1", [DM, FFN], BF16, kind="Internal").ap()
    d_f2 = dt("d_f2", [FFN, DM], BF16, kind="Internal").ap()
    d_gu = dt("d_gu", [NE, DM, GU2], BF16, kind="Internal").ap()
    d_dn = dt("d_dn", [NE, MI, DM], BF16, kind="Internal").ap()
    d_sg = dt("d_sg", [DM, SI], BF16, kind="Internal").ap()
    d_su = dt("d_su", [DM, SI], BF16, kind="Internal").ap()
    d_sd = dt("d_sd", [SI, DM], BF16, kind="Internal").ap()
    ofm = dt("ofm", [DM, T], F32, kind="ExternalOutput").ap()
    d_dbg = dt("d_dbg", [8, ST], F32, kind=dbgk).ap()
    d_dbg2 = dt("d_dbg2", [128, ST], F32, kind=dbgk).ap()
    d_dbg3 = dt("d_dbg3", [128, ST], F32, kind=dbgk).ap()

    nalloc = [0]
    off_ = [0]

    def at(shape, dtype, off):
        nalloc[0] += 1
        per = int(np.prod(shape[1:])) * mybir.dt.size(dtype)
        assert off + per <= SB_SZ, f"sbuf overflow {off}+{per}"
        h = nc.alloc_sbuf_tensor_at(f"sb{nalloc[0]}", list(shape), dtype,
                                    offset=off, align_bytes=32)
        return h.ap()

    def take(shape, dtype, n=1):
        per = int(np.prod(shape[1:])) * mybir.dt.size(dtype)
        per = -(-per // 32) * 32
        r = []
        for _ in range(n):
            r.append(at(shape, dtype, off_[0]))
            off_[0] += per
        return r if n > 1 else r[0]

    cursor = SB_SZ - nc.sbuf_bytes_remaining
    off_[0] = 17408
    assert cursor <= off_[0], ("bass internal allocs reach", cursor)
    s_lt_b = take([128, 128], BF16)
    s_ut_b = take([128, 128], BF16)
    s_id_b = take([128, 128], BF16)
    s_id_f = take([128, 128], F32)
    s_sel = take([32, 2048], BF16)
    s_o1f = take([1, 128], F32)
    s_ocb = take([128, 1], BF16)
    s_oh8 = take([8, 1024], F32)
    s_lnc = take([128, 48], F32)
    s_cwc = take([128, 68], F32)
    s_cbc = take([128, 17], F32)
    s_b1c = take([128, 32], F32)
    s_b2c = take([128, 8], F32)
    s_nwp = take([128, 16], F32)
    s_dpr = take([128, 16], F32)
    s_dtb = take([32, 1], F32)
    s_alg = take([32, 1], F32)
    s_acol = take([32, 1], F32)
    s_gtc = take([128, 8], F32, 8)
    s_seg = take([128, 8], F32)
    s_segb = take([128, 8], BF16)
    s_fc = take([128, 8], F32)
    assert off_[0] <= 30208, off_[0]

    WB = 30208
    W_SZ = 135168
    ipW = [at([128, DIP], BF16, WB + k * 8512) for k in range(NF)]
    opW = [at([128, DM], BF16, WB + 68096 + k * 2048) for k in range(16)]
    f1W = [at([128, FFN], BF16, WB + k * 8192) for k in range(NF)]
    f2W = [at([128, DM], BF16, WB + 65536 + k * 2048) for k in range(32)]
    guW = [at([128, GU2], BF16, WB + k * 5632) for k in range(NF)]
    dnW = [[at([128, DM], BF16, WB + 45056 + (p * NMI + k) * 2048)
            for k in range(NMI)] for p in range(2)]
    sgW = [at([128, SI], BF16, WB + k * 5632) for k in range(NF)]
    suW = [at([128, SI], BF16, WB + 45056 + k * 5632) for k in range(NF)]
    sdW = [at([128, DM], BF16, WB + 90112 + k * 2048) for k in range(NSI)]
    f1Wm = at([128, FFN * NF], BF16, WB)
    f2Wm = at([128, DM * 32], BF16, WB + 65536)
    guWm = at([128, GU2 * NF], BF16, WB)
    dnWm = [at([128, DM * NMI], BF16, WB + 45056 + p * NMI * 2048) for p in range(2)]
    sgWm = at([128, SI * NF], BF16, WB)
    suWm = at([128, SI * NF], BF16, WB + 45056)
    sdWm = at([128, DM * NSI], BF16, WB + 90112)
    p1bst = [at([128, 512], F32, WB + 131072 + i * 2048) for i in range(2)]

    AB = WB + W_SZ
    off_[0] = AB
    xavP = take([128, ST * NF], F32, 2)
    xaP = [[xavP[par][:, k * ST:(k + 1) * ST] for k in range(NF)] for par in range(2)]
    xmbv = take([128, ST * NF], BF16)
    xmb = [xmbv[:, k * ST:(k + 1) * ST] for k in range(NF)]
    xsc = take([128, ST], BF16, 2)
    xsq = take([128, ST], BF16, 2)
    xtmp = take([128, ST], F32, 2)
    zt = take([128, ST], BF16, NPAIR)
    xBCr = take([128, ST + 4], BF16, NCV)
    xst = take([128, ST], BF16, NPAIR)
    bct = take([128, ST], BF16)
    sCf = take([64, ST], BF16)
    HT = take([64, DI], BF16, 2)
    xsT = take([128, 128], BF16, NPAIR)
    assert off_[0] <= SB_SZ, off_[0]
    save_off = off_[0]
    off_[0] = WB + 100864
    sBT = take([128, 64], BF16)
    sBh = take([128, 64], BF16, 2)
    sGm = take([128, 128], BF16)
    sL2 = take([128, 128], BF16, 4)
    sM2 = take([128, 128], BF16, 4)
    sRL = take([128, 128], BF16, 4)
    sEb = take([128, 128], BF16, 2)
    sT1 = take([128, 128], BF16, 2)
    sE2 = take([128, 128], F32, 2)
    sdtT = take([128, 32], F32)
    slnd = take([128, 32], F32)
    saT = take([128, 32], F32)
    saTb = take([128, 32], BF16)
    swT = take([128, 32], F32)
    swd = take([128, 32], F32)
    sEf = take([32, 128], BF16)
    sLb = take([64, 32], F32)
    sLr = take([1, 32], F32)
    sLr2 = take([1, 32], F32)
    r_m, r_ms, r_t, r_v, r_r1, r_mr = take([1, ST], F32, 6)
    r_r2 = r_r1
    r_rr = r_r1
    dtraw = take([32, ST], F32)
    dtf = take([32, ST], F32)
    dtt = take([32, ST], F32)
    afm = dtraw
    g_lf = take([8, ST], F32)
    g_lt = take([128, 8], F32, 2)
    g_d = take([128, 8], F32)
    g_m1 = take([128, 1], F32)
    g_sp = take([128, 8], F32)
    g_ex = take([128, 8], F32)
    g_ss = take([128, 1], F32)
    g_w = take([128, 8], F32, 2)
    g_wf = g_lf
    Ast = take([128, 1024], F32, 2)
    AstB = take([128, 1024], BF16, 2)
    xns = take([128, ST * NF], BF16)
    assert off_[0] <= WB + W_SZ, off_[0]
    off_[0] = save_off

    STB = 512
    off_[0] = AB
    xcbv = take([128, STB * NF], BF16)
    xcb = [xcbv[:, k * STB:(k + 1) * STB] for k in range(NF)]
    g1b = take([128, STB], BF16, 32)
    xbfv = take([128, STB * NF], F32)
    xbf = [xbfv[:, k * STB:(k + 1) * STB] for k in range(NF)]
    assert off_[0] <= SB_SZ

    off_[0] = WB + 90112
    p2accv = take([128, QT * NF], F32)
    p2acc = [p2accv[:, k * QT:(k + 1) * QT] for k in range(NF)]
    assert off_[0] <= WB + W_SZ, off_[0]
    off_[0] = AB
    p2xnv = take([128, TT * NF], BF16)
    p2xn = [p2xnv[:, k * TT:(k + 1) * TT] for k in range(NF)]
    p2h = [take([128, TT], BF16, NMI) for _ in range(2)]
    p2wbc = take([128, TT], F32)
    p2tmp = take([128, TT], F32)
    p2w = take([8, QT], F32)
    p2st = take([128, 1408], F32, 4)
    assert off_[0] <= SB_SZ

    off_[0] = AB
    p3hs = take([128, ST], BF16, NSI)
    p3xnv = take([128, ST * NF], BF16)
    p3xn = [p3xnv[:, k * ST:(k + 1) * ST] for k in range(NF)]
    p3sv = take([128, ST * NF], F32)
    p3s = [p3sv[:, k * ST:(k + 1) * ST] for k in range(NF)]
    p3xv = take([128, ST * NF], F32)
    p3x = [p3xv[:, k * ST:(k + 1) * ST] for k in range(NF)]
    p3acv = take([128, ST * NF], F32)
    p3ac = [p3acv[:, k * ST:(k + 1) * ST] for k in range(NF)]
    p3gv = take([128, ST * NF], F32)
    p3g = [p3gv[:, k * ST:(k + 1) * ST] for k in range(NF)]
    p3sw = take([1, ST], F32)
    p3swb = take([128, ST], F32)
    p3st = take([128, 1408], F32, 2)
    assert off_[0] <= SB_SZ

    import contextlib
    stack = contextlib.ExitStack()
    pf = [stack.enter_context(nc.psum_tensor(f"pf{i}", [128, 512], F32)).ap()
          for i in range(6)]
    pb = [stack.enter_context(nc.psum_tensor(f"pb{i}", [128, 1024], BF16)).ap()
          for i in range(2)]
    sem_names = ["pe", "ve", "sc", "gp",
                 "dq_sync_0", "dq_sync_1", "dq_scalar_0", "dq_scalar_1",
                 "dq_gpsimd_0", "dq_gpsimd_1"]
    sems = {s: stack.enter_context(nc.semaphore(s)) for s in sem_names}

    B = Bld()
    bufs = {}

    def bf(x):
        if isinstance(x, Buf):
            return x
        if x not in bufs:
            bufs[x] = Buf(x)
        return bufs[x]

    PSB = [bf(f"ps{i}") for i in range(8)]

    FCVALS = [1e-5, 1.0, -0.5, -1.0, 0.5, 1.0 / 2048.0, 1.0 / 1024.0, 2.0]
    for fi, fv in enumerate(FCVALS):
        nc.const_aps.register(F32, fv, s_fc[:, fi:fi + 1])

    def E(eng, fn, r=(), w=(), deps=(), dma=False, tag=""):
        return B.emit(eng, fn, [bf(x) for x in r], [bf(x) for x in w], deps, dma, tag)

    def mm(out, lhsT, rhs, start, stop, rb=(), wb=(), tp=None, tag="mm"):
        def f(e, o=out, l=lhsT, rr=rhs, s=start, p=stop, t=tp):
            if t is None:
                return e.matmul(out=o, lhsT=l, rhs=rr, start=s, stop=p)
            return e.matmul(out=o, lhsT=l, rhs=rr, start=s, stop=p, tile_position=t)
        return E("tensor", f, r=rb, w=wb, tag=tag)

    def tpose(out, in_, ident, rb, wb, tag="tp"):
        def f(e, o=out, i_=in_, d=ident):
            return e.transpose(out=o, in_=i_, identity=d)
        return E("tensor", f, r=rb, w=wb, tag=tag)

    cl = [(s_lt_b, c_lt_b, "c_lt"), (s_ut_b, c_ut_b, "c_ut"), (s_id_b, c_id_b, "c_idb"),
          (s_id_f, c_id_f, "c_idf"), (s_sel, c_sel, "c_sel"), (s_o1f, c_o1f, "c_o1f"),
          (s_ocb, c_ocb, "c_ocb"), (s_oh8, c_oh8, "c_oh8"), (s_lnc, lnc, "c_lnc"),
          (s_cwc, cwc, "c_cwc"), (s_cbc, cbc, "c_cbc"), (s_b1c, b1c, "c_b1c"),
          (s_b2c, b2c, "c_b2c"), (s_nwp, nwp, "c_nwp"), (s_dpr, dpr, "c_dpr"),
          (s_dtb, dtb, "c_dtb"), (s_alg, alg, "c_alg"), (s_seg, segc, "c_seg"),
          (s_fc, c_fc, "c_fc")]
    for (dst, src, nm) in cl:
        if PREFIX and nm not in ("c_ocb", "c_fc", "c_alg", "c_dtb", "c_lt"):
            continue
        E("sync", lambda e, d=dst, s=src: e.dma_start(out=d, in_=s),
          w=[nm], dma=True, tag="constld")
    for k in range(8):
        if PREFIX:
            break
        E("sync", lambda e, d=s_gtc[k], s=gTc[k * 128:(k + 1) * 128, :]:
          e.dma_start(out=d, in_=s), w=[f"gtc{k}"], dma=True, tag="gtcld")
    E("scalar", lambda e: e.activation(out=s_acol, in_=s_alg, func=AF.Exp),
      r=["c_alg"], w=["acol"], tag="acol")
    E("vector", lambda e: e.tensor_scalar(out=s_acol, in0=s_acol, scalar1=-1.0,
                                          scalar2=None, op0=ALU.mult),
      r=["acol"], w=["acol"], tag="acol2")
    E("vector", lambda e: e.tensor_copy(out=s_segb, in_=s_seg),
      r=["c_seg"], w=["segb"], tag="segb")
    E("vector", lambda e: e.memset(HT[0][:], 0.0), w=["HT0"], tag="ht0")
    E("vector", lambda e: e.memset(HT[1][:], 0.0), w=["HT1"], tag="ht1")

    si = [0]

    def stage_conv(src, dstw, dstname, cols, eng="vector"):
        j = si[0] % 2
        sb = f"Ast{j}"
        E("sync", lambda e, d=Ast[j][:, 0:cols], s=src: e.dma_start(out=d, in_=s),
          w=[sb], dma=True, tag="wld")
        E(eng, lambda e, d=dstw, s=Ast[j][:, 0:cols]: e.tensor_copy(out=d, in_=s),
          r=[sb, "warena"], w=[dstname], tag="wcv")
        si[0] += 1

    if not PREFIX:
        for k in range(NF):
            for c4 in range(5):
                c0, c1 = c4 * 1024, min(DIP, (c4 + 1) * 1024)
                stage_conv(ipT[k * 128:(k + 1) * 128, c0:c1], ipW[k][:, c0:c1],
                           f"ipW{k}", c1 - c0)
        for k in range(16):
            stage_conv(opT[k * 128:(k + 1) * 128, :], opW[k][:, 0:1024], f"opW{k}", 1024)

    def w_stream(srcs, cve="vector"):
        si2 = [0]

        def one(srcap, dstap, cols, dstname, stg, spfx):
            j = si2[0] % len(stg)
            E("sync", lambda e, d=stg[j][:, 0:cols], s=srcap: e.dma_start(out=d, in_=s),
              w=[f"{spfx}{j}"], dma=True, tag="wsl")
            E(cve, lambda e, d=dstap, s=stg[j][:, 0:cols]: e.tensor_copy(out=d, in_=s),
              r=[f"{spfx}{j}", "warena"], w=[dstname], tag="wsc")
            si2[0] += 1
        for args in srcs:
            one(*args)

    def ln_rows(denom, tagp):
        E("vector", lambda e, p=pf[4][0:1, 0:ST]:
          e.tensor_scalar(out=r_m, in0=p, scalar1=1.0 / denom, scalar2=None, op0=ALU.mult),
          r=[PSB[4]], w=["r_m"], tag=tagp + "m")
        E("vector", lambda e, p=pf[5][0:1, 0:ST]:
          e.tensor_scalar(out=r_ms, in0=p, scalar1=1.0 / denom, scalar2=None, op0=ALU.mult),
          r=[PSB[5]], w=["r_ms"], tag=tagp + "ms")
        E("vector", lambda e: e.tensor_tensor(out=r_t, in0=r_m, in1=r_m, op=ALU.mult),
          r=["r_m"], w=["r_t"], tag=tagp + "t")
        E("vector", lambda e: e.tensor_tensor(out=r_v, in0=r_ms, in1=r_t, op=ALU.subtract),
          r=["r_ms", "r_t"], w=["r_v"], tag=tagp + "v")
        E("scalar", lambda e: e.activation(out=r_t, in_=r_v, func=AF.Ln, bias=EPS),
          r=["r_v", "c_fc"], w=["r_t"], tag=tagp + "ln")
        E("scalar", lambda e: e.activation(out=r_r1, in_=r_t, func=AF.Exp, scale=-0.5),
          r=["r_t", "c_fc"], w=["r_r1"], tag=tagp + "ex")
        E("vector", lambda e: e.tensor_tensor(out=r_mr, in0=r_m, in1=r_r1, op=ALU.mult),
          r=["r_m", "r_r1"], w=["r_mr"], tag=tagp + "mr")

    def bcast_row(row, rowname, bank, tagp):
        return mm(pf[bank][:, 0:ST], s_o1f, row, True, True,
                  rb=["c_o1f", rowname], wb=[PSB[bank]], tag=tagp)

    def stats_pass(srcs, srcnames, tagp):
        nsrc = len(srcs)
        for k in range(nsrc):
            sc_, sq_ = xsc[k % 2], xsq[k % 2]
            E("vector", lambda e, d=sc_, s=srcs[k]: e.tensor_copy(out=d, in_=s),
              r=[srcnames[k]], w=[f"xsc{k % 2}"], tag=tagp + "c")
            mm(pf[4][0:1, 0:ST], s_ocb, sc_, k == 0, k == nsrc - 1,
               rb=["c_ocb", f"xsc{k % 2}"], wb=[PSB[4]], tag=tagp + "m")
            E("scalar", lambda e, d=sq_, s=srcs[k]: e.activation(out=d, in_=s, func=AF.Square),
              r=[srcnames[k]], w=[f"xsq{k % 2}"], tag=tagp + "q")
            mm(pf[5][0:1, 0:ST], s_ocb, sq_, k == 0, k == nsrc - 1,
               rb=["c_ocb", f"xsq{k % 2}"], wb=[PSB[5]], tag=tagp + "qm")

    def ssd_chunk(st, qq, seq):
        qs = slice(qq * Q, (qq + 1) * Q)
        for j in range(NPAIR):
            bk = 6 + (j % 2)
            sl = slice(((j // 2) % 4) * 128, ((j // 2) % 4) * 128 + 128)
            tpose(pb[j % 2][:, sl], xst[j][:, qs], s_id_b,
                  rb=[f"xst{j}", "c_idb"], wb=[PSB[bk]], tag="xsT")
            if j % 4 < 2:
                E("vector", lambda e, d=xsT[j], s=pb[j % 2][:, sl]:
                  e.tensor_copy(out=d, in_=s), r=[PSB[bk]], w=[f"xsT{j}"], tag="xsTe")
            else:
                E("scalar", lambda e, d=xsT[j], s=pb[j % 2][:, sl]:
                  e.activation(out=d, in_=s, func=AF.Copy),
                  r=[PSB[bk]], w=[f"xsT{j}"], tag="xsTe")
        tpose(pb[0][:, 512:576], bct[0:64, qs], s_id_b[0:64, 0:64],
              rb=["bct", "c_idb"], wb=[PSB[6]], tag="BT")
        E("vector", lambda e, s=pb[0][:, 512:576]: e.tensor_copy(out=sBT, in_=s),
          r=[PSB[6]], w=["sBT"], tag="BTe")
        mm(pf[5][0:128, 0:128], bct[0:64, qs], sCf[:, qs], True, True,
           rb=["bct", "sCf"], wb=[PSB[5]], tag="G")
        E("vector", lambda e, p=pf[5][0:128, 0:128]:
          e.tensor_tensor(out=sGm, in0=p, in1=s_lt_b, op=ALU.mult),
          r=[PSB[5], "c_lt"], w=["sGm"], tag="Gm")
        tpose(pf[5][:, 128:160], dtf[:, qs], s_id_f[0:32, 0:32],
              rb=["dtf", "c_idf"], wb=[PSB[5]], tag="dtT")
        E("vector", lambda e, s=pf[5][:, 128:160]: e.tensor_copy(out=sdtT, in_=s),
          r=[PSB[5]], w=["sdtT"], tag="dtTe")
        E("scalar", lambda e, s=pf[5][:, 128:160]: e.activation(out=slnd, in_=s, func=AF.Ln),
          r=[PSB[5]], w=["slnd"], tag="lnd")
        tpose(pf[5][:, 160:192], afm[:, qs], s_id_f[0:32, 0:32],
              rb=["dtraw", "c_idf"], wb=[PSB[5]], tag="aT")
        E("vector", lambda e, s=pf[5][:, 160:192]: e.tensor_copy(out=saT, in_=s),
          r=[PSB[5]], w=["saT"], tag="aTe")
        E("vector", lambda e: e.tensor_copy(out=saTb, in_=saT),
          r=["saT"], w=["saTb"], tag="aTb")
        mm(pf[5][0:32, 192:320], saTb, s_lt_b, True, True,
           rb=["saTb", "c_lt"], wb=[PSB[5]], tag="sfm")
        E("scalar", lambda e, p=pf[5][0:32, 192:320]:
          e.activation(out=sEf, in_=p, func=AF.Exp), r=[PSB[5]], w=["sEf"], tag="Ef")
        mm(pf[5][:, 320:352], s_ut_b, saTb, True, True,
           rb=["c_ut", "saTb"], wb=[PSB[5]], tag="wtm")
        E("scalar", lambda e, p=pf[5][:, 320:352]:
          e.activation(out=swT, in_=p, func=AF.Exp), r=[PSB[5]], w=["swT"], tag="wte")
        E("vector", lambda e: e.tensor_tensor(out=swd, in0=swT, in1=sdtT, op=ALU.mult),
          r=["swT", "sdtT"], w=["swd"], tag="wtd")
        E("scalar", lambda e: e.activation(out=sLr, in_=saT[0:1, :], func=AF.Exp),
          r=["saT"], w=["sLr"], tag="La")
        E("vector", lambda e: e.tensor_tensor(out=sLr2, in0=sLr, in1=swT[0:1, :], op=ALU.mult),
          r=["sLr", "swT"], w=["sLr2"], tag="Lb")
        mm(pf[5][0:64, 352:384], s_o1f[:, 0:64], sLr2, True, True,
           rb=["c_o1f", "sLr2"], wb=[PSB[5]], tag="Lm")
        E("vector", lambda e, p=pf[5][0:64, 352:384]: e.tensor_copy(out=sLb, in_=p),
          r=[PSB[5]], w=["sLb"], tag="Le")
        for w in range(8):
            hs4 = [4 * w + i for i in range(4)]
            prs = [2 * w, 2 * w + 1]
            for h in hs4:
                E("vector", lambda e, d=sRL[h % 4], sc1=saT[:, h:h + 1]:
                  e.tensor_scalar(out=d, in0=s_lt_b, scalar1=sc1, scalar2=None,
                                  op0=ALU.mult),
                  r=["c_lt", "saT"], w=[f"sRL{h % 4}"], tag="rl")
            for h in hs4:
                sl = slice((h % 4 // 2) * 128, (h % 4 // 2) * 128 + 128)
                mm(pf[h % 2][:, sl], s_ut_b, sRL[h % 4], True, True,
                   rb=["c_ut", f"sRL{h % 4}"], wb=[PSB[h % 2]], tag="dif")
            for h in hs4:
                sl = slice((h % 4 // 2) * 128, (h % 4 // 2) * 128 + 128)
                E("scalar", lambda e, d=sL2[h % 4], pp=pf[h % 2][:, sl],
                  b=slnd[:, h:h + 1]:
                  e.activation(out=d, in_=pp, func=AF.Exp, bias=b),
                  r=[PSB[h % 2], "slnd"], w=[f"sL2{h % 4}"], tag="l2")
            for h in hs4:
                E("gpsimd", lambda e, d=sM2[h % 4], s=sL2[h % 4]:
                  e.tensor_tensor(out=d, in0=s, in1=sGm, op=ALU.mult),
                  r=[f"sL2{h % 4}", "sGm"], w=[f"sM2{h % 4}"], tag="m2")
            for p in prs:
                bI = 4 + p % 2
                mm(pf[bI][:, 128:256], s_sel[:, p * 128:(p + 1) * 128], sEf, True, True,
                   rb=["c_sel", "sEf"], wb=[PSB[bI]], tag="Ebm")
                mm(pf[bI][:, 0:128], HT[seq][:, p * 128:(p + 1) * 128], sCf[:, qs],
                   True, True, rb=[f"HT{seq}", "sCf"], wb=[PSB[bI]], tag="Yi")
                E("scalar", lambda e, d=sEb[p % 2], pp=pf[bI][:, 128:256]:
                  e.activation(out=d, in_=pp, func=AF.Copy),
                  r=[PSB[bI]], w=[f"sEb{p % 2}"], tag="Ebe")
                E("vector", lambda e, d=sT1[p % 2], pp=pf[bI][:, 0:128], eb=sEb[p % 2]:
                  e.tensor_tensor(out=d, in0=pp, in1=eb, op=ALU.mult),
                  r=[PSB[bI], f"sEb{p % 2}"], w=[f"sT1{p % 2}"], tag="t1")
            for h in hs4:
                p = h // 2
                hi = h % 2
                bP = 2 + p % 2
                mm(pf[bP][hi * 64:(hi + 1) * 64, 0:128],
                   xsT[p][:, hi * 64:(hi + 1) * 64], sM2[h % 4], True, True,
                   rb=[f"xsT{p}", f"sM2{h % 4}"], wb=[PSB[bP]],
                   tp=(0, hi * 64), tag="Ym")
            for h in hs4:
                p = h // 2
                hi = h % 2
                bP = 2 + p % 2
                E("vector", lambda e, d=sBh[h % 2], sc1=swd[:, h:h + 1]:
                  e.tensor_scalar(out=d, in0=sBT, scalar1=sc1, scalar2=None,
                                  op0=ALU.mult),
                  r=["sBT", "swd"], w=[f"sBh{h % 2}"], tag="bh")
                mm(pf[bP][0:64, 128 + hi * 64:128 + (hi + 1) * 64], sBh[h % 2],
                   xsT[p][:, hi * 64:(hi + 1) * 64], True, True,
                   rb=[f"sBh{h % 2}", f"xsT{p}"], wb=[PSB[bP]], tag="dH")
                E("vector", lambda e, d=HT[seq][:, h * 64:(h + 1) * 64],
                  sc1=sLb[:, h:h + 1],
                  pp=pf[bP][0:64, 128 + hi * 64:128 + (hi + 1) * 64]:
                  e.scalar_tensor_tensor(out=d, in0=d, scalar=sc1, in1=pp,
                                         op0=ALU.mult, op1=ALU.add),
                  r=[f"HT{seq}", "sLb", PSB[bP]], w=[f"HT{seq}"], tag="Hup")
            for p in prs:
                bP = 2 + p % 2
                E("vector", lambda e, d=sE2[p % 2], t1=sT1[p % 2], pp=pf[bP][:, 0:128]:
                  e.tensor_tensor(out=d, in0=t1, in1=pp, op=ALU.add),
                  r=[f"sT1{p % 2}", PSB[bP]], w=[f"sE2{p % 2}"], tag="e2")
                E("vector", lambda e, d=xst[p][:, qs], sc1=s_dpr[:, p:p + 1],
                  e2=sE2[p % 2]:
                  e.scalar_tensor_tensor(out=d, in0=d, scalar=sc1, in1=e2,
                                         op0=ALU.mult, op1=ALU.add),
                  r=[f"xst{p}", "c_dpr", f"sE2{p % 2}"], w=[f"xst{p}"], tag="e3")

    def emit_xld(st):
        par = st % 2
        cols = slice(st * ST, (st + 1) * ST)
        E("sync", lambda e, d=xavP[par].rearrange("p (k t) -> p k t", k=NF),
          s=xfm[:, cols].rearrange("(k p) t -> p k t", p=128):
          e.dma_start(out=d, in_=s), r=["aarena"],
          w=[f"xa{par}_{k}" for k in range(NF)], dma=True, tag="xld")

    def p1a_supertile(st):
        par = st % 2
        xa = xaP[par]
        cols = slice(st * ST, (st + 1) * ST)
        seq = st // 8
        if st == 0:
            emit_xld(0)
        if st + 1 < NST:
            emit_xld(st + 1)
        stats_pass(xa, [f"xa{par}_{k}" for k in range(NF)], "s1")
        ln_rows(DM, "r1")
        if DEBUG and st == 0:
            for di, (rr, rn) in enumerate([(r_m, "r_m"), (r_ms, "r_ms"), (r_v, "r_v"),
                                           (r_r1, "r_r1"), (r_mr, "r_mr"), (r_t, "r_t")]):
                E("gpsimd", lambda e, d=d_dbg[di:di + 1, :], s=rr:
                  e.dma_start(out=d, in_=s), r=[rn], w=["d_dbg"], dma=True, tag="dbg")
            E("vector", lambda e: e.tensor_copy(out=xtmp[0], in_=xsq[1]),
              r=["xsq1"], w=["xtmp0"], tag="dbgc")
            E("gpsimd", lambda e: e.dma_start(out=d_dbg2, in_=xtmp[0]),
              r=["xtmp0"], w=["d_dbg2"], dma=True, tag="dbg2")
            E("vector", lambda e: e.tensor_copy(out=xtmp[1], in_=xsq[0]),
              r=["xsq0"], w=["xtmp1"], tag="dbgc2")
            E("gpsimd", lambda e: e.dma_start(out=d_dbg3, in_=xtmp[1]),
              r=["xtmp1"], w=["d_dbg3"], dma=True, tag="dbg3")
        if PREFIX:
            E("gpsimd", lambda e: e.dma_start(out=d_dbg2, in_=xa[7]),
              r=["xa7"], w=["d_dbg2"], dma=True, tag="dbgxa")
            E("vector", lambda e: e.tensor_copy(out=xtmp[0], in_=xsq[1]),
              r=["xsq1"], w=["xtmp0"], tag="dbq")
            E("gpsimd", lambda e: e.dma_start(out=d_dbg3, in_=xtmp[0]),
              r=["xtmp0"], w=["d_dbg3"], dma=True, tag="dbq2")
            return
        bcast_row(r_r1, "r_r1", 4, "bc_r1")
        bcast_row(r_mr, "r_mr", 5, "bc_mr1")
        for k in range(NF):
            E("vector", lambda e, d=xa[k], p=pf[4][:, 0:ST]:
              e.tensor_tensor(out=d, in0=d, in1=p, op=ALU.mult),
              r=[f"xa{par}_{k}", PSB[4]], w=[f"xa{par}_{k}"], tag="xn1")
            E("vector", lambda e, d=xa[k], p=pf[5][:, 0:ST]:
              e.tensor_tensor(out=d, in0=d, in1=p, op=ALU.subtract),
              r=[f"xa{par}_{k}", PSB[5]], w=[f"xa{par}_{k}"], tag="xn2")
        for k in range(NF):
            mm(pf[3][0:8, 0:ST], s_gtc[k], xa[k], k == 0, k == NF - 1,
               rb=[f"gtc{k}", f"xa{par}_{k}"], wb=[PSB[3]], tag="gl")
        E("vector", lambda e, p=pf[3][0:8, 0:ST]: e.tensor_copy(out=g_lf, in_=p),
          r=[PSB[3]], w=["g_lf"], tag="glev")
        for k in range(NF):
            E("vector", lambda e, d=xns[:, k * ST:(k + 1) * ST], s=xa[k]:
              e.tensor_copy(out=d, in_=s), r=[f"xa{par}_{k}"], w=[f"xns{k}"], tag="xnb")
        E("gpsimd", lambda e, d=d_xn[:, cols].rearrange("(k p) t -> p k t", p=128),
          s=xns.rearrange("p (k t) -> p k t", k=NF): e.dma_start(out=d, in_=s),
          r=[f"xns{k}" for k in range(NF)],
          w=[f"dxn{st}_{k}" for k in range(NF)], dma=True, tag="xnsp")
        E("vector", lambda e: e.tensor_tensor(out=r_t, in0=r_r1, in1=r_r1, op=ALU.mult),
          r=["r_r1"], w=["r_t"], tag="r2a")
        E("vector", lambda e: e.tensor_tensor(out=r_v, in0=r_v, in1=r_t, op=ALU.mult),
          r=["r_v", "r_t"], w=["r_v"], tag="r2b")
        E("scalar", lambda e: e.activation(out=r_t, in_=r_v, func=AF.Ln, bias=EPS),
          r=["r_v", "c_fc"], w=["r_t"], tag="r2c")
        E("scalar", lambda e: e.activation(out=r_r2, in_=r_t, func=AF.Exp, scale=-0.5),
          r=["r_t", "c_fc"], w=["r_r1"], tag="r2d")
        bcast_row(r_r2, "r_r1", 4, "bc_r2")
        for k in range(NF):
            E("vector", lambda e, d=xa[k], p=pf[4][:, 0:ST]:
              e.tensor_tensor(out=d, in0=d, in1=p, op=ALU.mult),
              r=[f"xa{par}_{k}", PSB[4]], w=[f"xa{par}_{k}"], tag="l0a")
            E("vector", lambda e, d=xa[k], w1=s_lnc[:, k:k + 1], bb=s_lnc[:, 8 + k:9 + k]:
              e.tensor_scalar(out=d, in0=d, scalar1=w1, scalar2=bb,
                              op0=ALU.mult, op1=ALU.add),
              r=[f"xa{par}_{k}", "c_lnc"], w=[f"xa{par}_{k}"], tag="l0b")
        for h2 in range(2):
            sl = slice(h2 * 128, (h2 + 1) * 128)
            tpose(pf[5][:, 384:392], g_lf[:, sl], s_id_f[0:8, 0:8],
                  rb=["g_lf", "c_idf"], wb=[PSB[5]], tag="glt")
            E("vector", lambda e, d=g_lt[h2], p=pf[5][:, 384:392]:
              e.tensor_copy(out=d, in_=p), r=[PSB[5]], w=[f"glt{h2}"], tag="glte")
        for h2 in range(2):
            lt = g_lt[h2]
            lb = f"glt{h2}"
            E("vector", lambda e, s=lt: e.tensor_reduce(out=g_m1, in_=s, axis=AX.X,
                                                        op=ALU.max),
              r=[lb], w=["gm1"], tag="gm1")
            E("vector", lambda e, s=lt: e.tensor_scalar(out=g_d, in0=s, scalar1=g_m1,
                                                        scalar2=None, op0=ALU.subtract),
              r=[lb, "gm1"], w=["gd"], tag="gd")
            E("vector", lambda e: e.tensor_scalar(out=g_sp, in0=g_d, scalar1=0.0,
                                                  scalar2=-1e30, op0=ALU.is_ge,
                                                  op1=ALU.mult),
              r=["gd"], w=["gsp"], tag="gsp")
            E("vector", lambda e, s=lt: e.tensor_tensor(out=g_sp, in0=g_sp, in1=s,
                                                        op=ALU.add),
              r=["gsp", lb], w=["gsp"], tag="gsp2")
            E("vector", lambda e: e.tensor_reduce(out=g_m1, in_=g_sp, axis=AX.X,
                                                  op=ALU.max),
              r=["gsp"], w=["gm1"], tag="gm2")
            E("vector", lambda e, s=lt: e.tensor_scalar(out=g_sp, in0=s, scalar1=g_m1,
                                                        scalar2=None, op0=ALU.subtract),
              r=[lb, "gm1"], w=["gsp"], tag="gd2")
            E("vector", lambda e: e.tensor_scalar(out=g_sp, in0=g_sp, scalar1=0.0,
                                                  scalar2=None, op0=ALU.is_ge),
              r=["gsp"], w=["gsp"], tag="gmask")
            E("scalar", lambda e: e.activation(out=g_ex, in_=g_d, func=AF.Exp),
              r=["gd"], w=["gex"], tag="gex")
            E("vector", lambda e: e.tensor_reduce(out=g_ss, in_=g_ex, axis=AX.X,
                                                  op=ALU.add),
              r=["gex"], w=["gss"], tag="gss")
            E("vector", lambda e: e.reciprocal(out=g_ss, in_=g_ss),
              r=["gss"], w=["gss"], tag="grc")
            E("vector", lambda e: e.tensor_scalar(out=g_ex, in0=g_ex, scalar1=g_ss,
                                                  scalar2=None, op0=ALU.mult),
              r=["gex", "gss"], w=["gex"], tag="gsc")
            E("vector", lambda e, d=g_w[h2]: e.tensor_tensor(out=d, in0=g_ex, in1=g_sp,
                                                             op=ALU.mult),
              r=["gex", "gsp"], w=[f"gw{h2}"], tag="gw")
            tpose(pf[5][0:8, 384:512], g_w[h2], s_id_f,
                  rb=[f"gw{h2}", "c_idf"], wb=[PSB[5]], tag="gwt")
            E("vector", lambda e, d=g_wf[:, h2 * 128:(h2 + 1) * 128], p=pf[5][0:8, 384:512]:
              e.tensor_copy(out=d, in_=p), r=[PSB[5]], w=["g_lf"], tag="gwfe")
        E("gpsimd", lambda e, d=d_w[:, cols], s=g_wf: e.dma_start(out=d, in_=s),
          r=["g_lf"], w=[f"dw{st}"], dma=True, tag="wsp")
        stats_pass(xa, [f"xa{par}_{k}" for k in range(NF)], "s3")
        ln_rows(DM, "r3")
        bcast_row(r_r1, "r_r1", 4, "bc_r3")
        bcast_row(r_mr, "r_mr", 5, "bc_mr3")
        for k in range(NF):
            tm = xtmp[k % 2]
            E("vector", lambda e, d=tm, s=xa[k], p=pf[4][:, 0:ST]:
              e.tensor_tensor(out=d, in0=s, in1=p, op=ALU.mult),
              r=[f"xa{par}_{k}", PSB[4]], w=[f"xtmp{k % 2}"], tag="l1a")
            E("vector", lambda e, d=tm, p=pf[5][:, 0:ST]:
              e.tensor_tensor(out=d, in0=d, in1=p, op=ALU.subtract),
              r=[f"xtmp{k % 2}", PSB[5]], w=[f"xtmp{k % 2}"], tag="l1b")
            E("vector", lambda e, d=xmb[k], s=tm, w1=s_lnc[:, 16 + k:17 + k],
              bb=s_lnc[:, 24 + k:25 + k]:
              e.tensor_scalar(out=d, in0=s, scalar1=w1, scalar2=bb,
                              op0=ALU.mult, op1=ALU.add),
              r=[f"xtmp{k % 2}", "c_lnc"], w=[f"xmb{k}"], tag="l1c")
        for o in range(34):
            o0 = o * 128
            o1 = min(DIP, o0 + 128)
            ow = o1 - o0
            bank = o % 3
            for k in range(NF):
                mm(pf[bank][0:ow, 0:ST], ipW[k][:, o0:o1], xmb[k], k == 0, k == NF - 1,
                   rb=[f"ipW{k}", f"xmb{k}", "warena"], wb=[PSB[bank]], tag="ipm")
            if o < 16:
                E("scalar", lambda e, d=zt[o], p=pf[bank][0:128, 0:ST]:
                  e.activation(out=d, in_=p, func=AF.Silu),
                  r=[PSB[bank]], w=[f"zt{o}"], tag="zev")
            elif o < 33:
                j = o - 16
                E("scalar", lambda e, d=xBCr[j][:, 3:3 + ST], p=pf[bank][0:128, 0:ST]:
                  e.activation(out=d, in_=p, func=AF.Copy),
                  r=[PSB[bank]], w=[f"xBC{j}"], tag="xbcev")
            else:
                E("scalar", lambda e, p=pf[bank][0:32, 0:ST]:
                  e.activation(out=dtraw, in_=p, func=AF.Copy),
                  r=[PSB[bank]], w=["dtraw"], tag="dtev")
        if st == 0 or st == 8:
            for j in range(NCV):
                E("vector", lambda e, d=xBCr[j][:, 0:3]: e.memset(d, 0.0),
                  w=[f"xBC{j}"], tag="cvz")
        for j in range(NCV):
            dst = xst[j] if j < 16 else bct
            dn = f"xst{j}" if j < 16 else "bct"
            cs = xtmp[j % 2]
            E("vector", lambda e, d=cs, s=xBCr[j][:, 0:ST], w1=s_cwc[:, 4 * j:4 * j + 1]:
              e.tensor_scalar(out=d, in0=s, scalar1=w1, scalar2=None, op0=ALU.mult),
              r=[f"xBC{j}", "c_cwc"], w=[f"xtmp{j % 2}"], tag="cv0")
            for tap in (1, 2, 3):
                E("vector", lambda e, d=cs, s=xBCr[j][:, tap:tap + ST],
                  w1=s_cwc[:, 4 * j + tap:4 * j + tap + 1]:
                  e.scalar_tensor_tensor(out=d, in0=s, scalar=w1, in1=d,
                                         op0=ALU.mult, op1=ALU.add),
                  r=[f"xBC{j}", f"xtmp{j % 2}", "c_cwc"], w=[f"xtmp{j % 2}"], tag="cvt")
            E("scalar", lambda e, d=dst, s=cs, bb=s_cbc[:, j:j + 1]:
              e.activation(out=d, in_=s, func=AF.Silu, bias=bb),
              r=[f"xtmp{j % 2}", "c_cbc"], w=[dn], tag="cvs")
            E("gpsimd", lambda e, d=xBCr[j][:, 0:3], s=xBCr[j][:, ST:ST + 3]:
              e.tensor_copy(out=d, in_=s), r=[f"xBC{j}"], w=[f"xBC{j}"], tag="cvh")
        E("sync", lambda e: e.dma_start(out=sCf, in_=bct[64:128, :]),
          r=["bct"], w=["sCf"], dma=True, tag="crel")
        E("scalar", lambda e: e.activation(out=dtt, in_=dtraw, func=AF.Abs, bias=s_dtb),
          r=["dtraw", "c_dtb"], w=["dtt"], tag="dt1")
        E("scalar", lambda e: e.activation(out=dtt, in_=dtt, func=AF.Exp, scale=-1.0),
          r=["dtt", "c_fc"], w=["dtt"], tag="dt2")
        E("scalar", lambda e: e.activation(out=dtt, in_=dtt, func=AF.Ln, bias=1.0),
          r=["dtt", "c_fc"], w=["dtt"], tag="dt3")
        E("scalar", lambda e: e.activation(out=dtf, in_=dtraw, func=AF.Relu, bias=s_dtb),
          r=["dtraw", "c_dtb"], w=["dtf"], tag="dt4")
        E("vector", lambda e: e.tensor_tensor(out=dtf, in0=dtf, in1=dtt, op=ALU.add),
          r=["dtf", "dtt"], w=["dtf"], tag="dt5")
        E("vector", lambda e: e.tensor_scalar(out=afm, in0=dtf, scalar1=s_acol,
                                              scalar2=None, op0=ALU.mult),
          r=["dtf", "acol"], w=["dtraw"], tag="dta")
        for qq in range(2):
            ssd_chunk(st, qq, seq)
        for j in range(NPAIR):
            E("vector", lambda e, d=xst[j], s=zt[j]:
              e.tensor_tensor(out=d, in0=d, in1=s, op=ALU.mult),
              r=[f"xst{j}", f"zt{j}"], w=[f"xst{j}"], tag="gat")
        for j in range(NPAIR):
            sq_ = xsq[j % 2]
            E("scalar", lambda e, d=sq_, s=xst[j]: e.activation(out=d, in_=s,
                                                                func=AF.Square),
              r=[f"xst{j}"], w=[f"xsq{j % 2}"], tag="rmq")
            mm(pf[4][0:1, 0:ST], s_ocb, sq_, j == 0, j == NPAIR - 1,
               rb=["c_ocb", f"xsq{j % 2}"], wb=[PSB[4]], tag="rmm")
        E("scalar", lambda e, p=pf[4][0:1, 0:ST]:
          e.activation(out=r_t, in_=p, func=AF.Ln, scale=1.0 / DI, bias=EPS),
          r=[PSB[4], "c_fc"], w=["r_t"], tag="rml")
        E("scalar", lambda e: e.activation(out=r_rr, in_=r_t, func=AF.Exp, scale=-0.5),
          r=["r_t", "c_fc"], w=["r_r1"], tag="rme")
        bcast_row(r_rr, "r_r1", 5, "bc_rms")
        for j in range(NPAIR):
            E("vector", lambda e, d=zt[j], s=xst[j], w1=s_nwp[:, j:j + 1], p=pf[5][:, 0:ST]:
              e.scalar_tensor_tensor(out=d, in0=s, scalar=w1, in1=p,
                                     op0=ALU.mult, op1=ALU.mult),
              r=[f"xst{j}", "c_nwp", PSB[5]], w=[f"zt{j}"], tag="rma")
        for o in range(NF):
            bank = o % 3
            for k in range(16):
                mm(pf[bank][:, 0:ST], opW[k][:, o * 128:(o + 1) * 128], zt[k],
                   k == 0, k == 15, rb=[f"opW{k}", f"zt{k}", "warena"],
                   wb=[PSB[bank]], tag="opm")
            E("vector", lambda e, d=xa[o], p=pf[bank][:, 0:ST]:
              e.tensor_tensor(out=d, in0=d, in1=p, op=ALU.add),
              r=[f"xa{par}_{o}", PSB[bank]], w=[f"xa{par}_{o}"], tag="opev")
        E("gpsimd", lambda e, d=d_xb[:, cols].rearrange("(k p) t -> p k t", p=128),
          s=xavP[par].rearrange("p (k t) -> p k t", k=NF): e.dma_start(out=d, in_=s),
          r=[f"xa{par}_{k}" for k in range(NF)],
          w=[f"dxb{st}_{k}" for k in range(NF)], dma=True, tag="xbsp")
        stats_pass(xa, [f"xa{par}_{k}" for k in range(NF)], "s4")
        ln_rows(DM, "r4")
        bcast_row(r_r1, "r_r1", 4, "bc_r4")
        bcast_row(r_mr, "r_mr", 5, "bc_mr4")
        for k in range(NF):
            tm = xtmp[k % 2]
            E("vector", lambda e, d=tm, s=xa[k], p=pf[4][:, 0:ST]:
              e.tensor_tensor(out=d, in0=s, in1=p, op=ALU.mult),
              r=[f"xa{par}_{k}", PSB[4]], w=[f"xtmp{k % 2}"], tag="l2a")
            E("vector", lambda e, d=tm, p=pf[5][:, 0:ST]:
              e.tensor_tensor(out=d, in0=d, in1=p, op=ALU.subtract),
              r=[f"xtmp{k % 2}", PSB[5]], w=[f"xtmp{k % 2}"], tag="l2b")
            E("vector", lambda e, d=xmb[k], s=tm, w1=s_lnc[:, 32 + k:33 + k],
              bb=s_lnc[:, 40 + k:41 + k]:
              e.tensor_scalar(out=d, in0=s, scalar1=w1, scalar2=bb,
                              op0=ALU.mult, op1=ALU.add),
              r=[f"xtmp{k % 2}", "c_lnc"], w=[f"xmb{k}"], tag="l2c")
        E("gpsimd", lambda e, d=d_xc[:, cols].rearrange("(k p) t -> p k t", p=128),
          s=xmbv.rearrange("p (k t) -> p k t", k=NF): e.dma_start(out=d, in_=s),
          r=[f"xmb{k}" for k in range(NF)],
          w=[f"dxc{st}_{k}" for k in range(NF)], dma=True, tag="xcsp")

    def p1b():
        STB = 512
        E("vector", lambda e: e.engine_nop(), w=["warena"], deps=B.all_tokens(), tag="wgate1b")
        E("vector", lambda e: e.engine_nop(), w=["aarena"], deps=B.all_tokens(), tag="agate1b")
        jobs = []
        for k in range(NF):
            for c4 in range(8):
                c0, c1 = c4 * 512, (c4 + 1) * 512
                jobs.append((f1T[k * 128:(k + 1) * 128, c0:c1], f1W[k][:, c0:c1],
                             512, f"f1W{k}", p1bst, "p1bst"))
        for k in range(32):
            for c4 in range(2):
                c0, c1 = c4 * 512, (c4 + 1) * 512
                jobs.append((f2T[k * 128:(k + 1) * 128, c0:c1], f2W[k][:, c0:c1],
                             512, f"f2W{k}", p1bst, "p1bst"))
        w_stream(jobs)
        for st in range(8):
            cols = slice(st * STB, (st + 1) * STB)
            sts = (2 * st, 2 * st + 1)
            E("sync", lambda e, d=xcbv.rearrange("p (k t) -> p k t", k=NF),
              s=d_xc[:, cols].rearrange("(k p) t -> p k t", p=128):
              e.dma_start(out=d, in_=s),
              r=["aarena"] + [f"dxc{s_}_{k}" for s_ in sts for k in range(NF)],
              w=[f"xcb{k}" for k in range(NF)], dma=True, tag="xcld")
            E("sync", lambda e, d=xbfv.rearrange("p (k t) -> p k t", k=NF),
              s=d_xb[:, cols].rearrange("(k p) t -> p k t", p=128):
              e.dma_start(out=d, in_=s),
              r=["aarena"] + [f"dxb{s_}_{k}" for s_ in sts for k in range(NF)],
              w=[f"xbf{k}" for k in range(NF)], dma=True, tag="xbld")
            for o in range(32):
                bank = o % 3
                for k in range(NF):
                    mm(pf[bank][:, 0:STB], f1W[k][:, o * 128:(o + 1) * 128], xcb[k],
                       k == 0, k == NF - 1,
                       rb=[f"f1W{k}", f"xcb{k}", "warena"], wb=[PSB[bank]], tag="f1m")
                E("scalar", lambda e, d=g1b[o], p=pf[bank][:, 0:STB],
                  bb=s_b1c[:, o:o + 1]:
                  e.activation(out=d, in_=p, func=AF.Gelu, bias=bb),
                  r=[PSB[bank], "c_b1c"], w=[f"g1b{o}"], tag="gelu")
            for o in range(NF):
                bank = o % 3
                for k in range(32):
                    mm(pf[bank][:, 0:STB], f2W[k][:, o * 128:(o + 1) * 128], g1b[k],
                       k == 0, k == 31,
                       rb=[f"f2W{k}", f"g1b{k}", "warena"], wb=[PSB[bank]], tag="f2m")
                E("vector", lambda e, d=xbf[o], p=pf[bank][:, 0:STB],
                  bb=s_b2c[:, o:o + 1]:
                  e.scalar_tensor_tensor(out=d, in0=p, scalar=bb, in1=d,
                                         op0=ALU.add, op1=ALU.add),
                  r=[PSB[bank], "c_b2c", f"xbf{o}"], w=[f"xbf{o}"], tag="f2ev")
            E("gpsimd", lambda e, d=d_s[:, cols].rearrange("(k p) t -> p k t", p=128),
              s=xbfv.rearrange("p (k t) -> p k t", k=NF): e.dma_start(out=d, in_=s),
              r=[f"xbf{k}" for k in range(NF)],
              w=[f"ds{s_}_{k}" for s_ in sts for k in range(NF)], dma=True, tag="ssp")

    def p2():
        E("vector", lambda e: e.engine_nop(), w=["warena"], deps=B.all_tokens(), tag="wgate2")
        E("vector", lambda e: e.engine_nop(), w=["aarena"], deps=B.all_tokens(), tag="agate2")

        def load_gu(ee):
            jobs = []
            for k in range(NF):
                for c4 in range(2):
                    c0, c1 = c4 * 1408, (c4 + 1) * 1408
                    jobs.append((guT[ee, k * 128:(k + 1) * 128, c0:c1],
                                 guW[k][:, c0:c1], 1408, f"guW{k}", p2st, "p2st"))
            w_stream(jobs)

        def load_dn(ee, pp):
            jobs = []
            for k in range(NMI):
                jobs.append((dnT[ee, k * 128:(k + 1) * 128, :],
                             dnW[pp][k][:, 0:1024], 1024, f"dnW{pp}_{k}", p2st, "p2st"))
            w_stream(jobs)

        for qt in range(NQ):
            qcols = slice(qt * QT, (qt + 1) * QT)
            wsts = [qt * 4 + i for i in range(4)]
            E("sync", lambda e, d=p2w, s=d_w[:, qcols]: e.dma_start(out=d, in_=s),
              r=["aarena"] + [f"dw{s_}" for s_ in wsts], w=["p2w"], dma=True, tag="p2wld")
            for ee in range(NE):
                if qt == 0 and ee == 0:
                    load_gu(0)
                    load_dn(0, 0)
                for tt2 in range(2):
                    tcols = slice(qt * QT + tt2 * TT, qt * QT + (tt2 + 1) * TT)
                    xsts = [(qt * QT + tt2 * TT) // ST, (qt * QT + tt2 * TT) // ST + 1]
                    E("sync", lambda e, d=p2xnv.rearrange("p (k t) -> p k t", k=NF),
                      s=d_xn[:, tcols].rearrange("(k p) t -> p k t", p=128):
                      e.dma_start(out=d, in_=s),
                      r=["aarena"] + [f"dxn{s_}_{k}" for s_ in xsts for k in range(NF)],
                      w=[f"p2xn{k}" for k in range(NF)], dma=True, tag="p2xld")
                    for j in range(NMI):
                        for k in range(NF):
                            mm(pf[0][:, 0:TT], guW[k][:, j * 128:(j + 1) * 128], p2xn[k],
                               k == 0, k == NF - 1,
                               rb=[f"guW{k}", f"p2xn{k}", "warena"], wb=[PSB[0]],
                               tag="gum")
                        for k in range(NF):
                            mm(pf[1][:, 0:TT], guW[k][:, MI + j * 128:MI + (j + 1) * 128],
                               p2xn[k], k == 0, k == NF - 1,
                               rb=[f"guW{k}", f"p2xn{k}", "warena"], wb=[PSB[1]],
                               tag="guu")
                        E("scalar", lambda e, d=p2h[tt2][j], p=pf[0][:, 0:TT]:
                          e.activation(out=d, in_=p, func=AF.Silu),
                          r=[PSB[0]], w=[f"p2h{tt2}_{j}"], tag="hsl")
                        E("vector", lambda e, d=p2h[tt2][j], p=pf[1][:, 0:TT]:
                          e.tensor_tensor(out=d, in0=d, in1=p, op=ALU.mult),
                          r=[f"p2h{tt2}_{j}", PSB[1]], w=[f"p2h{tt2}_{j}"], tag="hml")
                ne_, nq_ = (ee + 1, qt) if ee < NE - 1 else (0, qt + 1)
                if nq_ < NQ:
                    load_gu(ne_)
                    load_dn(ne_, ne_ % 2)
                pp_ = ee % 2
                for tt2 in range(2):
                    mm(pf[4][:, 0:TT], s_oh8[:, ee * 128:(ee + 1) * 128],
                       p2w[:, tt2 * TT:(tt2 + 1) * TT], True, True,
                       rb=["c_oh8", "p2w"], wb=[PSB[4]], tag="wbm")
                    E("vector", lambda e, p=pf[4][:, 0:TT]: e.tensor_copy(out=p2wbc, in_=p),
                      r=[PSB[4]], w=["p2wbc"], tag="wbe")
                    for o in range(NF):
                        bank = 2 + o % 2
                        for k in range(NMI):
                            mm(pf[bank][:, 0:TT], dnW[pp_][k][:, o * 128:(o + 1) * 128],
                               p2h[tt2][k], k == 0, k == NMI - 1,
                               rb=[f"dnW{pp_}_{k}", f"p2h{tt2}_{k}", "warena"],
                               wb=[PSB[bank]], tag="dnm")
                        asl = slice(tt2 * TT, (tt2 + 1) * TT)
                        if ee == 0:
                            E("vector", lambda e, d=p2acc[o][:, asl], p=pf[bank][:, 0:TT]:
                              e.tensor_tensor(out=d, in0=p, in1=p2wbc, op=ALU.mult),
                              r=[PSB[bank], "p2wbc"], w=[f"p2acc{o}"], tag="cb0")
                        else:
                            E("vector", lambda e, p=pf[bank][:, 0:TT]:
                              e.tensor_tensor(out=p2tmp, in0=p, in1=p2wbc, op=ALU.mult),
                              r=[PSB[bank], "p2wbc"], w=["p2tmp"], tag="cbm")
                            E("gpsimd", lambda e, d=p2acc[o][:, asl]:
                              e.tensor_tensor(out=d, in0=d, in1=p2tmp, op=ALU.add),
                              r=[f"p2acc{o}", "p2tmp"], w=[f"p2acc{o}"], tag="cba")
            E("gpsimd", lambda e, d=d_acc[:, qcols].rearrange("(k p) t -> p k t", p=128),
              s=p2accv.rearrange("p (k t) -> p k t", k=NF): e.dma_start(out=d, in_=s),
              r=[f"p2acc{k}" for k in range(NF)],
              w=[f"dacc{qt}_{k}" for k in range(NF)], dma=True, tag="accsp")

    def p3():
        E("vector", lambda e: e.engine_nop(), w=["warena"], deps=B.all_tokens(), tag="wgate3")
        E("vector", lambda e: e.engine_nop(), w=["aarena"], deps=B.all_tokens(), tag="agate3")
        jobs = []
        for (srcT, dstL, nm) in [(sgT, sgW, "sgW"), (suT, suW, "suW")]:
            for k in range(NF):
                for c4 in range(2):
                    c0, c1 = c4 * 1408, (c4 + 1) * 1408
                    jobs.append((srcT[k * 128:(k + 1) * 128, c0:c1],
                                 dstL[k][:, c0:c1], 1408, f"{nm}{k}", p3st, "p3st"))
        for k in range(NSI):
            jobs.append((sdT[k * 128:(k + 1) * 128, :], sdW[k][:, 0:1024],
                         1024, f"sdW{k}", p3st, "p3st"))
        w_stream(jobs)
        for st in range(NST):
            cols = slice(st * ST, (st + 1) * ST)
            qt = st // 4
            for (dst, srct, rl, wpfx, tg) in [
                (p3xnv, d_xn, [f"dxn{st}_{k}" for k in range(NF)], "p3xn", "p3xnld"),
                (p3sv, d_s, [f"ds{st}_{k}" for k in range(NF)], "p3s", "p3sld"),
                (p3xv, xfm, [], "p3x", "p3xld"),
                (p3acv, d_acc, [f"dacc{qt}_{k}" for k in range(NF)], "p3ac", "p3acld"),
            ]:
                E("sync", lambda e, d=dst.rearrange("p (k t) -> p k t", k=NF),
                  s=srct[:, cols].rearrange("(k p) t -> p k t", p=128):
                  e.dma_start(out=d, in_=s), r=["aarena"] + rl,
                  w=[f"{wpfx}{k}" for k in range(NF)], dma=True, tag=tg)
            for j in range(NSI):
                b0 = (j % 2) * 2
                b1_ = b0 + 1
                for k in range(NF):
                    mm(pf[b0][:, 0:ST], sgW[k][:, j * 128:(j + 1) * 128], p3xn[k],
                       k == 0, k == NF - 1,
                       rb=[f"sgW{k}", f"p3xn{k}", "warena"], wb=[PSB[b0]], tag="shg")
                for k in range(NF):
                    mm(pf[b1_][:, 0:ST], suW[k][:, j * 128:(j + 1) * 128], p3xn[k],
                       k == 0, k == NF - 1,
                       rb=[f"suW{k}", f"p3xn{k}", "warena"], wb=[PSB[b1_]], tag="shu")
                E("scalar", lambda e, d=p3hs[j], p=pf[b0][:, 0:ST]:
                  e.activation(out=d, in_=p, func=AF.Silu),
                  r=[PSB[b0]], w=[f"p3hs{j}"], tag="shs")
                E("vector", lambda e, d=p3hs[j], p=pf[b1_][:, 0:ST]:
                  e.tensor_tensor(out=d, in0=d, in1=p, op=ALU.mult),
                  r=[f"p3hs{j}", PSB[b1_]], w=[f"p3hs{j}"], tag="shm")
            for k in range(NF):
                mm(pf[4][0:1, 0:ST], s_segb[:, k:k + 1], p3xn[k], k == 0, k == NF - 1,
                   rb=["segb", f"p3xn{k}"], wb=[PSB[4]], tag="swm")
            E("scalar", lambda e, p=pf[4][0:1, 0:ST]:
              e.activation(out=p3sw, in_=p, func=AF.Tanh, scale=0.5),
              r=[PSB[4], "c_fc"], w=["p3sw"], tag="swt")
            E("vector", lambda e: e.tensor_scalar(out=p3sw, in0=p3sw, scalar1=1.0,
                                                  scalar2=0.5, op0=ALU.add, op1=ALU.mult),
              r=["p3sw"], w=["p3sw"], tag="sws")
            mm(pf[5][:, 0:ST], s_o1f, p3sw, True, True,
               rb=["c_o1f", "p3sw"], wb=[PSB[5]], tag="swb")
            E("vector", lambda e, p=pf[5][:, 0:ST]: e.tensor_copy(out=p3swb, in_=p),
              r=[PSB[5]], w=["p3swb"], tag="swbe")
            for o in range(NF):
                bank = o % 2
                for k in range(NSI):
                    mm(pf[bank][:, 0:ST], sdW[k][:, o * 128:(o + 1) * 128], p3hs[k],
                       k == 0, k == NSI - 1,
                       rb=[f"sdW{k}", f"p3hs{k}", "warena"], wb=[PSB[bank]], tag="sdm")
                E("vector", lambda e, d=p3g[o], p=pf[bank][:, 0:ST]:
                  e.tensor_tensor(out=d, in0=p, in1=p3swb, op=ALU.mult),
                  r=[PSB[bank], "p3swb"], w=[f"p3g{o}"], tag="sde")
                E("vector", lambda e, d=p3g[o], s=p3ac[o]:
                  e.tensor_tensor(out=d, in0=d, in1=s, op=ALU.add),
                  r=[f"p3g{o}", f"p3ac{o}"], w=[f"p3g{o}"], tag="sda")
                E("scalar", lambda e, d=p3g[o]: e.activation(out=d, in_=d, func=AF.Silu),
                  r=[f"p3g{o}"], w=[f"p3g{o}"], tag="gsl")
                E("vector", lambda e, d=p3g[o], s=p3s[o]:
                  e.tensor_tensor(out=d, in0=d, in1=s, op=ALU.mult),
                  r=[f"p3g{o}", f"p3s{o}"], w=[f"p3g{o}"], tag="gms")
                E("vector", lambda e, d=p3g[o], s=p3x[o]:
                  e.tensor_tensor(out=d, in0=d, in1=s, op=ALU.add),
                  r=[f"p3g{o}", f"p3x{o}"], w=[f"p3g{o}"], tag="gax")
            E("gpsimd", lambda e, d=ofm[:, cols].rearrange("(k p) t -> p k t", p=128),
              s=p3gv.rearrange("p (k t) -> p k t", k=NF): e.dma_start(out=d, in_=s),
              r=[f"p3g{k}" for k in range(NF)], w=["ofm"], dma=True, tag="outsp")


    for st in range(PREFIX if PREFIX else NST):
        p1a_supertile(st)
    if not PREFIX:
        if PHASES in ("1b", "2", "full"):
            p1b()
        if PHASES in ("2", "full"):
            p2()
        if PHASES == "full":
            p3()

    B.finalize()

    with nc.Block() as block:
        @block.sync
        def _(e):
            B.run_engine("sync", e, sems)

        @block.tensor
        def _(e):
            B.run_engine("tensor", e, sems)

        @block.vector
        def _(e):
            B.run_engine("vector", e, sems)

        @block.scalar
        def _(e):
            B.run_engine("scalar", e, sems)

        @block.gpsimd
        def _(e):
            B.run_engine("gpsimd", e, sems)
            for s in ("dq_sync_0", "dq_sync_1", "dq_scalar_0", "dq_scalar_1",
                      "dq_gpsimd_0", "dq_gpsimd_1"):
                if B.cnt.get(s):
                    e.wait_ge(sems[s], B.cnt[s])

    stack.close()
    return nc


def _host_prep(inputs):
    x = np.asarray(inputs["L0_attn_x"])
    f = np.ascontiguousarray
    d = {}
    d["ipT"] = f(np.asarray(inputs["L0_attn_in_proj_w"]).T)
    d["opT"] = f(np.asarray(inputs["L0_attn_out_proj_w"]).T)
    d["f1T"] = f(np.asarray(inputs["L0_attn_ffn_w1"]).T)
    d["f2T"] = f(np.asarray(inputs["L0_attn_ffn_w2"]).T)
    d["guT"] = f(np.asarray(inputs["L0_ffn_experts_gate_up"]).transpose(0, 2, 1))
    d["dnT"] = f(np.asarray(inputs["L0_ffn_experts_down"]).transpose(0, 2, 1))
    d["sgT"] = f(np.asarray(inputs["L0_ffn_shared_gate_w"]).T)
    d["suT"] = f(np.asarray(inputs["L0_ffn_shared_up_w"]).T)
    d["sdT"] = f(np.asarray(inputs["L0_ffn_shared_down_w"]).T)
    d["gTc"] = f(np.asarray(inputs["L0_ffn_gate_w"]).T)
    d["segc"] = f(np.asarray(inputs["L0_ffn_shared_expert_gate_w"]).reshape(8, 128).T)
    ln = np.zeros((128, 48), np.float32)
    for i, nm in enumerate(["ln0_w", "ln0_b", "ln1_w", "ln1_b", "ln2_w", "ln2_b"]):
        ln[:, i * 8:(i + 1) * 8] = np.asarray(inputs["L0_attn_" + nm]).reshape(8, 128).T
    d["lnc"] = ln
    d["cwc"] = f(np.asarray(inputs["L0_attn_conv_w"]).reshape(17, 128, 4)
                 .transpose(1, 0, 2).reshape(128, 68))
    d["cbc"] = f(np.asarray(inputs["L0_attn_conv_b"]).reshape(17, 128).T)
    d["b1c"] = f(np.asarray(inputs["L0_attn_ffn_b1"]).reshape(32, 128).T)
    d["b2c"] = f(np.asarray(inputs["L0_attn_ffn_b2"]).reshape(8, 128).T)
    d["nwp"] = f(np.asarray(inputs["L0_attn_norm_w"]).reshape(16, 128).T)
    D = np.asarray(inputs["L0_attn_D_param"]).astype(np.float32)
    dp = np.zeros((128, 16), np.float32)
    for c in range(16):
        dp[0:64, c] = D[2 * c]
        dp[64:128, c] = D[2 * c + 1]
    d["dpr"] = dp
    d["dtb"] = f(np.asarray(inputs["L0_attn_dt_bias"]).reshape(32, 1))
    d["alg"] = f(np.asarray(inputs["L0_attn_A_log"]).reshape(32, 1))
    idx = np.arange(128)
    d["c_lt_b"] = (idx[:, None] <= idx[None, :]).astype(BFNP)
    d["c_ut_b"] = (idx[:, None] > idx[None, :]).astype(BFNP)
    d["c_id_b"] = np.eye(128, dtype=BFNP)
    d["c_id_f"] = np.eye(128, dtype=np.float32)
    sel = np.zeros((32, 2048), BFNP)
    for c in range(16):
        for p_ in range(128):
            sel[2 * c + (p_ >= 64), c * 128 + p_] = 1
    d["c_sel"] = sel
    d["c_o1f"] = np.ones((1, 128), np.float32)
    d["c_ocb"] = np.ones((128, 1), BFNP)
    oh = np.zeros((8, 1024), np.float32)
    for e_ in range(8):
        oh[e_, e_ * 128:(e_ + 1) * 128] = 1
    d["c_oh8"] = oh
    fc = np.zeros((128, 8), np.float32)
    for fi, fv in enumerate([1e-5, 1.0, -0.5, -1.0, 0.5, 1.0 / 2048.0, 1.0 / 1024.0, 2.0]):
        fc[:, fi] = fv
    d["c_fc"] = fc
    xcore = []
    for i in range(8):
        xf = np.empty((1024, 4096), np.float32)
        xf[:, :2048] = x[2 * i].T
        xf[:, 2048:] = x[2 * i + 1].T
        xcore.append(xf)
    return d, xcore


_LAST = {}


def kernel(**inputs):
    shared, xcore = _host_prep(inputs)
    nc = build_program()
    in_maps = []
    for i in range(8):
        m = dict(shared)
        m["xfm"] = xcore[i]
        in_maps.append(m)
    res = run_bass_kernel_spmd(nc, in_maps, core_ids=list(range(8)))
    _LAST["res"] = res
    out = np.empty((16, 2048, 1024), np.float32)
    for i in range(8):
        o = res.results[i]["ofm"]
        out[2 * i] = o[:, :2048].T
        out[2 * i + 1] = o[:, 2048:].T
    return out


# version 10
# speedup vs baseline: 1.1902x; 1.1488x over previous
import numpy as np
import ml_dtypes
import concourse.bass as bass
import concourse.mybir as mybir
from concourse.bass_utils import run_bass_kernel_spmd

F32 = mybir.dt.float32
BF16 = mybir.dt.bfloat16
AF = mybir.ActivationFunctionType
ALU = mybir.AluOpType
AX = mybir.AxisListType
BFNP = ml_dtypes.bfloat16

T = 4096
DM = 1024
NF = 8
DIP = 4256
DI = 2048
NCV = 17
NH = 32
NPAIR = 16
FFN = 4096
ST = 256
NST = 16
Q = 128
NE = 8
MI = 1408
NMI = 11
GU2 = 2816
NGU = 22
SI = 2816
NSI = 22
QT = 1024
NQ = 4
TT = 512
EPS = 1e-5

SB_SZ = 229376

DEBUG = False
PHASES = "full"
PREFIX = 0


class Buf:
    __slots__ = ("w", "rs", "name")

    def __init__(self, name=""):
        self.w = None
        self.rs = []
        self.name = name


class Bld:
    def __init__(self):
        self.ops = []

    def emit(self, eng, fn, reads=(), writes=(), deps=(), dma=False, tag=""):
        i = len(self.ops)
        dl = list(deps)
        for b in reads:
            if b.w is not None:
                dl.append((b.w, True))
        for b in writes:
            if b.w is not None:
                dl.append((b.w, False))
            for t in b.rs:
                dl.append((t, False))
        self.ops.append((eng, fn, dl, dma, tag))
        for b in reads:
            b.rs.append(i)
        for b in writes:
            b.w = i
            b.rs = []
        if not hasattr(self, "last"):
            self.last = {}
        self.last[eng] = i
        return i

    def all_tokens(self):
        return [(i, False) for i in getattr(self, "last", {}).values()]

    def finalize(self):
        semof = {"tensor": "pe", "vector": "ve", "scalar": "sc", "gpsimd": "gp"}
        n = len(self.ops)
        eng_of = [o[0] for o in self.ops]
        is_dma = [o[3] for o in self.ops]
        need = [None] * n
        depended = set()
        for i in range(n):
            eng = eng_of[i]
            nd = []
            for (j, raw) in self.ops[i][2]:
                if j == i:
                    continue
                same = eng_of[j] == eng and not is_dma[j] and not is_dma[i]
                if same and eng == "tensor":
                    continue
                if same and not raw:
                    continue
                nd.append(j)
                depended.add(j)
            need[i] = nd
        G = 4
        val = {}
        cnt = {}
        qidx = {}
        self.fence = {}
        qgrp_end = {}
        per_q_ops = {}
        for i in range(n):
            if is_dma[i]:
                q = eng_of[i]
                per_q_ops.setdefault(q, []).append(i)
        for q, ops in per_q_ops.items():
            g = -1
            gsz = 0
            prev = -10
            for i in ops:
                if i != prev + 1 or gsz >= G:
                    g += 1
                    gsz = 0
                gsz += 1
                prev = i
                s = f"dq_{q}_{g % 2}"
                cnt[s] = cnt.get(s, 0) + 16
                qidx[i] = (q, g, s)
            acc = {}
            for i in ops:
                q_, g, s = qidx[i]
                acc[s] = acc.get(s, 0) + 16
                qgrp_end[(q, g)] = (s, acc[s])
            seen = set()
            for i in ops:
                q_, g, s = qidx[i]
                if g not in seen:
                    seen.add(g)
                    if g >= 2:
                        self.fence[i] = qgrp_end[(q, g - 2)]
        for i in range(n):
            if is_dma[i]:
                q, g, s = qidx[i]
                val[i] = (s, qgrp_end[(q, g)][1], 16)
            elif i in depended:
                s = semof[eng_of[i]]
                cnt[s] = cnt.get(s, 0) + 1
                val[i] = (s, cnt[s], 1)
        wm = {}
        waits = [None] * n
        for i in range(n):
            m = wm.setdefault(eng_of[i], {})
            wl = []
            for j in sorted(set(need[i])):
                s, v, a = val[j]
                if m.get(s, 0) < v:
                    wl.append((s, v))
                    m[s] = v
            waits[i] = wl
        self.waits = waits
        self.val = val
        self.cnt = cnt

    def run_engine(self, eng_name, e, sems):
        for i, (eng, fn, deps, dma, tag) in enumerate(self.ops):
            if eng != eng_name:
                continue
            for (s, v) in self.waits[i]:
                e.wait_ge(sems[s], v)
            if dma and i in self.fence:
                fs, fv = self.fence[i]
                e.wait_ge(sems[fs], fv)
            inst = fn(e)
            if i in self.val:
                s, v, a = self.val[i]
                inst.then_inc(sems[s], 16 if dma else a)


def build_program():
    nc = bass.Bass("TRN2", target_bir_lowering=False, debug=False)
    dt = nc.dram_tensor

    xfm = dt("xfm", [DM, T], F32, kind="ExternalInput").ap()
    ipT = dt("ipT", [DM, DIP], F32, kind="ExternalInput").ap()
    opT = dt("opT", [DI, DM], F32, kind="ExternalInput").ap()
    f1T = dt("f1T", [DM, FFN], F32, kind="ExternalInput").ap()
    f2T = dt("f2T", [FFN, DM], F32, kind="ExternalInput").ap()
    guT = dt("guT", [NE, DM, GU2], F32, kind="ExternalInput").ap()
    dnT = dt("dnT", [NE, MI, DM], F32, kind="ExternalInput").ap()
    sgT = dt("sgT", [DM, SI], F32, kind="ExternalInput").ap()
    suT = dt("suT", [DM, SI], F32, kind="ExternalInput").ap()
    sdT = dt("sdT", [SI, DM], F32, kind="ExternalInput").ap()
    gTc = dt("gTc", [DM, NE], F32, kind="ExternalInput").ap()
    segc = dt("segc", [128, 8], F32, kind="ExternalInput").ap()
    lnc = dt("lnc", [128, 48], F32, kind="ExternalInput").ap()
    cwc = dt("cwc", [128, 68], F32, kind="ExternalInput").ap()
    cbc = dt("cbc", [128, 17], F32, kind="ExternalInput").ap()
    b1c = dt("b1c", [128, 32], F32, kind="ExternalInput").ap()
    b2c = dt("b2c", [128, 8], F32, kind="ExternalInput").ap()
    nwp = dt("nwp", [128, 16], F32, kind="ExternalInput").ap()
    dpr = dt("dpr", [128, 16], F32, kind="ExternalInput").ap()
    dtb = dt("dtb", [32, 1], F32, kind="ExternalInput").ap()
    alg = dt("alg", [32, 1], F32, kind="ExternalInput").ap()
    c_lt_b = dt("c_lt_b", [128, 128], BF16, kind="ExternalInput").ap()
    c_ut_b = dt("c_ut_b", [128, 128], BF16, kind="ExternalInput").ap()
    c_id_b = dt("c_id_b", [128, 128], BF16, kind="ExternalInput").ap()
    c_id_f = dt("c_id_f", [128, 128], F32, kind="ExternalInput").ap()
    c_sel = dt("c_sel", [32, 2048], BF16, kind="ExternalInput").ap()
    c_o1f = dt("c_o1f", [1, 128], F32, kind="ExternalInput").ap()
    c_ocb = dt("c_ocb", [128, 1], BF16, kind="ExternalInput").ap()
    c_oh8 = dt("c_oh8", [8, 1024], F32, kind="ExternalInput").ap()
    c_fc = dt("c_fc", [128, 8], F32, kind="ExternalInput").ap()

    dbgk = "ExternalOutput" if DEBUG else "Internal"
    d_xn = dt("d_xn", [DM, T], BF16, kind=dbgk).ap()
    d_xb = dt("d_xb", [DM, T], F32, kind=dbgk).ap()
    d_xc = dt("d_xc", [DM, T], BF16, kind=dbgk).ap()
    d_w = dt("d_w", [8, T], F32, kind=dbgk).ap()
    d_s = dt("d_s", [DM, T], F32, kind=dbgk).ap()
    d_acc = dt("d_acc", [DM, T], F32, kind=dbgk).ap()
    d_f1 = dt("d_f1", [DM, FFN], BF16, kind="Internal").ap()
    d_f2 = dt("d_f2", [FFN, DM], BF16, kind="Internal").ap()
    d_gu = dt("d_gu", [NE, DM, GU2], BF16, kind="Internal").ap()
    d_dn = dt("d_dn", [NE, MI, DM], BF16, kind="Internal").ap()
    d_sg = dt("d_sg", [DM, SI], BF16, kind="Internal").ap()
    d_su = dt("d_su", [DM, SI], BF16, kind="Internal").ap()
    d_sd = dt("d_sd", [SI, DM], BF16, kind="Internal").ap()
    ofm = dt("ofm", [DM, T], F32, kind="ExternalOutput").ap()
    d_dbg = dt("d_dbg", [8, ST], F32, kind=dbgk).ap()
    d_dbg2 = dt("d_dbg2", [128, ST], F32, kind=dbgk).ap()
    d_dbg3 = dt("d_dbg3", [128, ST], F32, kind=dbgk).ap()

    nalloc = [0]
    off_ = [0]

    def at(shape, dtype, off):
        nalloc[0] += 1
        per = int(np.prod(shape[1:])) * mybir.dt.size(dtype)
        assert off + per <= SB_SZ, f"sbuf overflow {off}+{per}"
        h = nc.alloc_sbuf_tensor_at(f"sb{nalloc[0]}", list(shape), dtype,
                                    offset=off, align_bytes=32)
        return h.ap()

    def take(shape, dtype, n=1):
        per = int(np.prod(shape[1:])) * mybir.dt.size(dtype)
        per = -(-per // 32) * 32
        r = []
        for _ in range(n):
            r.append(at(shape, dtype, off_[0]))
            off_[0] += per
        return r if n > 1 else r[0]

    cursor = SB_SZ - nc.sbuf_bytes_remaining
    off_[0] = 17408
    assert cursor <= off_[0], ("bass internal allocs reach", cursor)
    s_lt_b = take([128, 128], BF16)
    s_ut_b = take([128, 128], BF16)
    s_id_b = take([128, 128], BF16)
    s_id_f = take([128, 128], F32)
    s_sel = take([32, 2048], BF16)
    s_o1f = take([1, 128], F32)
    s_ocb = take([128, 1], BF16)
    s_oh8 = take([8, 1024], F32)
    s_lnc = take([128, 48], F32)
    s_cwc = take([128, 68], F32)
    s_cbc = take([128, 17], F32)
    s_b1c = take([128, 32], F32)
    s_b2c = take([128, 8], F32)
    s_nwp = take([128, 16], F32)
    s_dpr = take([128, 16], F32)
    s_dtb = take([32, 1], F32)
    s_alg = take([32, 1], F32)
    s_acol = take([32, 1], F32)
    s_gtc = take([128, 8], F32, 8)
    s_seg = take([128, 8], F32)
    s_segb = take([128, 8], BF16)
    s_fc = take([128, 8], F32)
    assert off_[0] <= 30208, off_[0]

    WB = 30208
    W_SZ = 135168
    ipW = [at([128, DIP], BF16, WB + k * 8512) for k in range(NF)]
    opW = [at([128, DM], BF16, WB + 68096 + k * 2048) for k in range(16)]
    f1W = [at([128, FFN], BF16, WB + k * 8192) for k in range(NF)]
    f2W = [at([128, DM], BF16, WB + 65536 + k * 2048) for k in range(32)]
    guW = [at([128, GU2], BF16, WB + k * 5632) for k in range(NF)]
    dnW = [[at([128, DM], BF16, WB + 45056 + (p * NMI + k) * 2048)
            for k in range(NMI)] for p in range(2)]
    sgW = [at([128, SI], BF16, WB + k * 5632) for k in range(NF)]
    suW = [at([128, SI], BF16, WB + 45056 + k * 5632) for k in range(NF)]
    sdW = [at([128, DM], BF16, WB + 90112 + k * 2048) for k in range(NSI)]
    f1Wm = at([128, FFN * NF], BF16, WB)
    f2Wm = at([128, DM * 32], BF16, WB + 65536)
    guWm = at([128, GU2 * NF], BF16, WB)
    dnWm = [at([128, DM * NMI], BF16, WB + 45056 + p * NMI * 2048) for p in range(2)]
    sgWm = at([128, SI * NF], BF16, WB)
    suWm = at([128, SI * NF], BF16, WB + 45056)
    sdWm = at([128, DM * NSI], BF16, WB + 90112)
    p1bst = [at([128, 512], F32, WB + 131072 + i * 2048) for i in range(2)]

    AB = WB + W_SZ
    off_[0] = AB
    xavP = take([128, ST * NF], F32, 2)
    xaP = [[xavP[par][:, k * ST:(k + 1) * ST] for k in range(NF)] for par in range(2)]
    xmbv = take([128, ST * NF], BF16)
    xmb = [xmbv[:, k * ST:(k + 1) * ST] for k in range(NF)]
    xsc = take([128, ST], BF16, 2)
    xsq = take([128, ST], BF16, 2)
    xtmp = take([128, ST], F32, 2)
    zt = take([128, ST], BF16, NPAIR)
    xBCr = take([128, ST + 4], BF16, NCV)
    xst = take([128, ST], BF16, NPAIR)
    bct = take([128, ST], BF16)
    sCf = take([64, ST], BF16)
    HT = take([64, DI], BF16, 2)
    xsT = take([128, 128], BF16, NPAIR)
    assert off_[0] <= SB_SZ, off_[0]
    save_off = off_[0]
    off_[0] = WB + 100864
    sBT = take([128, 64], BF16)
    sBh = take([128, 64], BF16, 2)
    sGm = take([128, 128], BF16)
    sL2 = take([128, 128], BF16, 4)
    sM2 = take([128, 128], BF16, 4)
    sRL = take([128, 128], BF16, 4)
    sEb = take([128, 128], BF16, 2)
    sT1 = take([128, 128], BF16, 2)
    sE2 = take([128, 128], F32, 2)
    sdtT = take([128, 32], F32)
    slnd = take([128, 32], F32)
    saT = take([128, 32], F32)
    saTb = take([128, 32], BF16)
    swT = take([128, 32], F32)
    swd = take([128, 32], F32)
    sEf = take([32, 128], BF16)
    sLb = take([64, 32], F32)
    sLr = take([1, 32], F32)
    sLr2 = take([1, 32], F32)
    r_m, r_ms, r_t, r_v, r_r1, r_mr = take([1, ST], F32, 6)
    r_r2 = r_r1
    r_rr = r_r1
    dtraw = take([32, ST], F32)
    dtf = take([32, ST], F32)
    dtt = take([32, ST], F32)
    afm = dtraw
    g_lf = take([8, ST], F32)
    g_lt = take([128, 8], F32, 2)
    g_d = take([128, 8], F32)
    g_m1 = take([128, 1], F32)
    g_sp = take([128, 8], F32)
    g_ex = take([128, 8], F32)
    g_ss = take([128, 1], F32)
    g_w = take([128, 8], F32, 2)
    g_wf = g_lf
    Ast = take([128, 1024], F32, 2)
    AstB = take([128, 1024], BF16, 2)
    xns = take([128, ST * NF], BF16)
    assert off_[0] <= WB + W_SZ, off_[0]
    off_[0] = save_off

    STB = 512
    off_[0] = AB
    xcbv = take([128, STB * NF], BF16)
    xcb = [xcbv[:, k * STB:(k + 1) * STB] for k in range(NF)]
    g1b = take([128, STB], BF16, 32)
    xbfv = take([128, STB * NF], F32)
    xbf = [xbfv[:, k * STB:(k + 1) * STB] for k in range(NF)]
    assert off_[0] <= SB_SZ

    off_[0] = WB + 90112
    p2accv = take([128, QT * NF], F32)
    p2acc = [p2accv[:, k * QT:(k + 1) * QT] for k in range(NF)]
    assert off_[0] <= WB + W_SZ, off_[0]
    off_[0] = AB
    p2xnv = take([128, TT * NF], BF16)
    p2xn = [p2xnv[:, k * TT:(k + 1) * TT] for k in range(NF)]
    p2h = [take([128, TT], BF16, NMI) for _ in range(2)]
    p2wbc = take([128, TT], F32)
    p2tmp = take([128, TT], F32)
    p2w = take([8, QT], F32)
    p2st = take([128, 1408], F32, 4)
    assert off_[0] <= SB_SZ

    off_[0] = AB
    p3hs = take([128, ST], BF16, NSI)
    p3xnv = take([128, ST * NF], BF16)
    p3xn = [p3xnv[:, k * ST:(k + 1) * ST] for k in range(NF)]
    p3sv = take([128, ST * NF], F32)
    p3s = [p3sv[:, k * ST:(k + 1) * ST] for k in range(NF)]
    p3xv = take([128, ST * NF], F32)
    p3x = [p3xv[:, k * ST:(k + 1) * ST] for k in range(NF)]
    p3acv = take([128, ST * NF], F32)
    p3ac = [p3acv[:, k * ST:(k + 1) * ST] for k in range(NF)]
    p3gv = take([128, ST * NF], F32)
    p3g = [p3gv[:, k * ST:(k + 1) * ST] for k in range(NF)]
    p3sw = take([1, ST], F32)
    p3swb = take([128, ST], F32)
    p3st = take([128, 1408], F32, 2)
    assert off_[0] <= SB_SZ

    import contextlib
    stack = contextlib.ExitStack()
    pf = [stack.enter_context(nc.psum_tensor(f"pf{i}", [128, 512], F32)).ap()
          for i in range(6)]
    pb = [stack.enter_context(nc.psum_tensor(f"pb{i}", [128, 1024], BF16)).ap()
          for i in range(2)]
    sem_names = ["pe", "ve", "sc", "gp",
                 "dq_sync_0", "dq_sync_1", "dq_scalar_0", "dq_scalar_1",
                 "dq_gpsimd_0", "dq_gpsimd_1"]
    sems = {s: stack.enter_context(nc.semaphore(s)) for s in sem_names}

    B = Bld()
    bufs = {}

    def bf(x):
        if isinstance(x, Buf):
            return x
        if x not in bufs:
            bufs[x] = Buf(x)
        return bufs[x]

    PSB = [bf(f"ps{i}") for i in range(8)]

    FCVALS = [1e-5, 1.0, -0.5, -1.0, 0.5, 1.0 / 2048.0, 1.0 / 1024.0, 2.0]
    for fi, fv in enumerate(FCVALS):
        nc.const_aps.register(F32, fv, s_fc[:, fi:fi + 1])

    def E(eng, fn, r=(), w=(), deps=(), dma=False, tag=""):
        return B.emit(eng, fn, [bf(x) for x in r], [bf(x) for x in w], deps, dma, tag)

    def mm(out, lhsT, rhs, start, stop, rb=(), wb=(), tp=None, tag="mm"):
        def f(e, o=out, l=lhsT, rr=rhs, s=start, p=stop, t=tp):
            if t is None:
                return e.matmul(out=o, lhsT=l, rhs=rr, start=s, stop=p)
            return e.matmul(out=o, lhsT=l, rhs=rr, start=s, stop=p, tile_position=t)
        return E("tensor", f, r=rb, w=wb, tag=tag)

    def tpose(out, in_, ident, rb, wb, tag="tp"):
        def f(e, o=out, i_=in_, d=ident):
            return e.transpose(out=o, in_=i_, identity=d)
        return E("tensor", f, r=rb, w=wb, tag=tag)

    cl = [(s_lt_b, c_lt_b, "c_lt"), (s_ut_b, c_ut_b, "c_ut"), (s_id_b, c_id_b, "c_idb"),
          (s_id_f, c_id_f, "c_idf"), (s_sel, c_sel, "c_sel"), (s_o1f, c_o1f, "c_o1f"),
          (s_ocb, c_ocb, "c_ocb"), (s_oh8, c_oh8, "c_oh8"), (s_lnc, lnc, "c_lnc"),
          (s_cwc, cwc, "c_cwc"), (s_cbc, cbc, "c_cbc"), (s_b1c, b1c, "c_b1c"),
          (s_b2c, b2c, "c_b2c"), (s_nwp, nwp, "c_nwp"), (s_dpr, dpr, "c_dpr"),
          (s_dtb, dtb, "c_dtb"), (s_alg, alg, "c_alg"), (s_seg, segc, "c_seg"),
          (s_fc, c_fc, "c_fc")]
    for (dst, src, nm) in cl:
        if PREFIX and nm not in ("c_ocb", "c_fc", "c_alg", "c_dtb", "c_lt"):
            continue
        E("sync", lambda e, d=dst, s=src: e.dma_start(out=d, in_=s),
          w=[nm], dma=True, tag="constld")
    for k in range(8):
        if PREFIX:
            break
        E("sync", lambda e, d=s_gtc[k], s=gTc[k * 128:(k + 1) * 128, :]:
          e.dma_start(out=d, in_=s), w=[f"gtc{k}"], dma=True, tag="gtcld")
    E("scalar", lambda e: e.activation(out=s_acol, in_=s_alg, func=AF.Exp),
      r=["c_alg"], w=["acol"], tag="acol")
    E("vector", lambda e: e.tensor_scalar(out=s_acol, in0=s_acol, scalar1=-1.0,
                                          scalar2=None, op0=ALU.mult),
      r=["acol"], w=["acol"], tag="acol2")
    E("vector", lambda e: e.tensor_copy(out=s_segb, in_=s_seg),
      r=["c_seg"], w=["segb"], tag="segb")
    E("vector", lambda e: e.memset(HT[0][:], 0.0), w=["HT0"], tag="ht0")
    E("vector", lambda e: e.memset(HT[1][:], 0.0), w=["HT1"], tag="ht1")

    si = [0]

    def stage_conv(src, dstw, dstname, cols, eng="vector"):
        j = si[0] % 2
        sb = f"Ast{j}"
        E("sync", lambda e, d=Ast[j][:, 0:cols], s=src: e.dma_start(out=d, in_=s),
          w=[sb], dma=True, tag="wld")
        E(eng, lambda e, d=dstw, s=Ast[j][:, 0:cols]: e.tensor_copy(out=d, in_=s),
          r=[sb, "warena"], w=[dstname], tag="wcv")
        si[0] += 1

    if not PREFIX:
        for k in range(NF):
            for c4 in range(5):
                c0, c1 = c4 * 1024, min(DIP, (c4 + 1) * 1024)
                stage_conv(ipT[k * 128:(k + 1) * 128, c0:c1], ipW[k][:, c0:c1],
                           f"ipW{k}", c1 - c0)
        for k in range(16):
            stage_conv(opT[k * 128:(k + 1) * 128, :], opW[k][:, 0:1024], f"opW{k}", 1024)

    def w_stream(srcs, cve="vector"):
        if not srcs:
            return
        depth = len(srcs[0][4])
        for b0 in range(0, len(srcs), depth):
            batch = srcs[b0:b0 + depth]
            for j, (srcap, dstap, cols, dstname, stg, spfx) in enumerate(batch):
                E("sync", lambda e, d=stg[j][:, 0:cols], s=srcap:
                  e.dma_start(out=d, in_=s), w=[f"{spfx}{j}"], dma=True, tag="wsl")
            for j, (srcap, dstap, cols, dstname, stg, spfx) in enumerate(batch):
                E(cve, lambda e, d=dstap, s=stg[j][:, 0:cols]: e.tensor_copy(out=d, in_=s),
                  r=[f"{spfx}{j}", "warena"], w=[dstname], tag="wsc")

    def ln_rows(denom, tagp):
        E("vector", lambda e, p=pf[4][0:1, 0:ST]:
          e.tensor_scalar(out=r_m, in0=p, scalar1=1.0 / denom, scalar2=None, op0=ALU.mult),
          r=[PSB[4]], w=["r_m"], tag=tagp + "m")
        E("vector", lambda e, p=pf[5][0:1, 0:ST]:
          e.tensor_scalar(out=r_ms, in0=p, scalar1=1.0 / denom, scalar2=None, op0=ALU.mult),
          r=[PSB[5]], w=["r_ms"], tag=tagp + "ms")
        E("vector", lambda e: e.tensor_tensor(out=r_t, in0=r_m, in1=r_m, op=ALU.mult),
          r=["r_m"], w=["r_t"], tag=tagp + "t")
        E("vector", lambda e: e.tensor_tensor(out=r_v, in0=r_ms, in1=r_t, op=ALU.subtract),
          r=["r_ms", "r_t"], w=["r_v"], tag=tagp + "v")
        E("scalar", lambda e: e.activation(out=r_t, in_=r_v, func=AF.Ln, bias=EPS),
          r=["r_v", "c_fc"], w=["r_t"], tag=tagp + "ln")
        E("scalar", lambda e: e.activation(out=r_r1, in_=r_t, func=AF.Exp, scale=-0.5),
          r=["r_t", "c_fc"], w=["r_r1"], tag=tagp + "ex")
        E("vector", lambda e: e.tensor_tensor(out=r_mr, in0=r_m, in1=r_r1, op=ALU.mult),
          r=["r_m", "r_r1"], w=["r_mr"], tag=tagp + "mr")

    def bcast_row(row, rowname, bank, tagp):
        return mm(pf[bank][:, 0:ST], s_o1f, row, True, True,
                  rb=["c_o1f", rowname], wb=[PSB[bank]], tag=tagp)

    def stats_pass(srcs, srcnames, tagp):
        nsrc = len(srcs)
        for k in range(nsrc):
            sc_, sq_ = xsc[k % 2], xsq[k % 2]
            E("vector", lambda e, d=sc_, s=srcs[k]: e.tensor_copy(out=d, in_=s),
              r=[srcnames[k]], w=[f"xsc{k % 2}"], tag=tagp + "c")
            mm(pf[4][0:1, 0:ST], s_ocb, sc_, k == 0, k == nsrc - 1,
               rb=["c_ocb", f"xsc{k % 2}"], wb=[PSB[4]], tag=tagp + "m")
            E("scalar", lambda e, d=sq_, s=srcs[k]: e.activation(out=d, in_=s, func=AF.Square),
              r=[srcnames[k]], w=[f"xsq{k % 2}"], tag=tagp + "q")
            mm(pf[5][0:1, 0:ST], s_ocb, sq_, k == 0, k == nsrc - 1,
               rb=["c_ocb", f"xsq{k % 2}"], wb=[PSB[5]], tag=tagp + "qm")

    def ssd_chunk(st, qq, seq):
        qs = slice(qq * Q, (qq + 1) * Q)
        for j in range(NPAIR):
            bk = 6 + (j % 2)
            sl = slice(((j // 2) % 4) * 128, ((j // 2) % 4) * 128 + 128)
            tpose(pb[j % 2][:, sl], xst[j][:, qs], s_id_b,
                  rb=[f"xst{j}", "c_idb"], wb=[PSB[bk]], tag="xsT")
            if j % 4 < 2:
                E("vector", lambda e, d=xsT[j], s=pb[j % 2][:, sl]:
                  e.tensor_copy(out=d, in_=s), r=[PSB[bk]], w=[f"xsT{j}"], tag="xsTe")
            else:
                E("scalar", lambda e, d=xsT[j], s=pb[j % 2][:, sl]:
                  e.activation(out=d, in_=s, func=AF.Copy),
                  r=[PSB[bk]], w=[f"xsT{j}"], tag="xsTe")
        tpose(pb[0][:, 512:576], bct[0:64, qs], s_id_b[0:64, 0:64],
              rb=["bct", "c_idb"], wb=[PSB[6]], tag="BT")
        E("vector", lambda e, s=pb[0][:, 512:576]: e.tensor_copy(out=sBT, in_=s),
          r=[PSB[6]], w=["sBT"], tag="BTe")
        mm(pf[5][0:128, 0:128], bct[0:64, qs], sCf[:, qs], True, True,
           rb=["bct", "sCf"], wb=[PSB[5]], tag="G")
        E("vector", lambda e, p=pf[5][0:128, 0:128]:
          e.tensor_tensor(out=sGm, in0=p, in1=s_lt_b, op=ALU.mult),
          r=[PSB[5], "c_lt"], w=["sGm"], tag="Gm")
        tpose(pf[5][:, 128:160], dtf[:, qs], s_id_f[0:32, 0:32],
              rb=["dtf", "c_idf"], wb=[PSB[5]], tag="dtT")
        E("vector", lambda e, s=pf[5][:, 128:160]: e.tensor_copy(out=sdtT, in_=s),
          r=[PSB[5]], w=["sdtT"], tag="dtTe")
        E("scalar", lambda e, s=pf[5][:, 128:160]: e.activation(out=slnd, in_=s, func=AF.Ln),
          r=[PSB[5]], w=["slnd"], tag="lnd")
        tpose(pf[5][:, 160:192], afm[:, qs], s_id_f[0:32, 0:32],
              rb=["dtraw", "c_idf"], wb=[PSB[5]], tag="aT")
        E("vector", lambda e, s=pf[5][:, 160:192]: e.tensor_copy(out=saT, in_=s),
          r=[PSB[5]], w=["saT"], tag="aTe")
        E("vector", lambda e: e.tensor_copy(out=saTb, in_=saT),
          r=["saT"], w=["saTb"], tag="aTb")
        mm(pf[5][0:32, 192:320], saTb, s_lt_b, True, True,
           rb=["saTb", "c_lt"], wb=[PSB[5]], tag="sfm")
        E("scalar", lambda e, p=pf[5][0:32, 192:320]:
          e.activation(out=sEf, in_=p, func=AF.Exp), r=[PSB[5]], w=["sEf"], tag="Ef")
        mm(pf[5][:, 320:352], s_ut_b, saTb, True, True,
           rb=["c_ut", "saTb"], wb=[PSB[5]], tag="wtm")
        E("scalar", lambda e, p=pf[5][:, 320:352]:
          e.activation(out=swT, in_=p, func=AF.Exp), r=[PSB[5]], w=["swT"], tag="wte")
        E("vector", lambda e: e.tensor_tensor(out=swd, in0=swT, in1=sdtT, op=ALU.mult),
          r=["swT", "sdtT"], w=["swd"], tag="wtd")
        E("scalar", lambda e: e.activation(out=sLr, in_=saT[0:1, :], func=AF.Exp),
          r=["saT"], w=["sLr"], tag="La")
        E("vector", lambda e: e.tensor_tensor(out=sLr2, in0=sLr, in1=swT[0:1, :], op=ALU.mult),
          r=["sLr", "swT"], w=["sLr2"], tag="Lb")
        mm(pf[5][0:64, 352:384], s_o1f[:, 0:64], sLr2, True, True,
           rb=["c_o1f", "sLr2"], wb=[PSB[5]], tag="Lm")
        E("vector", lambda e, p=pf[5][0:64, 352:384]: e.tensor_copy(out=sLb, in_=p),
          r=[PSB[5]], w=["sLb"], tag="Le")
        for w in range(8):
            hs4 = [4 * w + i for i in range(4)]
            prs = [2 * w, 2 * w + 1]
            for h in hs4:
                E("vector", lambda e, d=sRL[h % 4], sc1=saT[:, h:h + 1]:
                  e.tensor_scalar(out=d, in0=s_lt_b, scalar1=sc1, scalar2=None,
                                  op0=ALU.mult),
                  r=["c_lt", "saT"], w=[f"sRL{h % 4}"], tag="rl")
            for h in hs4:
                sl = slice((h % 4 // 2) * 128, (h % 4 // 2) * 128 + 128)
                mm(pf[h % 2][:, sl], s_ut_b, sRL[h % 4], True, True,
                   rb=["c_ut", f"sRL{h % 4}"], wb=[PSB[h % 2]], tag="dif")
            for h in hs4:
                sl = slice((h % 4 // 2) * 128, (h % 4 // 2) * 128 + 128)
                E("scalar", lambda e, d=sL2[h % 4], pp=pf[h % 2][:, sl],
                  b=slnd[:, h:h + 1]:
                  e.activation(out=d, in_=pp, func=AF.Exp, bias=b),
                  r=[PSB[h % 2], "slnd"], w=[f"sL2{h % 4}"], tag="l2")
            for h in hs4:
                E("gpsimd", lambda e, d=sM2[h % 4], s=sL2[h % 4]:
                  e.tensor_tensor(out=d, in0=s, in1=sGm, op=ALU.mult),
                  r=[f"sL2{h % 4}", "sGm"], w=[f"sM2{h % 4}"], tag="m2")
            for p in prs:
                bI = 4 + p % 2
                mm(pf[bI][:, 128:256], s_sel[:, p * 128:(p + 1) * 128], sEf, True, True,
                   rb=["c_sel", "sEf"], wb=[PSB[bI]], tag="Ebm")
                mm(pf[bI][:, 0:128], HT[seq][:, p * 128:(p + 1) * 128], sCf[:, qs],
                   True, True, rb=[f"HT{seq}", "sCf"], wb=[PSB[bI]], tag="Yi")
                E("scalar", lambda e, d=sEb[p % 2], pp=pf[bI][:, 128:256]:
                  e.activation(out=d, in_=pp, func=AF.Copy),
                  r=[PSB[bI]], w=[f"sEb{p % 2}"], tag="Ebe")
                E("vector", lambda e, d=sT1[p % 2], pp=pf[bI][:, 0:128], eb=sEb[p % 2]:
                  e.tensor_tensor(out=d, in0=pp, in1=eb, op=ALU.mult),
                  r=[PSB[bI], f"sEb{p % 2}"], w=[f"sT1{p % 2}"], tag="t1")
            for h in hs4:
                p = h // 2
                hi = h % 2
                bP = 2 + p % 2
                mm(pf[bP][hi * 64:(hi + 1) * 64, 0:128],
                   xsT[p][:, hi * 64:(hi + 1) * 64], sM2[h % 4], True, True,
                   rb=[f"xsT{p}", f"sM2{h % 4}"], wb=[PSB[bP]],
                   tp=(0, hi * 64), tag="Ym")
            for h in hs4:
                p = h // 2
                hi = h % 2
                bP = 2 + p % 2
                E("vector", lambda e, d=sBh[h % 2], sc1=swd[:, h:h + 1]:
                  e.tensor_scalar(out=d, in0=sBT, scalar1=sc1, scalar2=None,
                                  op0=ALU.mult),
                  r=["sBT", "swd"], w=[f"sBh{h % 2}"], tag="bh")
                mm(pf[bP][0:64, 128 + hi * 64:128 + (hi + 1) * 64], sBh[h % 2],
                   xsT[p][:, hi * 64:(hi + 1) * 64], True, True,
                   rb=[f"sBh{h % 2}", f"xsT{p}"], wb=[PSB[bP]], tag="dH")
                E("vector", lambda e, d=HT[seq][:, h * 64:(h + 1) * 64],
                  sc1=sLb[:, h:h + 1],
                  pp=pf[bP][0:64, 128 + hi * 64:128 + (hi + 1) * 64]:
                  e.scalar_tensor_tensor(out=d, in0=d, scalar=sc1, in1=pp,
                                         op0=ALU.mult, op1=ALU.add),
                  r=[f"HT{seq}", "sLb", PSB[bP]], w=[f"HT{seq}"], tag="Hup")
            for p in prs:
                bP = 2 + p % 2
                E("vector", lambda e, d=sE2[p % 2], t1=sT1[p % 2], pp=pf[bP][:, 0:128]:
                  e.tensor_tensor(out=d, in0=t1, in1=pp, op=ALU.add),
                  r=[f"sT1{p % 2}", PSB[bP]], w=[f"sE2{p % 2}"], tag="e2")
                E("vector", lambda e, d=xst[p][:, qs], sc1=s_dpr[:, p:p + 1],
                  e2=sE2[p % 2]:
                  e.scalar_tensor_tensor(out=d, in0=d, scalar=sc1, in1=e2,
                                         op0=ALU.mult, op1=ALU.add),
                  r=[f"xst{p}", "c_dpr", f"sE2{p % 2}"], w=[f"xst{p}"], tag="e3")

    def emit_xld(st):
        par = st % 2
        cols = slice(st * ST, (st + 1) * ST)
        E("sync", lambda e, d=xavP[par].rearrange("p (k t) -> p k t", k=NF),
          s=xfm[:, cols].rearrange("(k p) t -> p k t", p=128):
          e.dma_start(out=d, in_=s), r=["aarena"],
          w=[f"xa{par}_{k}" for k in range(NF)], dma=True, tag="xld")

    def p1a_supertile(st):
        par = st % 2
        xa = xaP[par]
        cols = slice(st * ST, (st + 1) * ST)
        seq = st // 8
        if st == 0:
            emit_xld(0)
        if st + 1 < NST:
            emit_xld(st + 1)
        stats_pass(xa, [f"xa{par}_{k}" for k in range(NF)], "s1")
        ln_rows(DM, "r1")
        if DEBUG and st == 0:
            for di, (rr, rn) in enumerate([(r_m, "r_m"), (r_ms, "r_ms"), (r_v, "r_v"),
                                           (r_r1, "r_r1"), (r_mr, "r_mr"), (r_t, "r_t")]):
                E("gpsimd", lambda e, d=d_dbg[di:di + 1, :], s=rr:
                  e.dma_start(out=d, in_=s), r=[rn], w=["d_dbg"], dma=True, tag="dbg")
            E("vector", lambda e: e.tensor_copy(out=xtmp[0], in_=xsq[1]),
              r=["xsq1"], w=["xtmp0"], tag="dbgc")
            E("gpsimd", lambda e: e.dma_start(out=d_dbg2, in_=xtmp[0]),
              r=["xtmp0"], w=["d_dbg2"], dma=True, tag="dbg2")
            E("vector", lambda e: e.tensor_copy(out=xtmp[1], in_=xsq[0]),
              r=["xsq0"], w=["xtmp1"], tag="dbgc2")
            E("gpsimd", lambda e: e.dma_start(out=d_dbg3, in_=xtmp[1]),
              r=["xtmp1"], w=["d_dbg3"], dma=True, tag="dbg3")
        if PREFIX:
            E("gpsimd", lambda e: e.dma_start(out=d_dbg2, in_=xa[7]),
              r=["xa7"], w=["d_dbg2"], dma=True, tag="dbgxa")
            E("vector", lambda e: e.tensor_copy(out=xtmp[0], in_=xsq[1]),
              r=["xsq1"], w=["xtmp0"], tag="dbq")
            E("gpsimd", lambda e: e.dma_start(out=d_dbg3, in_=xtmp[0]),
              r=["xtmp0"], w=["d_dbg3"], dma=True, tag="dbq2")
            return
        bcast_row(r_r1, "r_r1", 4, "bc_r1")
        bcast_row(r_mr, "r_mr", 5, "bc_mr1")
        for k in range(NF):
            E("vector", lambda e, d=xa[k], p=pf[4][:, 0:ST]:
              e.tensor_tensor(out=d, in0=d, in1=p, op=ALU.mult),
              r=[f"xa{par}_{k}", PSB[4]], w=[f"xa{par}_{k}"], tag="xn1")
            E("vector", lambda e, d=xa[k], p=pf[5][:, 0:ST]:
              e.tensor_tensor(out=d, in0=d, in1=p, op=ALU.subtract),
              r=[f"xa{par}_{k}", PSB[5]], w=[f"xa{par}_{k}"], tag="xn2")
        for k in range(NF):
            mm(pf[3][0:8, 0:ST], s_gtc[k], xa[k], k == 0, k == NF - 1,
               rb=[f"gtc{k}", f"xa{par}_{k}"], wb=[PSB[3]], tag="gl")
        E("vector", lambda e, p=pf[3][0:8, 0:ST]: e.tensor_copy(out=g_lf, in_=p),
          r=[PSB[3]], w=["g_lf"], tag="glev")
        for k in range(NF):
            E("vector", lambda e, d=xns[:, k * ST:(k + 1) * ST], s=xa[k]:
              e.tensor_copy(out=d, in_=s), r=[f"xa{par}_{k}"], w=[f"xns{k}"], tag="xnb")
        E("gpsimd", lambda e, d=d_xn[:, cols].rearrange("(k p) t -> p k t", p=128),
          s=xns.rearrange("p (k t) -> p k t", k=NF): e.dma_start(out=d, in_=s),
          r=[f"xns{k}" for k in range(NF)],
          w=[f"dxn{st}_{k}" for k in range(NF)], dma=True, tag="xnsp")
        E("vector", lambda e: e.tensor_tensor(out=r_t, in0=r_r1, in1=r_r1, op=ALU.mult),
          r=["r_r1"], w=["r_t"], tag="r2a")
        E("vector", lambda e: e.tensor_tensor(out=r_v, in0=r_v, in1=r_t, op=ALU.mult),
          r=["r_v", "r_t"], w=["r_v"], tag="r2b")
        E("scalar", lambda e: e.activation(out=r_t, in_=r_v, func=AF.Ln, bias=EPS),
          r=["r_v", "c_fc"], w=["r_t"], tag="r2c")
        E("scalar", lambda e: e.activation(out=r_r2, in_=r_t, func=AF.Exp, scale=-0.5),
          r=["r_t", "c_fc"], w=["r_r1"], tag="r2d")
        bcast_row(r_r2, "r_r1", 4, "bc_r2")
        for k in range(NF):
            E("vector", lambda e, d=xa[k], p=pf[4][:, 0:ST]:
              e.tensor_tensor(out=d, in0=d, in1=p, op=ALU.mult),
              r=[f"xa{par}_{k}", PSB[4]], w=[f"xa{par}_{k}"], tag="l0a")
            E("vector", lambda e, d=xa[k], w1=s_lnc[:, k:k + 1], bb=s_lnc[:, 8 + k:9 + k]:
              e.tensor_scalar(out=d, in0=d, scalar1=w1, scalar2=bb,
                              op0=ALU.mult, op1=ALU.add),
              r=[f"xa{par}_{k}", "c_lnc"], w=[f"xa{par}_{k}"], tag="l0b")
        for h2 in range(2):
            sl = slice(h2 * 128, (h2 + 1) * 128)
            tpose(pf[5][:, 384:392], g_lf[:, sl], s_id_f[0:8, 0:8],
                  rb=["g_lf", "c_idf"], wb=[PSB[5]], tag="glt")
            E("vector", lambda e, d=g_lt[h2], p=pf[5][:, 384:392]:
              e.tensor_copy(out=d, in_=p), r=[PSB[5]], w=[f"glt{h2}"], tag="glte")
        for h2 in range(2):
            lt = g_lt[h2]
            lb = f"glt{h2}"
            E("vector", lambda e, s=lt: e.tensor_reduce(out=g_m1, in_=s, axis=AX.X,
                                                        op=ALU.max),
              r=[lb], w=["gm1"], tag="gm1")
            E("vector", lambda e, s=lt: e.tensor_scalar(out=g_d, in0=s, scalar1=g_m1,
                                                        scalar2=None, op0=ALU.subtract),
              r=[lb, "gm1"], w=["gd"], tag="gd")
            E("vector", lambda e: e.tensor_scalar(out=g_sp, in0=g_d, scalar1=0.0,
                                                  scalar2=-1e30, op0=ALU.is_ge,
                                                  op1=ALU.mult),
              r=["gd"], w=["gsp"], tag="gsp")
            E("vector", lambda e, s=lt: e.tensor_tensor(out=g_sp, in0=g_sp, in1=s,
                                                        op=ALU.add),
              r=["gsp", lb], w=["gsp"], tag="gsp2")
            E("vector", lambda e: e.tensor_reduce(out=g_m1, in_=g_sp, axis=AX.X,
                                                  op=ALU.max),
              r=["gsp"], w=["gm1"], tag="gm2")
            E("vector", lambda e, s=lt: e.tensor_scalar(out=g_sp, in0=s, scalar1=g_m1,
                                                        scalar2=None, op0=ALU.subtract),
              r=[lb, "gm1"], w=["gsp"], tag="gd2")
            E("vector", lambda e: e.tensor_scalar(out=g_sp, in0=g_sp, scalar1=0.0,
                                                  scalar2=None, op0=ALU.is_ge),
              r=["gsp"], w=["gsp"], tag="gmask")
            E("scalar", lambda e: e.activation(out=g_ex, in_=g_d, func=AF.Exp),
              r=["gd"], w=["gex"], tag="gex")
            E("vector", lambda e: e.tensor_reduce(out=g_ss, in_=g_ex, axis=AX.X,
                                                  op=ALU.add),
              r=["gex"], w=["gss"], tag="gss")
            E("vector", lambda e: e.reciprocal(out=g_ss, in_=g_ss),
              r=["gss"], w=["gss"], tag="grc")
            E("vector", lambda e: e.tensor_scalar(out=g_ex, in0=g_ex, scalar1=g_ss,
                                                  scalar2=None, op0=ALU.mult),
              r=["gex", "gss"], w=["gex"], tag="gsc")
            E("vector", lambda e, d=g_w[h2]: e.tensor_tensor(out=d, in0=g_ex, in1=g_sp,
                                                             op=ALU.mult),
              r=["gex", "gsp"], w=[f"gw{h2}"], tag="gw")
            tpose(pf[5][0:8, 384:512], g_w[h2], s_id_f,
                  rb=[f"gw{h2}", "c_idf"], wb=[PSB[5]], tag="gwt")
            E("vector", lambda e, d=g_wf[:, h2 * 128:(h2 + 1) * 128], p=pf[5][0:8, 384:512]:
              e.tensor_copy(out=d, in_=p), r=[PSB[5]], w=["g_lf"], tag="gwfe")
        E("gpsimd", lambda e, d=d_w[:, cols], s=g_wf: e.dma_start(out=d, in_=s),
          r=["g_lf"], w=[f"dw{st}"], dma=True, tag="wsp")
        stats_pass(xa, [f"xa{par}_{k}" for k in range(NF)], "s3")
        ln_rows(DM, "r3")
        bcast_row(r_r1, "r_r1", 4, "bc_r3")
        bcast_row(r_mr, "r_mr", 5, "bc_mr3")
        for k in range(NF):
            tm = xtmp[k % 2]
            E("vector", lambda e, d=tm, s=xa[k], p=pf[4][:, 0:ST]:
              e.tensor_tensor(out=d, in0=s, in1=p, op=ALU.mult),
              r=[f"xa{par}_{k}", PSB[4]], w=[f"xtmp{k % 2}"], tag="l1a")
            E("vector", lambda e, d=tm, p=pf[5][:, 0:ST]:
              e.tensor_tensor(out=d, in0=d, in1=p, op=ALU.subtract),
              r=[f"xtmp{k % 2}", PSB[5]], w=[f"xtmp{k % 2}"], tag="l1b")
            E("vector", lambda e, d=xmb[k], s=tm, w1=s_lnc[:, 16 + k:17 + k],
              bb=s_lnc[:, 24 + k:25 + k]:
              e.tensor_scalar(out=d, in0=s, scalar1=w1, scalar2=bb,
                              op0=ALU.mult, op1=ALU.add),
              r=[f"xtmp{k % 2}", "c_lnc"], w=[f"xmb{k}"], tag="l1c")
        for o in range(34):
            o0 = o * 128
            o1 = min(DIP, o0 + 128)
            ow = o1 - o0
            bank = o % 3
            for k in range(NF):
                mm(pf[bank][0:ow, 0:ST], ipW[k][:, o0:o1], xmb[k], k == 0, k == NF - 1,
                   rb=[f"ipW{k}", f"xmb{k}", "warena"], wb=[PSB[bank]], tag="ipm")
            if o < 16:
                E("scalar", lambda e, d=zt[o], p=pf[bank][0:128, 0:ST]:
                  e.activation(out=d, in_=p, func=AF.Silu),
                  r=[PSB[bank]], w=[f"zt{o}"], tag="zev")
            elif o < 33:
                j = o - 16
                E("scalar", lambda e, d=xBCr[j][:, 3:3 + ST], p=pf[bank][0:128, 0:ST]:
                  e.activation(out=d, in_=p, func=AF.Copy),
                  r=[PSB[bank]], w=[f"xBC{j}"], tag="xbcev")
            else:
                E("scalar", lambda e, p=pf[bank][0:32, 0:ST]:
                  e.activation(out=dtraw, in_=p, func=AF.Copy),
                  r=[PSB[bank]], w=["dtraw"], tag="dtev")
        if st == 0 or st == 8:
            for j in range(NCV):
                E("vector", lambda e, d=xBCr[j][:, 0:3]: e.memset(d, 0.0),
                  w=[f"xBC{j}"], tag="cvz")
        for j in range(NCV):
            dst = xst[j] if j < 16 else bct
            dn = f"xst{j}" if j < 16 else "bct"
            cs = xtmp[j % 2]
            E("vector", lambda e, d=cs, s=xBCr[j][:, 0:ST], w1=s_cwc[:, 4 * j:4 * j + 1]:
              e.tensor_scalar(out=d, in0=s, scalar1=w1, scalar2=None, op0=ALU.mult),
              r=[f"xBC{j}", "c_cwc"], w=[f"xtmp{j % 2}"], tag="cv0")
            for tap in (1, 2, 3):
                E("vector", lambda e, d=cs, s=xBCr[j][:, tap:tap + ST],
                  w1=s_cwc[:, 4 * j + tap:4 * j + tap + 1]:
                  e.scalar_tensor_tensor(out=d, in0=s, scalar=w1, in1=d,
                                         op0=ALU.mult, op1=ALU.add),
                  r=[f"xBC{j}", f"xtmp{j % 2}", "c_cwc"], w=[f"xtmp{j % 2}"], tag="cvt")
            E("scalar", lambda e, d=dst, s=cs, bb=s_cbc[:, j:j + 1]:
              e.activation(out=d, in_=s, func=AF.Silu, bias=bb),
              r=[f"xtmp{j % 2}", "c_cbc"], w=[dn], tag="cvs")
            E("gpsimd", lambda e, d=xBCr[j][:, 0:3], s=xBCr[j][:, ST:ST + 3]:
              e.tensor_copy(out=d, in_=s), r=[f"xBC{j}"], w=[f"xBC{j}"], tag="cvh")
        E("sync", lambda e: e.dma_start(out=sCf, in_=bct[64:128, :]),
          r=["bct"], w=["sCf"], dma=True, tag="crel")
        E("scalar", lambda e: e.activation(out=dtt, in_=dtraw, func=AF.Abs, bias=s_dtb),
          r=["dtraw", "c_dtb"], w=["dtt"], tag="dt1")
        E("scalar", lambda e: e.activation(out=dtt, in_=dtt, func=AF.Exp, scale=-1.0),
          r=["dtt", "c_fc"], w=["dtt"], tag="dt2")
        E("scalar", lambda e: e.activation(out=dtt, in_=dtt, func=AF.Ln, bias=1.0),
          r=["dtt", "c_fc"], w=["dtt"], tag="dt3")
        E("scalar", lambda e: e.activation(out=dtf, in_=dtraw, func=AF.Relu, bias=s_dtb),
          r=["dtraw", "c_dtb"], w=["dtf"], tag="dt4")
        E("vector", lambda e: e.tensor_tensor(out=dtf, in0=dtf, in1=dtt, op=ALU.add),
          r=["dtf", "dtt"], w=["dtf"], tag="dt5")
        E("vector", lambda e: e.tensor_scalar(out=afm, in0=dtf, scalar1=s_acol,
                                              scalar2=None, op0=ALU.mult),
          r=["dtf", "acol"], w=["dtraw"], tag="dta")
        for qq in range(2):
            ssd_chunk(st, qq, seq)
        for j in range(NPAIR):
            E("vector", lambda e, d=xst[j], s=zt[j]:
              e.tensor_tensor(out=d, in0=d, in1=s, op=ALU.mult),
              r=[f"xst{j}", f"zt{j}"], w=[f"xst{j}"], tag="gat")
        for j in range(NPAIR):
            sq_ = xsq[j % 2]
            E("scalar", lambda e, d=sq_, s=xst[j]: e.activation(out=d, in_=s,
                                                                func=AF.Square),
              r=[f"xst{j}"], w=[f"xsq{j % 2}"], tag="rmq")
            mm(pf[4][0:1, 0:ST], s_ocb, sq_, j == 0, j == NPAIR - 1,
               rb=["c_ocb", f"xsq{j % 2}"], wb=[PSB[4]], tag="rmm")
        E("scalar", lambda e, p=pf[4][0:1, 0:ST]:
          e.activation(out=r_t, in_=p, func=AF.Ln, scale=1.0 / DI, bias=EPS),
          r=[PSB[4], "c_fc"], w=["r_t"], tag="rml")
        E("scalar", lambda e: e.activation(out=r_rr, in_=r_t, func=AF.Exp, scale=-0.5),
          r=["r_t", "c_fc"], w=["r_r1"], tag="rme")
        bcast_row(r_rr, "r_r1", 5, "bc_rms")
        for j in range(NPAIR):
            E("vector", lambda e, d=zt[j], s=xst[j], w1=s_nwp[:, j:j + 1], p=pf[5][:, 0:ST]:
              e.scalar_tensor_tensor(out=d, in0=s, scalar=w1, in1=p,
                                     op0=ALU.mult, op1=ALU.mult),
              r=[f"xst{j}", "c_nwp", PSB[5]], w=[f"zt{j}"], tag="rma")
        for o in range(NF):
            bank = o % 3
            for k in range(16):
                mm(pf[bank][:, 0:ST], opW[k][:, o * 128:(o + 1) * 128], zt[k],
                   k == 0, k == 15, rb=[f"opW{k}", f"zt{k}", "warena"],
                   wb=[PSB[bank]], tag="opm")
            E("vector", lambda e, d=xa[o], p=pf[bank][:, 0:ST]:
              e.tensor_tensor(out=d, in0=d, in1=p, op=ALU.add),
              r=[f"xa{par}_{o}", PSB[bank]], w=[f"xa{par}_{o}"], tag="opev")
        E("gpsimd", lambda e, d=d_xb[:, cols].rearrange("(k p) t -> p k t", p=128),
          s=xavP[par].rearrange("p (k t) -> p k t", k=NF): e.dma_start(out=d, in_=s),
          r=[f"xa{par}_{k}" for k in range(NF)],
          w=[f"dxb{st}_{k}" for k in range(NF)], dma=True, tag="xbsp")
        stats_pass(xa, [f"xa{par}_{k}" for k in range(NF)], "s4")
        ln_rows(DM, "r4")
        bcast_row(r_r1, "r_r1", 4, "bc_r4")
        bcast_row(r_mr, "r_mr", 5, "bc_mr4")
        for k in range(NF):
            tm = xtmp[k % 2]
            E("vector", lambda e, d=tm, s=xa[k], p=pf[4][:, 0:ST]:
              e.tensor_tensor(out=d, in0=s, in1=p, op=ALU.mult),
              r=[f"xa{par}_{k}", PSB[4]], w=[f"xtmp{k % 2}"], tag="l2a")
            E("vector", lambda e, d=tm, p=pf[5][:, 0:ST]:
              e.tensor_tensor(out=d, in0=d, in1=p, op=ALU.subtract),
              r=[f"xtmp{k % 2}", PSB[5]], w=[f"xtmp{k % 2}"], tag="l2b")
            E("vector", lambda e, d=xmb[k], s=tm, w1=s_lnc[:, 32 + k:33 + k],
              bb=s_lnc[:, 40 + k:41 + k]:
              e.tensor_scalar(out=d, in0=s, scalar1=w1, scalar2=bb,
                              op0=ALU.mult, op1=ALU.add),
              r=[f"xtmp{k % 2}", "c_lnc"], w=[f"xmb{k}"], tag="l2c")
        E("gpsimd", lambda e, d=d_xc[:, cols].rearrange("(k p) t -> p k t", p=128),
          s=xmbv.rearrange("p (k t) -> p k t", k=NF): e.dma_start(out=d, in_=s),
          r=[f"xmb{k}" for k in range(NF)],
          w=[f"dxc{st}_{k}" for k in range(NF)], dma=True, tag="xcsp")

    def p1b():
        STB = 512
        E("vector", lambda e: e.engine_nop(), w=["warena"], deps=B.all_tokens(), tag="wgate1b")
        E("vector", lambda e: e.engine_nop(), w=["aarena"], deps=B.all_tokens(), tag="agate1b")
        jobs = []
        for k in range(NF):
            for c4 in range(8):
                c0, c1 = c4 * 512, (c4 + 1) * 512
                jobs.append((f1T[k * 128:(k + 1) * 128, c0:c1], f1W[k][:, c0:c1],
                             512, f"f1W{k}", p1bst, "p1bst"))
        for k in range(32):
            for c4 in range(2):
                c0, c1 = c4 * 512, (c4 + 1) * 512
                jobs.append((f2T[k * 128:(k + 1) * 128, c0:c1], f2W[k][:, c0:c1],
                             512, f"f2W{k}", p1bst, "p1bst"))
        w_stream(jobs)
        for st in range(8):
            cols = slice(st * STB, (st + 1) * STB)
            sts = (2 * st, 2 * st + 1)
            E("sync", lambda e, d=xcbv.rearrange("p (k t) -> p k t", k=NF),
              s=d_xc[:, cols].rearrange("(k p) t -> p k t", p=128):
              e.dma_start(out=d, in_=s),
              r=["aarena"] + [f"dxc{s_}_{k}" for s_ in sts for k in range(NF)],
              w=[f"xcb{k}" for k in range(NF)], dma=True, tag="xcld")
            E("sync", lambda e, d=xbfv.rearrange("p (k t) -> p k t", k=NF),
              s=d_xb[:, cols].rearrange("(k p) t -> p k t", p=128):
              e.dma_start(out=d, in_=s),
              r=["aarena"] + [f"dxb{s_}_{k}" for s_ in sts for k in range(NF)],
              w=[f"xbf{k}" for k in range(NF)], dma=True, tag="xbld")
            for o in range(32):
                bank = o % 3
                for k in range(NF):
                    mm(pf[bank][:, 0:STB], f1W[k][:, o * 128:(o + 1) * 128], xcb[k],
                       k == 0, k == NF - 1,
                       rb=[f"f1W{k}", f"xcb{k}", "warena"], wb=[PSB[bank]], tag="f1m")
                E("scalar", lambda e, d=g1b[o], p=pf[bank][:, 0:STB],
                  bb=s_b1c[:, o:o + 1]:
                  e.activation(out=d, in_=p, func=AF.Gelu, bias=bb),
                  r=[PSB[bank], "c_b1c"], w=[f"g1b{o}"], tag="gelu")
            for o in range(NF):
                bank = o % 3
                for k in range(32):
                    mm(pf[bank][:, 0:STB], f2W[k][:, o * 128:(o + 1) * 128], g1b[k],
                       k == 0, k == 31,
                       rb=[f"f2W{k}", f"g1b{k}", "warena"], wb=[PSB[bank]], tag="f2m")
                E("vector", lambda e, d=xbf[o], p=pf[bank][:, 0:STB],
                  bb=s_b2c[:, o:o + 1]:
                  e.scalar_tensor_tensor(out=d, in0=p, scalar=bb, in1=d,
                                         op0=ALU.add, op1=ALU.add),
                  r=[PSB[bank], "c_b2c", f"xbf{o}"], w=[f"xbf{o}"], tag="f2ev")
            E("gpsimd", lambda e, d=d_s[:, cols].rearrange("(k p) t -> p k t", p=128),
              s=xbfv.rearrange("p (k t) -> p k t", k=NF): e.dma_start(out=d, in_=s),
              r=[f"xbf{k}" for k in range(NF)],
              w=[f"ds{s_}_{k}" for s_ in sts for k in range(NF)], dma=True, tag="ssp")

    def p2():
        E("vector", lambda e: e.engine_nop(), w=["warena"], deps=B.all_tokens(), tag="wgate2")
        E("vector", lambda e: e.engine_nop(), w=["aarena"], deps=B.all_tokens(), tag="agate2")

        def load_gu(ee):
            jobs = []
            for k in range(NF):
                for c4 in range(2):
                    c0, c1 = c4 * 1408, (c4 + 1) * 1408
                    jobs.append((guT[ee, k * 128:(k + 1) * 128, c0:c1],
                                 guW[k][:, c0:c1], 1408, f"guW{k}", p2st, "p2st"))
            w_stream(jobs)

        def load_dn(ee, pp):
            jobs = []
            for k in range(NMI):
                jobs.append((dnT[ee, k * 128:(k + 1) * 128, :],
                             dnW[pp][k][:, 0:1024], 1024, f"dnW{pp}_{k}", p2st, "p2st"))
            w_stream(jobs)

        for qt in range(NQ):
            qcols = slice(qt * QT, (qt + 1) * QT)
            wsts = [qt * 4 + i for i in range(4)]
            E("sync", lambda e, d=p2w, s=d_w[:, qcols]: e.dma_start(out=d, in_=s),
              r=["aarena"] + [f"dw{s_}" for s_ in wsts], w=["p2w"], dma=True, tag="p2wld")
            for ee in range(NE):
                if qt == 0 and ee == 0:
                    load_gu(0)
                    load_dn(0, 0)
                for tt2 in range(2):
                    tcols = slice(qt * QT + tt2 * TT, qt * QT + (tt2 + 1) * TT)
                    xsts = [(qt * QT + tt2 * TT) // ST, (qt * QT + tt2 * TT) // ST + 1]
                    E("sync", lambda e, d=p2xnv.rearrange("p (k t) -> p k t", k=NF),
                      s=d_xn[:, tcols].rearrange("(k p) t -> p k t", p=128):
                      e.dma_start(out=d, in_=s),
                      r=["aarena"] + [f"dxn{s_}_{k}" for s_ in xsts for k in range(NF)],
                      w=[f"p2xn{k}" for k in range(NF)], dma=True, tag="p2xld")
                    for j in range(NMI):
                        for k in range(NF):
                            mm(pf[0][:, 0:TT], guW[k][:, j * 128:(j + 1) * 128], p2xn[k],
                               k == 0, k == NF - 1,
                               rb=[f"guW{k}", f"p2xn{k}", "warena"], wb=[PSB[0]],
                               tag="gum")
                        for k in range(NF):
                            mm(pf[1][:, 0:TT], guW[k][:, MI + j * 128:MI + (j + 1) * 128],
                               p2xn[k], k == 0, k == NF - 1,
                               rb=[f"guW{k}", f"p2xn{k}", "warena"], wb=[PSB[1]],
                               tag="guu")
                        E("scalar", lambda e, d=p2h[tt2][j], p=pf[0][:, 0:TT]:
                          e.activation(out=d, in_=p, func=AF.Silu),
                          r=[PSB[0]], w=[f"p2h{tt2}_{j}"], tag="hsl")
                        E("vector", lambda e, d=p2h[tt2][j], p=pf[1][:, 0:TT]:
                          e.tensor_tensor(out=d, in0=d, in1=p, op=ALU.mult),
                          r=[f"p2h{tt2}_{j}", PSB[1]], w=[f"p2h{tt2}_{j}"], tag="hml")
                ne_, nq_ = (ee + 1, qt) if ee < NE - 1 else (0, qt + 1)
                if nq_ < NQ:
                    load_gu(ne_)
                    load_dn(ne_, ne_ % 2)
                pp_ = ee % 2
                for tt2 in range(2):
                    mm(pf[4][:, 0:TT], s_oh8[:, ee * 128:(ee + 1) * 128],
                       p2w[:, tt2 * TT:(tt2 + 1) * TT], True, True,
                       rb=["c_oh8", "p2w"], wb=[PSB[4]], tag="wbm")
                    E("vector", lambda e, p=pf[4][:, 0:TT]: e.tensor_copy(out=p2wbc, in_=p),
                      r=[PSB[4]], w=["p2wbc"], tag="wbe")
                    for o in range(NF):
                        bank = 2 + o % 2
                        for k in range(NMI):
                            mm(pf[bank][:, 0:TT], dnW[pp_][k][:, o * 128:(o + 1) * 128],
                               p2h[tt2][k], k == 0, k == NMI - 1,
                               rb=[f"dnW{pp_}_{k}", f"p2h{tt2}_{k}", "warena"],
                               wb=[PSB[bank]], tag="dnm")
                        asl = slice(tt2 * TT, (tt2 + 1) * TT)
                        if ee == 0:
                            E("vector", lambda e, d=p2acc[o][:, asl], p=pf[bank][:, 0:TT]:
                              e.tensor_tensor(out=d, in0=p, in1=p2wbc, op=ALU.mult),
                              r=[PSB[bank], "p2wbc"], w=[f"p2acc{o}"], tag="cb0")
                        else:
                            E("vector", lambda e, p=pf[bank][:, 0:TT]:
                              e.tensor_tensor(out=p2tmp, in0=p, in1=p2wbc, op=ALU.mult),
                              r=[PSB[bank], "p2wbc"], w=["p2tmp"], tag="cbm")
                            E("gpsimd", lambda e, d=p2acc[o][:, asl]:
                              e.tensor_tensor(out=d, in0=d, in1=p2tmp, op=ALU.add),
                              r=[f"p2acc{o}", "p2tmp"], w=[f"p2acc{o}"], tag="cba")
            E("gpsimd", lambda e, d=d_acc[:, qcols].rearrange("(k p) t -> p k t", p=128),
              s=p2accv.rearrange("p (k t) -> p k t", k=NF): e.dma_start(out=d, in_=s),
              r=[f"p2acc{k}" for k in range(NF)],
              w=[f"dacc{qt}_{k}" for k in range(NF)], dma=True, tag="accsp")

    def p3():
        E("vector", lambda e: e.engine_nop(), w=["warena"], deps=B.all_tokens(), tag="wgate3")
        E("vector", lambda e: e.engine_nop(), w=["aarena"], deps=B.all_tokens(), tag="agate3")
        jobs = []
        for (srcT, dstL, nm) in [(sgT, sgW, "sgW"), (suT, suW, "suW")]:
            for k in range(NF):
                for c4 in range(2):
                    c0, c1 = c4 * 1408, (c4 + 1) * 1408
                    jobs.append((srcT[k * 128:(k + 1) * 128, c0:c1],
                                 dstL[k][:, c0:c1], 1408, f"{nm}{k}", p3st, "p3st"))
        for k in range(NSI):
            jobs.append((sdT[k * 128:(k + 1) * 128, :], sdW[k][:, 0:1024],
                         1024, f"sdW{k}", p3st, "p3st"))
        w_stream(jobs)
        for st in range(NST):
            cols = slice(st * ST, (st + 1) * ST)
            qt = st // 4
            for (dst, srct, rl, wpfx, tg) in [
                (p3xnv, d_xn, [f"dxn{st}_{k}" for k in range(NF)], "p3xn", "p3xnld"),
                (p3sv, d_s, [f"ds{st}_{k}" for k in range(NF)], "p3s", "p3sld"),
                (p3xv, xfm, [], "p3x", "p3xld"),
                (p3acv, d_acc, [f"dacc{qt}_{k}" for k in range(NF)], "p3ac", "p3acld"),
            ]:
                E("sync", lambda e, d=dst.rearrange("p (k t) -> p k t", k=NF),
                  s=srct[:, cols].rearrange("(k p) t -> p k t", p=128):
                  e.dma_start(out=d, in_=s), r=["aarena"] + rl,
                  w=[f"{wpfx}{k}" for k in range(NF)], dma=True, tag=tg)
            for j in range(NSI):
                b0 = (j % 2) * 2
                b1_ = b0 + 1
                for k in range(NF):
                    mm(pf[b0][:, 0:ST], sgW[k][:, j * 128:(j + 1) * 128], p3xn[k],
                       k == 0, k == NF - 1,
                       rb=[f"sgW{k}", f"p3xn{k}", "warena"], wb=[PSB[b0]], tag="shg")
                for k in range(NF):
                    mm(pf[b1_][:, 0:ST], suW[k][:, j * 128:(j + 1) * 128], p3xn[k],
                       k == 0, k == NF - 1,
                       rb=[f"suW{k}", f"p3xn{k}", "warena"], wb=[PSB[b1_]], tag="shu")
                E("scalar", lambda e, d=p3hs[j], p=pf[b0][:, 0:ST]:
                  e.activation(out=d, in_=p, func=AF.Silu),
                  r=[PSB[b0]], w=[f"p3hs{j}"], tag="shs")
                E("vector", lambda e, d=p3hs[j], p=pf[b1_][:, 0:ST]:
                  e.tensor_tensor(out=d, in0=d, in1=p, op=ALU.mult),
                  r=[f"p3hs{j}", PSB[b1_]], w=[f"p3hs{j}"], tag="shm")
            for k in range(NF):
                mm(pf[4][0:1, 0:ST], s_segb[:, k:k + 1], p3xn[k], k == 0, k == NF - 1,
                   rb=["segb", f"p3xn{k}"], wb=[PSB[4]], tag="swm")
            E("scalar", lambda e, p=pf[4][0:1, 0:ST]:
              e.activation(out=p3sw, in_=p, func=AF.Tanh, scale=0.5),
              r=[PSB[4], "c_fc"], w=["p3sw"], tag="swt")
            E("vector", lambda e: e.tensor_scalar(out=p3sw, in0=p3sw, scalar1=1.0,
                                                  scalar2=0.5, op0=ALU.add, op1=ALU.mult),
              r=["p3sw"], w=["p3sw"], tag="sws")
            mm(pf[5][:, 0:ST], s_o1f, p3sw, True, True,
               rb=["c_o1f", "p3sw"], wb=[PSB[5]], tag="swb")
            E("vector", lambda e, p=pf[5][:, 0:ST]: e.tensor_copy(out=p3swb, in_=p),
              r=[PSB[5]], w=["p3swb"], tag="swbe")
            for o in range(NF):
                bank = o % 2
                for k in range(NSI):
                    mm(pf[bank][:, 0:ST], sdW[k][:, o * 128:(o + 1) * 128], p3hs[k],
                       k == 0, k == NSI - 1,
                       rb=[f"sdW{k}", f"p3hs{k}", "warena"], wb=[PSB[bank]], tag="sdm")
                E("vector", lambda e, d=p3g[o], p=pf[bank][:, 0:ST]:
                  e.tensor_tensor(out=d, in0=p, in1=p3swb, op=ALU.mult),
                  r=[PSB[bank], "p3swb"], w=[f"p3g{o}"], tag="sde")
                E("vector", lambda e, d=p3g[o], s=p3ac[o]:
                  e.tensor_tensor(out=d, in0=d, in1=s, op=ALU.add),
                  r=[f"p3g{o}", f"p3ac{o}"], w=[f"p3g{o}"], tag="sda")
                E("scalar", lambda e, d=p3g[o]: e.activation(out=d, in_=d, func=AF.Silu),
                  r=[f"p3g{o}"], w=[f"p3g{o}"], tag="gsl")
                E("vector", lambda e, d=p3g[o], s=p3s[o]:
                  e.tensor_tensor(out=d, in0=d, in1=s, op=ALU.mult),
                  r=[f"p3g{o}", f"p3s{o}"], w=[f"p3g{o}"], tag="gms")
                E("vector", lambda e, d=p3g[o], s=p3x[o]:
                  e.tensor_tensor(out=d, in0=d, in1=s, op=ALU.add),
                  r=[f"p3g{o}", f"p3x{o}"], w=[f"p3g{o}"], tag="gax")
            E("gpsimd", lambda e, d=ofm[:, cols].rearrange("(k p) t -> p k t", p=128),
              s=p3gv.rearrange("p (k t) -> p k t", k=NF): e.dma_start(out=d, in_=s),
              r=[f"p3g{k}" for k in range(NF)], w=["ofm"], dma=True, tag="outsp")


    for st in range(PREFIX if PREFIX else NST):
        p1a_supertile(st)
    if not PREFIX:
        if PHASES in ("1b", "2", "full"):
            p1b()
        if PHASES in ("2", "full"):
            p2()
        if PHASES == "full":
            p3()

    B.finalize()

    with nc.Block() as block:
        @block.sync
        def _(e):
            B.run_engine("sync", e, sems)

        @block.tensor
        def _(e):
            B.run_engine("tensor", e, sems)

        @block.vector
        def _(e):
            B.run_engine("vector", e, sems)

        @block.scalar
        def _(e):
            B.run_engine("scalar", e, sems)

        @block.gpsimd
        def _(e):
            B.run_engine("gpsimd", e, sems)
            for s in ("dq_sync_0", "dq_sync_1", "dq_scalar_0", "dq_scalar_1",
                      "dq_gpsimd_0", "dq_gpsimd_1"):
                if B.cnt.get(s):
                    e.wait_ge(sems[s], B.cnt[s])

    stack.close()
    return nc


def _host_prep(inputs):
    x = np.asarray(inputs["L0_attn_x"])
    f = np.ascontiguousarray
    d = {}
    d["ipT"] = f(np.asarray(inputs["L0_attn_in_proj_w"]).T)
    d["opT"] = f(np.asarray(inputs["L0_attn_out_proj_w"]).T)
    d["f1T"] = f(np.asarray(inputs["L0_attn_ffn_w1"]).T)
    d["f2T"] = f(np.asarray(inputs["L0_attn_ffn_w2"]).T)
    d["guT"] = f(np.asarray(inputs["L0_ffn_experts_gate_up"]).transpose(0, 2, 1))
    d["dnT"] = f(np.asarray(inputs["L0_ffn_experts_down"]).transpose(0, 2, 1))
    d["sgT"] = f(np.asarray(inputs["L0_ffn_shared_gate_w"]).T)
    d["suT"] = f(np.asarray(inputs["L0_ffn_shared_up_w"]).T)
    d["sdT"] = f(np.asarray(inputs["L0_ffn_shared_down_w"]).T)
    d["gTc"] = f(np.asarray(inputs["L0_ffn_gate_w"]).T)
    d["segc"] = f(np.asarray(inputs["L0_ffn_shared_expert_gate_w"]).reshape(8, 128).T)
    ln = np.zeros((128, 48), np.float32)
    for i, nm in enumerate(["ln0_w", "ln0_b", "ln1_w", "ln1_b", "ln2_w", "ln2_b"]):
        ln[:, i * 8:(i + 1) * 8] = np.asarray(inputs["L0_attn_" + nm]).reshape(8, 128).T
    d["lnc"] = ln
    d["cwc"] = f(np.asarray(inputs["L0_attn_conv_w"]).reshape(17, 128, 4)
                 .transpose(1, 0, 2).reshape(128, 68))
    d["cbc"] = f(np.asarray(inputs["L0_attn_conv_b"]).reshape(17, 128).T)
    d["b1c"] = f(np.asarray(inputs["L0_attn_ffn_b1"]).reshape(32, 128).T)
    d["b2c"] = f(np.asarray(inputs["L0_attn_ffn_b2"]).reshape(8, 128).T)
    d["nwp"] = f(np.asarray(inputs["L0_attn_norm_w"]).reshape(16, 128).T)
    D = np.asarray(inputs["L0_attn_D_param"]).astype(np.float32)
    dp = np.zeros((128, 16), np.float32)
    for c in range(16):
        dp[0:64, c] = D[2 * c]
        dp[64:128, c] = D[2 * c + 1]
    d["dpr"] = dp
    d["dtb"] = f(np.asarray(inputs["L0_attn_dt_bias"]).reshape(32, 1))
    d["alg"] = f(np.asarray(inputs["L0_attn_A_log"]).reshape(32, 1))
    idx = np.arange(128)
    d["c_lt_b"] = (idx[:, None] <= idx[None, :]).astype(BFNP)
    d["c_ut_b"] = (idx[:, None] > idx[None, :]).astype(BFNP)
    d["c_id_b"] = np.eye(128, dtype=BFNP)
    d["c_id_f"] = np.eye(128, dtype=np.float32)
    sel = np.zeros((32, 2048), BFNP)
    for c in range(16):
        for p_ in range(128):
            sel[2 * c + (p_ >= 64), c * 128 + p_] = 1
    d["c_sel"] = sel
    d["c_o1f"] = np.ones((1, 128), np.float32)
    d["c_ocb"] = np.ones((128, 1), BFNP)
    oh = np.zeros((8, 1024), np.float32)
    for e_ in range(8):
        oh[e_, e_ * 128:(e_ + 1) * 128] = 1
    d["c_oh8"] = oh
    fc = np.zeros((128, 8), np.float32)
    for fi, fv in enumerate([1e-5, 1.0, -0.5, -1.0, 0.5, 1.0 / 2048.0, 1.0 / 1024.0, 2.0]):
        fc[:, fi] = fv
    d["c_fc"] = fc
    xcore = []
    for i in range(8):
        xf = np.empty((1024, 4096), np.float32)
        xf[:, :2048] = x[2 * i].T
        xf[:, 2048:] = x[2 * i + 1].T
        xcore.append(xf)
    return d, xcore


_LAST = {}


def kernel(**inputs):
    shared, xcore = _host_prep(inputs)
    nc = build_program()
    in_maps = []
    for i in range(8):
        m = dict(shared)
        m["xfm"] = xcore[i]
        in_maps.append(m)
    res = run_bass_kernel_spmd(nc, in_maps, core_ids=list(range(8)))
    _LAST["res"] = res
    out = np.empty((16, 2048, 1024), np.float32)
    for i in range(8):
        o = res.results[i]["ofm"]
        out[2 * i] = o[:, :2048].T
        out[2 * i + 1] = o[:, 2048:].T
    return out


# version 12
# speedup vs baseline: 1.2445x; 1.0456x over previous
import numpy as np
import ml_dtypes
import concourse.bass as bass
import concourse.mybir as mybir
from concourse.bass_utils import run_bass_kernel_spmd

F32 = mybir.dt.float32
BF16 = mybir.dt.bfloat16
AF = mybir.ActivationFunctionType
ALU = mybir.AluOpType
AX = mybir.AxisListType
BFNP = ml_dtypes.bfloat16

T = 4096
DM = 1024
NF = 8
DIP = 4256
DI = 2048
NCV = 17
NH = 32
NPAIR = 16
FFN = 4096
ST = 256
NST = 16
Q = 128
NE = 8
MI = 1408
NMI = 11
GU2 = 2816
NGU = 22
SI = 2816
NSI = 22
QT = 1024
NQ = 4
TT = 512
EPS = 1e-5

SB_SZ = 229376

DEBUG = False
PHASES = "full"
PREFIX = 0


class Buf:
    __slots__ = ("w", "rs", "name")

    def __init__(self, name=""):
        self.w = None
        self.rs = []
        self.name = name


class Bld:
    def __init__(self):
        self.ops = []

    def emit(self, eng, fn, reads=(), writes=(), deps=(), dma=False, tag=""):
        i = len(self.ops)
        dl = list(deps)
        for b in reads:
            if b.w is not None:
                dl.append((b.w, True))
        for b in writes:
            if b.w is not None:
                dl.append((b.w, False))
            for t in b.rs:
                dl.append((t, False))
        self.ops.append((eng, fn, dl, dma, tag))
        for b in reads:
            b.rs.append(i)
        for b in writes:
            b.w = i
            b.rs = []
        if not hasattr(self, "last"):
            self.last = {}
        self.last[eng] = i
        return i

    def all_tokens(self):
        return [(i, False) for i in getattr(self, "last", {}).values()]

    def finalize(self):
        semof = {"tensor": "pe", "vector": "ve", "scalar": "sc", "gpsimd": "gp"}
        n = len(self.ops)
        eng_of = [o[0] for o in self.ops]
        is_dma = [o[3] for o in self.ops]
        need = [None] * n
        depended = set()
        for i in range(n):
            eng = eng_of[i]
            nd = []
            for (j, raw) in self.ops[i][2]:
                if j == i:
                    continue
                same = eng_of[j] == eng and not is_dma[j] and not is_dma[i]
                if same and eng == "tensor":
                    continue
                if same and not raw:
                    continue
                nd.append(j)
                depended.add(j)
            need[i] = nd
        G = 4
        val = {}
        cnt = {}
        qidx = {}
        self.fence = {}
        qgrp_end = {}
        per_q_ops = {}
        for i in range(n):
            if is_dma[i]:
                q = eng_of[i]
                per_q_ops.setdefault(q, []).append(i)
        for q, ops in per_q_ops.items():
            g = -1
            gsz = 0
            prev = -10
            for i in ops:
                if i != prev + 1 or gsz >= G:
                    g += 1
                    gsz = 0
                gsz += 1
                prev = i
                s = f"dq_{q}_{g % 2}"
                cnt[s] = cnt.get(s, 0) + 16
                qidx[i] = (q, g, s)
            acc = {}
            for i in ops:
                q_, g, s = qidx[i]
                acc[s] = acc.get(s, 0) + 16
                qgrp_end[(q, g)] = (s, acc[s])
            seen = set()
            for i in ops:
                q_, g, s = qidx[i]
                if g not in seen:
                    seen.add(g)
                    if g >= 2:
                        self.fence[i] = qgrp_end[(q, g - 2)]
        for i in range(n):
            if is_dma[i]:
                q, g, s = qidx[i]
                val[i] = (s, qgrp_end[(q, g)][1], 16)
            elif i in depended:
                s = semof[eng_of[i]]
                cnt[s] = cnt.get(s, 0) + 1
                val[i] = (s, cnt[s], 1)
        wm = {}
        waits = [None] * n
        for i in range(n):
            m = wm.setdefault(eng_of[i], {})
            wl = []
            for j in sorted(set(need[i])):
                s, v, a = val[j]
                if m.get(s, 0) < v:
                    wl.append((s, v))
                    m[s] = v
            waits[i] = wl
        self.waits = waits
        self.val = val
        self.cnt = cnt

    def run_engine(self, eng_name, e, sems):
        for i, (eng, fn, deps, dma, tag) in enumerate(self.ops):
            if eng != eng_name:
                continue
            for (s, v) in self.waits[i]:
                e.wait_ge(sems[s], v)
            if dma and i in self.fence:
                fs, fv = self.fence[i]
                e.wait_ge(sems[fs], fv)
            inst = fn(e)
            if i in self.val:
                s, v, a = self.val[i]
                inst.then_inc(sems[s], 16 if dma else a)


def build_program():
    nc = bass.Bass("TRN2", target_bir_lowering=False, debug=False)
    dt = nc.dram_tensor

    xfm = dt("xfm", [DM, T], F32, kind="ExternalInput").ap()
    ipT = dt("ipT", [DM, DIP], F32, kind="ExternalInput").ap()
    opT = dt("opT", [DI, DM], F32, kind="ExternalInput").ap()
    f1T = dt("f1T", [DM, FFN], F32, kind="ExternalInput").ap()
    f2T = dt("f2T", [FFN, DM], F32, kind="ExternalInput").ap()
    guT = dt("guT", [NE, DM, GU2], F32, kind="ExternalInput").ap()
    dnT = dt("dnT", [NE, MI, DM], F32, kind="ExternalInput").ap()
    sgT = dt("sgT", [DM, SI], F32, kind="ExternalInput").ap()
    suT = dt("suT", [DM, SI], F32, kind="ExternalInput").ap()
    sdT = dt("sdT", [SI, DM], F32, kind="ExternalInput").ap()
    gTc = dt("gTc", [DM, NE], F32, kind="ExternalInput").ap()
    segc = dt("segc", [128, 8], F32, kind="ExternalInput").ap()
    lnc = dt("lnc", [128, 48], F32, kind="ExternalInput").ap()
    cwc = dt("cwc", [128, 68], F32, kind="ExternalInput").ap()
    cbc = dt("cbc", [128, 17], F32, kind="ExternalInput").ap()
    b1c = dt("b1c", [128, 32], F32, kind="ExternalInput").ap()
    b2c = dt("b2c", [128, 8], F32, kind="ExternalInput").ap()
    nwp = dt("nwp", [128, 16], F32, kind="ExternalInput").ap()
    dpr = dt("dpr", [128, 16], F32, kind="ExternalInput").ap()
    dtb = dt("dtb", [32, 1], F32, kind="ExternalInput").ap()
    alg = dt("alg", [32, 1], F32, kind="ExternalInput").ap()
    c_lt_b = dt("c_lt_b", [128, 128], BF16, kind="ExternalInput").ap()
    c_ut_b = dt("c_ut_b", [128, 128], BF16, kind="ExternalInput").ap()
    c_id_b = dt("c_id_b", [128, 128], BF16, kind="ExternalInput").ap()
    c_id_f = dt("c_id_f", [128, 128], F32, kind="ExternalInput").ap()
    c_sel = dt("c_sel", [32, 2048], BF16, kind="ExternalInput").ap()
    c_o1f = dt("c_o1f", [1, 128], F32, kind="ExternalInput").ap()
    c_ocb = dt("c_ocb", [128, 1], BF16, kind="ExternalInput").ap()
    c_oh8 = dt("c_oh8", [8, 1024], F32, kind="ExternalInput").ap()
    c_fc = dt("c_fc", [128, 8], F32, kind="ExternalInput").ap()

    dbgk = "ExternalOutput" if DEBUG else "Internal"
    d_xn = dt("d_xn", [DM, T], BF16, kind=dbgk).ap()
    d_xb = dt("d_xb", [DM, T], F32, kind=dbgk).ap()
    d_xc = dt("d_xc", [DM, T], BF16, kind=dbgk).ap()
    d_w = dt("d_w", [8, T], F32, kind=dbgk).ap()
    d_s = dt("d_s", [DM, T], F32, kind=dbgk).ap()
    d_acc = dt("d_acc", [DM, T], F32, kind=dbgk).ap()
    d_f1 = dt("d_f1", [DM, FFN], BF16, kind="Internal").ap()
    d_f2 = dt("d_f2", [FFN, DM], BF16, kind="Internal").ap()
    d_gu = dt("d_gu", [NE, DM, GU2], BF16, kind="Internal").ap()
    d_dn = dt("d_dn", [NE, MI, DM], BF16, kind="Internal").ap()
    d_sg = dt("d_sg", [DM, SI], BF16, kind="Internal").ap()
    d_su = dt("d_su", [DM, SI], BF16, kind="Internal").ap()
    d_sd = dt("d_sd", [SI, DM], BF16, kind="Internal").ap()
    ofm = dt("ofm", [DM, T], F32, kind="ExternalOutput").ap()
    d_dbg = dt("d_dbg", [8, ST], F32, kind=dbgk).ap()
    d_dbg2 = dt("d_dbg2", [128, ST], F32, kind=dbgk).ap()
    d_dbg3 = dt("d_dbg3", [128, ST], F32, kind=dbgk).ap()

    nalloc = [0]
    off_ = [0]

    def at(shape, dtype, off):
        nalloc[0] += 1
        per = int(np.prod(shape[1:])) * mybir.dt.size(dtype)
        assert off + per <= SB_SZ, f"sbuf overflow {off}+{per}"
        h = nc.alloc_sbuf_tensor_at(f"sb{nalloc[0]}", list(shape), dtype,
                                    offset=off, align_bytes=32)
        return h.ap()

    def take(shape, dtype, n=1):
        per = int(np.prod(shape[1:])) * mybir.dt.size(dtype)
        per = -(-per // 32) * 32
        r = []
        for _ in range(n):
            r.append(at(shape, dtype, off_[0]))
            off_[0] += per
        return r if n > 1 else r[0]

    cursor = SB_SZ - nc.sbuf_bytes_remaining
    off_[0] = 17408
    assert cursor <= off_[0], ("bass internal allocs reach", cursor)
    s_lt_b = take([128, 128], BF16)
    s_ut_b = take([128, 128], BF16)
    s_id_b = take([128, 128], BF16)
    s_id_f = take([128, 128], F32)
    s_sel = take([32, 2048], BF16)
    s_o1f = take([1, 128], F32)
    s_ocb = take([128, 1], BF16)
    s_oh8 = take([8, 1024], F32)
    s_lnc = take([128, 48], F32)
    s_cwc = take([128, 68], F32)
    s_cbc = take([128, 17], F32)
    s_b1c = take([128, 32], F32)
    s_b2c = take([128, 8], F32)
    s_nwp = take([128, 16], F32)
    s_dpr = take([128, 16], F32)
    s_dtb = take([32, 1], F32)
    s_alg = take([32, 1], F32)
    s_acol = take([32, 1], F32)
    s_gtc = take([128, 8], F32, 8)
    s_seg = take([128, 8], F32)
    s_segb = take([128, 8], BF16)
    s_fc = take([128, 8], F32)
    assert off_[0] <= 30208, off_[0]

    WB = 30208
    W_SZ = 135168
    ipW = [at([128, DIP], BF16, WB + k * 8512) for k in range(NF)]
    opW = [at([128, DM], BF16, WB + 68096 + k * 2048) for k in range(16)]
    f1W = [at([128, FFN], BF16, WB + k * 8192) for k in range(NF)]
    f2W = [at([128, DM], BF16, WB + 65536 + k * 2048) for k in range(32)]
    guW = [at([128, GU2], BF16, WB + k * 5632) for k in range(NF)]
    dnW = [[at([128, DM], BF16, WB + 45056 + (p * NMI + k) * 2048)
            for k in range(NMI)] for p in range(2)]
    sgW = [at([128, SI], BF16, WB + k * 5632) for k in range(NF)]
    suW = [at([128, SI], BF16, WB + 45056 + k * 5632) for k in range(NF)]
    sdW = [at([128, DM], BF16, WB + 90112 + k * 2048) for k in range(NSI)]
    f1Wm = at([128, FFN * NF], BF16, WB)
    f2Wm = at([128, DM * 32], BF16, WB + 65536)
    guWm = at([128, GU2 * NF], BF16, WB)
    dnWm = [at([128, DM * NMI], BF16, WB + 45056 + p * NMI * 2048) for p in range(2)]
    sgWm = at([128, SI * NF], BF16, WB)
    suWm = at([128, SI * NF], BF16, WB + 45056)
    sdWm = at([128, DM * NSI], BF16, WB + 90112)
    p1bst = [at([128, 512], F32, WB + 131072 + i * 2048) for i in range(2)]

    AB = WB + W_SZ
    off_[0] = AB
    xavP = take([128, ST * NF], F32, 2)
    xaP = [[xavP[par][:, k * ST:(k + 1) * ST] for k in range(NF)] for par in range(2)]
    xmbv = take([128, ST * NF], BF16)
    xmb = [xmbv[:, k * ST:(k + 1) * ST] for k in range(NF)]
    xsc = take([128, ST], BF16, 2)
    xsq = take([128, ST], BF16, 2)
    xtmp = take([128, ST], F32, 2)
    zt = take([128, ST], BF16, NPAIR)
    xBCr = take([128, ST + 4], BF16, NCV)
    xst = take([128, ST], BF16, NPAIR)
    bct = take([128, ST], BF16)
    sCf = take([64, ST], BF16)
    HT = take([64, DI], BF16, 2)
    xsT = take([128, 128], BF16, NPAIR)
    assert off_[0] <= SB_SZ, off_[0]
    save_off = off_[0]
    off_[0] = WB + 100864
    sBT = take([128, 64], BF16)
    sBh = take([128, 64], BF16, 2)
    sGm = take([128, 128], BF16)
    sL2 = take([128, 128], BF16, 4)
    sM2 = take([128, 128], BF16, 4)
    sRL = take([128, 128], BF16, 4)
    sEb = take([128, 128], BF16, 2)
    sT1 = take([128, 128], BF16, 2)
    sE2 = take([128, 128], F32, 2)
    sdtT = take([128, 32], F32)
    slnd = take([128, 32], F32)
    saT = take([128, 32], F32)
    saTb = take([128, 32], BF16)
    swT = take([128, 32], F32)
    swd = take([128, 32], F32)
    sEf = take([32, 128], BF16)
    sLb = take([64, 32], F32)
    sLr = take([1, 32], F32)
    sLr2 = take([1, 32], F32)
    r_m, r_ms, r_t, r_v, r_r1, r_mr = take([1, ST], F32, 6)
    r_r2 = r_r1
    r_rr = r_r1
    dtraw = take([32, ST], F32)
    dtf = take([32, ST], F32)
    dtt = take([32, ST], F32)
    afm = dtraw
    g_lf = take([8, ST], F32)
    g_lt = take([128, 8], F32, 2)
    g_d = take([128, 8], F32)
    g_m1 = take([128, 1], F32)
    g_sp = take([128, 8], F32)
    g_ex = take([128, 8], F32)
    g_ss = take([128, 1], F32)
    g_w = take([128, 8], F32, 2)
    g_wf = g_lf
    Ast = take([128, 1024], F32, 2)
    AstB = take([128, 1024], BF16, 2)
    xns = take([128, ST * NF], BF16)
    assert off_[0] <= WB + W_SZ, off_[0]
    off_[0] = save_off

    STB = 512
    off_[0] = AB
    xcbv = take([128, STB * NF], BF16)
    xcb = [xcbv[:, k * STB:(k + 1) * STB] for k in range(NF)]
    g1b = take([128, STB], BF16, 32)
    xbfv = take([128, STB * NF], F32)
    xbf = [xbfv[:, k * STB:(k + 1) * STB] for k in range(NF)]
    assert off_[0] <= SB_SZ

    off_[0] = WB + 90112
    p2accv = take([128, QT * NF], F32)
    p2acc = [p2accv[:, k * QT:(k + 1) * QT] for k in range(NF)]
    assert off_[0] <= WB + W_SZ, off_[0]
    off_[0] = AB
    p2xnv = take([128, TT * NF], BF16)
    p2xn = [p2xnv[:, k * TT:(k + 1) * TT] for k in range(NF)]
    p2h = [take([128, TT], BF16, NMI) for _ in range(2)]
    p2wbc = take([128, TT], F32)
    p2tmp = take([128, TT], F32)
    p2w = take([8, QT], F32)
    p2st = take([128, 1408], F32, 4)
    assert off_[0] <= SB_SZ

    off_[0] = AB
    p3hs = take([128, ST], BF16, NSI)
    p3xnv = take([128, ST * NF], BF16)
    p3xn = [p3xnv[:, k * ST:(k + 1) * ST] for k in range(NF)]
    p3sv = take([128, ST * NF], F32)
    p3s = [p3sv[:, k * ST:(k + 1) * ST] for k in range(NF)]
    p3xv = take([128, ST * NF], F32)
    p3x = [p3xv[:, k * ST:(k + 1) * ST] for k in range(NF)]
    p3acv = take([128, ST * NF], F32)
    p3ac = [p3acv[:, k * ST:(k + 1) * ST] for k in range(NF)]
    p3gv = take([128, ST * NF], F32)
    p3g = [p3gv[:, k * ST:(k + 1) * ST] for k in range(NF)]
    p3sw = take([1, ST], F32)
    p3swb = take([128, ST], F32)
    p3st = take([128, 1408], F32, 2)
    assert off_[0] <= SB_SZ

    import contextlib
    stack = contextlib.ExitStack()
    pf = [stack.enter_context(nc.psum_tensor(f"pf{i}", [128, 512], F32)).ap()
          for i in range(6)]
    pb = [stack.enter_context(nc.psum_tensor(f"pb{i}", [128, 1024], BF16)).ap()
          for i in range(2)]
    sem_names = ["pe", "ve", "sc", "gp",
                 "dq_sync_0", "dq_sync_1", "dq_scalar_0", "dq_scalar_1",
                 "dq_gpsimd_0", "dq_gpsimd_1"]
    sems = {s: stack.enter_context(nc.semaphore(s)) for s in sem_names}

    B = Bld()
    bufs = {}

    def bf(x):
        if isinstance(x, Buf):
            return x
        if x not in bufs:
            bufs[x] = Buf(x)
        return bufs[x]

    PSB = [bf(f"ps{i}") for i in range(8)]

    FCVALS = [1e-5, 1.0, -0.5, -1.0, 0.5, 1.0 / 2048.0, 1.0 / 1024.0, 2.0]
    for fi, fv in enumerate(FCVALS):
        nc.const_aps.register(F32, fv, s_fc[:, fi:fi + 1])

    def E(eng, fn, r=(), w=(), deps=(), dma=False, tag=""):
        return B.emit(eng, fn, [bf(x) for x in r], [bf(x) for x in w], deps, dma, tag)

    def mm(out, lhsT, rhs, start, stop, rb=(), wb=(), tp=None, tag="mm"):
        def f(e, o=out, l=lhsT, rr=rhs, s=start, p=stop, t=tp):
            if t is None:
                return e.matmul(out=o, lhsT=l, rhs=rr, start=s, stop=p)
            return e.matmul(out=o, lhsT=l, rhs=rr, start=s, stop=p, tile_position=t)
        return E("tensor", f, r=rb, w=wb, tag=tag)

    def tpose(out, in_, ident, rb, wb, tag="tp"):
        def f(e, o=out, i_=in_, d=ident):
            return e.transpose(out=o, in_=i_, identity=d)
        return E("tensor", f, r=rb, w=wb, tag=tag)

    cl = [(s_lt_b, c_lt_b, "c_lt"), (s_ut_b, c_ut_b, "c_ut"), (s_id_b, c_id_b, "c_idb"),
          (s_id_f, c_id_f, "c_idf"), (s_sel, c_sel, "c_sel"), (s_o1f, c_o1f, "c_o1f"),
          (s_ocb, c_ocb, "c_ocb"), (s_oh8, c_oh8, "c_oh8"), (s_lnc, lnc, "c_lnc"),
          (s_cwc, cwc, "c_cwc"), (s_cbc, cbc, "c_cbc"), (s_b1c, b1c, "c_b1c"),
          (s_b2c, b2c, "c_b2c"), (s_nwp, nwp, "c_nwp"), (s_dpr, dpr, "c_dpr"),
          (s_dtb, dtb, "c_dtb"), (s_alg, alg, "c_alg"), (s_seg, segc, "c_seg"),
          (s_fc, c_fc, "c_fc")]
    for (dst, src, nm) in cl:
        if PREFIX and nm not in ("c_ocb", "c_fc", "c_alg", "c_dtb", "c_lt"):
            continue
        E("sync", lambda e, d=dst, s=src: e.dma_start(out=d, in_=s),
          w=[nm], dma=True, tag="constld")
    for k in range(8):
        if PREFIX:
            break
        E("sync", lambda e, d=s_gtc[k], s=gTc[k * 128:(k + 1) * 128, :]:
          e.dma_start(out=d, in_=s), w=[f"gtc{k}"], dma=True, tag="gtcld")
    E("scalar", lambda e: e.activation(out=s_acol, in_=s_alg, func=AF.Exp),
      r=["c_alg"], w=["acol"], tag="acol")
    E("vector", lambda e: e.tensor_scalar(out=s_acol, in0=s_acol, scalar1=-1.0,
                                          scalar2=None, op0=ALU.mult),
      r=["acol"], w=["acol"], tag="acol2")
    E("vector", lambda e: e.tensor_copy(out=s_segb, in_=s_seg),
      r=["c_seg"], w=["segb"], tag="segb")
    E("vector", lambda e: e.memset(HT[0][:], 0.0), w=["HT0"], tag="ht0")
    E("vector", lambda e: e.memset(HT[1][:], 0.0), w=["HT1"], tag="ht1")

    si = [0]

    def stage_conv(src, dstw, dstname, cols, eng="vector"):
        j = si[0] % 2
        sb = f"Ast{j}"
        E("sync", lambda e, d=Ast[j][:, 0:cols], s=src: e.dma_start(out=d, in_=s),
          w=[sb], dma=True, tag="wld")
        E(eng, lambda e, d=dstw, s=Ast[j][:, 0:cols]: e.tensor_copy(out=d, in_=s),
          r=[sb, "warena"], w=[dstname], tag="wcv")
        si[0] += 1

    if not PREFIX:
        for k in range(NF):
            for c4 in range(5):
                c0, c1 = c4 * 1024, min(DIP, (c4 + 1) * 1024)
                stage_conv(ipT[k * 128:(k + 1) * 128, c0:c1], ipW[k][:, c0:c1],
                           f"ipW{k}", c1 - c0)
        for k in range(16):
            stage_conv(opT[k * 128:(k + 1) * 128, :], opW[k][:, 0:1024], f"opW{k}", 1024)

    p0_jobs = []
    for k in range(NF):
        for c4 in range(4):
            p0_jobs.append((f1T[k * 128:(k + 1) * 128, c4 * 1024:(c4 + 1) * 1024],
                            d_f1[k * 128:(k + 1) * 128, c4 * 1024:(c4 + 1) * 1024],
                            1024, "w_f1"))
    for k in range(32):
        p0_jobs.append((f2T[k * 128:(k + 1) * 128, :],
                        d_f2[k * 128:(k + 1) * 128, :], 1024, "w_f2"))
    for ee in range(NE):
        for k in range(NF):
            for c4 in range(3):
                c0, c1 = c4 * 1024, min(GU2, (c4 + 1) * 1024)
                p0_jobs.append((guT[ee, k * 128:(k + 1) * 128, c0:c1],
                                d_gu[ee, k * 128:(k + 1) * 128, c0:c1],
                                c1 - c0, f"w_gu{ee}"))
        for k in range(NMI):
            p0_jobs.append((dnT[ee, k * 128:(k + 1) * 128, :],
                            d_dn[ee, k * 128:(k + 1) * 128, :], 1024, f"w_dn{ee}"))
    for (srcT, dstT, wname) in [(sgT, d_sg, "w_sg"), (suT, d_su, "w_su")]:
        for k in range(NF):
            for c4 in range(3):
                c0, c1 = c4 * 1024, min(SI, (c4 + 1) * 1024)
                p0_jobs.append((srcT[k * 128:(k + 1) * 128, c0:c1],
                                dstT[k * 128:(k + 1) * 128, c0:c1], c1 - c0, wname))
    for k in range(NSI):
        p0_jobs.append((sdT[k * 128:(k + 1) * 128, :],
                        d_sd[k * 128:(k + 1) * 128, :], 1024, "w_sd"))
    p0_i = [0]

    def p0_pump(n):
        while n > 0 and p0_i[0] < len(p0_jobs):
            batch = p0_jobs[p0_i[0]:p0_i[0] + 1]
            for j, (srcap, dst, cols, wname) in enumerate(batch):
                E("sync", lambda e, d=Ast[j][:, 0:cols], s=srcap: e.dma_start(out=d, in_=s),
                  w=[f"Ast{j}"], dma=True, tag="p0ld")
            for j, (srcap, dst, cols, wname) in enumerate(batch):
                E("gpsimd", lambda e, d=AstB[j][:, 0:cols], s=Ast[j][:, 0:cols]:
                  e.tensor_copy(out=d, in_=s), r=[f"Ast{j}"], w=[f"AstB{j}"], tag="p0cv")
            for j, (srcap, dst, cols, wname) in enumerate(batch):
                E("sync", lambda e, d=dst, s=AstB[j][:, 0:cols]: e.dma_start(out=d, in_=s),
                  r=[f"AstB{j}"], w=[wname], dma=True, tag="p0st")
            p0_i[0] += len(batch)
            n -= len(batch)

    def w_stream(srcs, cve="vector"):
        if not srcs:
            return
        depth = len(srcs[0][4])
        for b0 in range(0, len(srcs), depth):
            batch = srcs[b0:b0 + depth]
            for j, (srcap, dstap, cols, dstname, stg, spfx) in enumerate(batch):
                E("sync", lambda e, d=stg[j][:, 0:cols], s=srcap:
                  e.dma_start(out=d, in_=s), w=[f"{spfx}{j}"], dma=True, tag="wsl")
            for j, (srcap, dstap, cols, dstname, stg, spfx) in enumerate(batch):
                E(cve, lambda e, d=dstap, s=stg[j][:, 0:cols]: e.tensor_copy(out=d, in_=s),
                  r=[f"{spfx}{j}", "warena"], w=[dstname], tag="wsc")

    def ln_rows(denom, tagp):
        E("vector", lambda e, p=pf[4][0:1, 0:ST]:
          e.tensor_scalar(out=r_m, in0=p, scalar1=1.0 / denom, scalar2=None, op0=ALU.mult),
          r=[PSB[4]], w=["r_m"], tag=tagp + "m")
        E("vector", lambda e, p=pf[5][0:1, 0:ST]:
          e.tensor_scalar(out=r_ms, in0=p, scalar1=1.0 / denom, scalar2=None, op0=ALU.mult),
          r=[PSB[5]], w=["r_ms"], tag=tagp + "ms")
        E("vector", lambda e: e.tensor_tensor(out=r_t, in0=r_m, in1=r_m, op=ALU.mult),
          r=["r_m"], w=["r_t"], tag=tagp + "t")
        E("vector", lambda e: e.tensor_tensor(out=r_v, in0=r_ms, in1=r_t, op=ALU.subtract),
          r=["r_ms", "r_t"], w=["r_v"], tag=tagp + "v")
        E("scalar", lambda e: e.activation(out=r_t, in_=r_v, func=AF.Ln, bias=EPS),
          r=["r_v", "c_fc"], w=["r_t"], tag=tagp + "ln")
        E("scalar", lambda e: e.activation(out=r_r1, in_=r_t, func=AF.Exp, scale=-0.5),
          r=["r_t", "c_fc"], w=["r_r1"], tag=tagp + "ex")
        E("vector", lambda e: e.tensor_tensor(out=r_mr, in0=r_m, in1=r_r1, op=ALU.mult),
          r=["r_m", "r_r1"], w=["r_mr"], tag=tagp + "mr")

    def bcast_row(row, rowname, bank, tagp):
        return mm(pf[bank][:, 0:ST], s_o1f, row, True, True,
                  rb=["c_o1f", rowname], wb=[PSB[bank]], tag=tagp)

    def stats_pass(srcs, srcnames, tagp):
        nsrc = len(srcs)
        for k in range(nsrc):
            sc_, sq_ = xsc[k % 2], xsq[k % 2]
            E("vector", lambda e, d=sc_, s=srcs[k]: e.tensor_copy(out=d, in_=s),
              r=[srcnames[k]], w=[f"xsc{k % 2}"], tag=tagp + "c")
            mm(pf[4][0:1, 0:ST], s_ocb, sc_, k == 0, k == nsrc - 1,
               rb=["c_ocb", f"xsc{k % 2}"], wb=[PSB[4]], tag=tagp + "m")
            E("scalar", lambda e, d=sq_, s=srcs[k]: e.activation(out=d, in_=s, func=AF.Square),
              r=[srcnames[k]], w=[f"xsq{k % 2}"], tag=tagp + "q")
            mm(pf[5][0:1, 0:ST], s_ocb, sq_, k == 0, k == nsrc - 1,
               rb=["c_ocb", f"xsq{k % 2}"], wb=[PSB[5]], tag=tagp + "qm")

    def ssd_chunk(st, qq, seq):
        qs = slice(qq * Q, (qq + 1) * Q)
        for j in range(NPAIR):
            bk = 6 + (j % 2)
            sl = slice(((j // 2) % 4) * 128, ((j // 2) % 4) * 128 + 128)
            tpose(pb[j % 2][:, sl], xst[j][:, qs], s_id_b,
                  rb=[f"xst{j}", "c_idb"], wb=[PSB[bk]], tag="xsT")
            if j % 4 < 2:
                E("vector", lambda e, d=xsT[j], s=pb[j % 2][:, sl]:
                  e.tensor_copy(out=d, in_=s), r=[PSB[bk]], w=[f"xsT{j}"], tag="xsTe")
            else:
                E("scalar", lambda e, d=xsT[j], s=pb[j % 2][:, sl]:
                  e.activation(out=d, in_=s, func=AF.Copy),
                  r=[PSB[bk]], w=[f"xsT{j}"], tag="xsTe")
        tpose(pb[0][:, 512:576], bct[0:64, qs], s_id_b[0:64, 0:64],
              rb=["bct", "c_idb"], wb=[PSB[6]], tag="BT")
        E("vector", lambda e, s=pb[0][:, 512:576]: e.tensor_copy(out=sBT, in_=s),
          r=[PSB[6]], w=["sBT"], tag="BTe")
        mm(pf[5][0:128, 0:128], bct[0:64, qs], sCf[:, qs], True, True,
           rb=["bct", "sCf"], wb=[PSB[5]], tag="G")
        E("vector", lambda e, p=pf[5][0:128, 0:128]:
          e.tensor_tensor(out=sGm, in0=p, in1=s_lt_b, op=ALU.mult),
          r=[PSB[5], "c_lt"], w=["sGm"], tag="Gm")
        tpose(pf[5][:, 128:160], dtf[:, qs], s_id_f[0:32, 0:32],
              rb=["dtf", "c_idf"], wb=[PSB[5]], tag="dtT")
        E("vector", lambda e, s=pf[5][:, 128:160]: e.tensor_copy(out=sdtT, in_=s),
          r=[PSB[5]], w=["sdtT"], tag="dtTe")
        E("scalar", lambda e, s=pf[5][:, 128:160]: e.activation(out=slnd, in_=s, func=AF.Ln),
          r=[PSB[5]], w=["slnd"], tag="lnd")
        tpose(pf[5][:, 160:192], afm[:, qs], s_id_f[0:32, 0:32],
              rb=["dtraw", "c_idf"], wb=[PSB[5]], tag="aT")
        E("vector", lambda e, s=pf[5][:, 160:192]: e.tensor_copy(out=saT, in_=s),
          r=[PSB[5]], w=["saT"], tag="aTe")
        E("vector", lambda e: e.tensor_copy(out=saTb, in_=saT),
          r=["saT"], w=["saTb"], tag="aTb")
        mm(pf[5][0:32, 192:320], saTb, s_lt_b, True, True,
           rb=["saTb", "c_lt"], wb=[PSB[5]], tag="sfm")
        E("scalar", lambda e, p=pf[5][0:32, 192:320]:
          e.activation(out=sEf, in_=p, func=AF.Exp), r=[PSB[5]], w=["sEf"], tag="Ef")
        mm(pf[5][:, 320:352], s_ut_b, saTb, True, True,
           rb=["c_ut", "saTb"], wb=[PSB[5]], tag="wtm")
        E("scalar", lambda e, p=pf[5][:, 320:352]:
          e.activation(out=swT, in_=p, func=AF.Exp), r=[PSB[5]], w=["swT"], tag="wte")
        E("vector", lambda e: e.tensor_tensor(out=swd, in0=swT, in1=sdtT, op=ALU.mult),
          r=["swT", "sdtT"], w=["swd"], tag="wtd")
        E("scalar", lambda e: e.activation(out=sLr, in_=saT[0:1, :], func=AF.Exp),
          r=["saT"], w=["sLr"], tag="La")
        E("vector", lambda e: e.tensor_tensor(out=sLr2, in0=sLr, in1=swT[0:1, :], op=ALU.mult),
          r=["sLr", "swT"], w=["sLr2"], tag="Lb")
        mm(pf[5][0:64, 352:384], s_o1f[:, 0:64], sLr2, True, True,
           rb=["c_o1f", "sLr2"], wb=[PSB[5]], tag="Lm")
        E("vector", lambda e, p=pf[5][0:64, 352:384]: e.tensor_copy(out=sLb, in_=p),
          r=[PSB[5]], w=["sLb"], tag="Le")
        for w in range(8):
            hs4 = [4 * w + i for i in range(4)]
            prs = [2 * w, 2 * w + 1]
            for h in hs4:
                E("vector", lambda e, d=sRL[h % 4], sc1=saT[:, h:h + 1]:
                  e.tensor_scalar(out=d, in0=s_lt_b, scalar1=sc1, scalar2=None,
                                  op0=ALU.mult),
                  r=["c_lt", "saT"], w=[f"sRL{h % 4}"], tag="rl")
            for h in hs4:
                sl = slice((h % 4 // 2) * 128, (h % 4 // 2) * 128 + 128)
                mm(pf[h % 2][:, sl], s_ut_b, sRL[h % 4], True, True,
                   rb=["c_ut", f"sRL{h % 4}"], wb=[PSB[h % 2]], tag="dif")
            for h in hs4:
                sl = slice((h % 4 // 2) * 128, (h % 4 // 2) * 128 + 128)
                E("scalar", lambda e, d=sL2[h % 4], pp=pf[h % 2][:, sl],
                  b=slnd[:, h:h + 1]:
                  e.activation(out=d, in_=pp, func=AF.Exp, bias=b),
                  r=[PSB[h % 2], "slnd"], w=[f"sL2{h % 4}"], tag="l2")
            for h in hs4:
                E("gpsimd", lambda e, d=sM2[h % 4], s=sL2[h % 4]:
                  e.tensor_tensor(out=d, in0=s, in1=sGm, op=ALU.mult),
                  r=[f"sL2{h % 4}", "sGm"], w=[f"sM2{h % 4}"], tag="m2")
            for p in prs:
                bI = 4 + p % 2
                mm(pf[bI][:, 128:256], s_sel[:, p * 128:(p + 1) * 128], sEf, True, True,
                   rb=["c_sel", "sEf"], wb=[PSB[bI]], tag="Ebm")
                mm(pf[bI][:, 0:128], HT[seq][:, p * 128:(p + 1) * 128], sCf[:, qs],
                   True, True, rb=[f"HT{seq}", "sCf"], wb=[PSB[bI]], tag="Yi")
                E("scalar", lambda e, d=sEb[p % 2], pp=pf[bI][:, 128:256]:
                  e.activation(out=d, in_=pp, func=AF.Copy),
                  r=[PSB[bI]], w=[f"sEb{p % 2}"], tag="Ebe")
                E("vector", lambda e, d=sT1[p % 2], pp=pf[bI][:, 0:128], eb=sEb[p % 2]:
                  e.tensor_tensor(out=d, in0=pp, in1=eb, op=ALU.mult),
                  r=[PSB[bI], f"sEb{p % 2}"], w=[f"sT1{p % 2}"], tag="t1")
            for h in hs4:
                p = h // 2
                hi = h % 2
                bP = 2 + p % 2
                mm(pf[bP][hi * 64:(hi + 1) * 64, 0:128],
                   xsT[p][:, hi * 64:(hi + 1) * 64], sM2[h % 4], True, True,
                   rb=[f"xsT{p}", f"sM2{h % 4}"], wb=[PSB[bP]],
                   tp=(0, hi * 64), tag="Ym")
            for h in hs4:
                p = h // 2
                hi = h % 2
                bP = 2 + p % 2
                E("vector", lambda e, d=sBh[h % 2], sc1=swd[:, h:h + 1]:
                  e.tensor_scalar(out=d, in0=sBT, scalar1=sc1, scalar2=None,
                                  op0=ALU.mult),
                  r=["sBT", "swd"], w=[f"sBh{h % 2}"], tag="bh")
                mm(pf[bP][0:64, 128 + hi * 64:128 + (hi + 1) * 64], sBh[h % 2],
                   xsT[p][:, hi * 64:(hi + 1) * 64], True, True,
                   rb=[f"sBh{h % 2}", f"xsT{p}"], wb=[PSB[bP]], tag="dH")
                E("vector", lambda e, d=HT[seq][:, h * 64:(h + 1) * 64],
                  sc1=sLb[:, h:h + 1],
                  pp=pf[bP][0:64, 128 + hi * 64:128 + (hi + 1) * 64]:
                  e.scalar_tensor_tensor(out=d, in0=d, scalar=sc1, in1=pp,
                                         op0=ALU.mult, op1=ALU.add),
                  r=[f"HT{seq}", "sLb", PSB[bP]], w=[f"HT{seq}"], tag="Hup")
            for p in prs:
                bP = 2 + p % 2
                E("vector", lambda e, d=sE2[p % 2], t1=sT1[p % 2], pp=pf[bP][:, 0:128]:
                  e.tensor_tensor(out=d, in0=t1, in1=pp, op=ALU.add),
                  r=[f"sT1{p % 2}", PSB[bP]], w=[f"sE2{p % 2}"], tag="e2")
                E("vector", lambda e, d=xst[p][:, qs], sc1=s_dpr[:, p:p + 1],
                  e2=sE2[p % 2]:
                  e.scalar_tensor_tensor(out=d, in0=d, scalar=sc1, in1=e2,
                                         op0=ALU.mult, op1=ALU.add),
                  r=[f"xst{p}", "c_dpr", f"sE2{p % 2}"], w=[f"xst{p}"], tag="e3")

    def emit_xld(st):
        par = st % 2
        cols = slice(st * ST, (st + 1) * ST)
        E("sync", lambda e, d=xavP[par].rearrange("p (k t) -> p k t", k=NF),
          s=xfm[:, cols].rearrange("(k p) t -> p k t", p=128):
          e.dma_start(out=d, in_=s), r=["aarena"],
          w=[f"xa{par}_{k}" for k in range(NF)], dma=True, tag="xld")

    def p1a_supertile(st):
        par = st % 2
        xa = xaP[par]
        cols = slice(st * ST, (st + 1) * ST)
        seq = st // 8
        if st == 0:
            emit_xld(0)
        if st + 1 < NST:
            emit_xld(st + 1)
        stats_pass(xa, [f"xa{par}_{k}" for k in range(NF)], "s1")
        ln_rows(DM, "r1")
        if DEBUG and st == 0:
            for di, (rr, rn) in enumerate([(r_m, "r_m"), (r_ms, "r_ms"), (r_v, "r_v"),
                                           (r_r1, "r_r1"), (r_mr, "r_mr"), (r_t, "r_t")]):
                E("gpsimd", lambda e, d=d_dbg[di:di + 1, :], s=rr:
                  e.dma_start(out=d, in_=s), r=[rn], w=["d_dbg"], dma=True, tag="dbg")
            E("vector", lambda e: e.tensor_copy(out=xtmp[0], in_=xsq[1]),
              r=["xsq1"], w=["xtmp0"], tag="dbgc")
            E("gpsimd", lambda e: e.dma_start(out=d_dbg2, in_=xtmp[0]),
              r=["xtmp0"], w=["d_dbg2"], dma=True, tag="dbg2")
            E("vector", lambda e: e.tensor_copy(out=xtmp[1], in_=xsq[0]),
              r=["xsq0"], w=["xtmp1"], tag="dbgc2")
            E("gpsimd", lambda e: e.dma_start(out=d_dbg3, in_=xtmp[1]),
              r=["xtmp1"], w=["d_dbg3"], dma=True, tag="dbg3")
        if PREFIX:
            E("gpsimd", lambda e: e.dma_start(out=d_dbg2, in_=xa[7]),
              r=["xa7"], w=["d_dbg2"], dma=True, tag="dbgxa")
            E("vector", lambda e: e.tensor_copy(out=xtmp[0], in_=xsq[1]),
              r=["xsq1"], w=["xtmp0"], tag="dbq")
            E("gpsimd", lambda e: e.dma_start(out=d_dbg3, in_=xtmp[0]),
              r=["xtmp0"], w=["d_dbg3"], dma=True, tag="dbq2")
            return
        bcast_row(r_r1, "r_r1", 4, "bc_r1")
        bcast_row(r_mr, "r_mr", 5, "bc_mr1")
        for k in range(NF):
            E("vector", lambda e, d=xa[k], p=pf[4][:, 0:ST]:
              e.tensor_tensor(out=d, in0=d, in1=p, op=ALU.mult),
              r=[f"xa{par}_{k}", PSB[4]], w=[f"xa{par}_{k}"], tag="xn1")
            E("vector", lambda e, d=xa[k], p=pf[5][:, 0:ST]:
              e.tensor_tensor(out=d, in0=d, in1=p, op=ALU.subtract),
              r=[f"xa{par}_{k}", PSB[5]], w=[f"xa{par}_{k}"], tag="xn2")
        for k in range(NF):
            mm(pf[3][0:8, 0:ST], s_gtc[k], xa[k], k == 0, k == NF - 1,
               rb=[f"gtc{k}", f"xa{par}_{k}"], wb=[PSB[3]], tag="gl")
        E("vector", lambda e, p=pf[3][0:8, 0:ST]: e.tensor_copy(out=g_lf, in_=p),
          r=[PSB[3]], w=["g_lf"], tag="glev")
        for k in range(NF):
            E("vector", lambda e, d=xns[:, k * ST:(k + 1) * ST], s=xa[k]:
              e.tensor_copy(out=d, in_=s), r=[f"xa{par}_{k}"], w=[f"xns{k}"], tag="xnb")
        E("gpsimd", lambda e, d=d_xn[:, cols].rearrange("(k p) t -> p k t", p=128),
          s=xns.rearrange("p (k t) -> p k t", k=NF): e.dma_start(out=d, in_=s),
          r=[f"xns{k}" for k in range(NF)],
          w=[f"dxn{st}_{k}" for k in range(NF)], dma=True, tag="xnsp")
        E("vector", lambda e: e.tensor_tensor(out=r_t, in0=r_r1, in1=r_r1, op=ALU.mult),
          r=["r_r1"], w=["r_t"], tag="r2a")
        E("vector", lambda e: e.tensor_tensor(out=r_v, in0=r_v, in1=r_t, op=ALU.mult),
          r=["r_v", "r_t"], w=["r_v"], tag="r2b")
        E("scalar", lambda e: e.activation(out=r_t, in_=r_v, func=AF.Ln, bias=EPS),
          r=["r_v", "c_fc"], w=["r_t"], tag="r2c")
        E("scalar", lambda e: e.activation(out=r_r2, in_=r_t, func=AF.Exp, scale=-0.5),
          r=["r_t", "c_fc"], w=["r_r1"], tag="r2d")
        bcast_row(r_r2, "r_r1", 4, "bc_r2")
        for k in range(NF):
            E("vector", lambda e, d=xa[k], p=pf[4][:, 0:ST]:
              e.tensor_tensor(out=d, in0=d, in1=p, op=ALU.mult),
              r=[f"xa{par}_{k}", PSB[4]], w=[f"xa{par}_{k}"], tag="l0a")
            E("vector", lambda e, d=xa[k], w1=s_lnc[:, k:k + 1], bb=s_lnc[:, 8 + k:9 + k]:
              e.tensor_scalar(out=d, in0=d, scalar1=w1, scalar2=bb,
                              op0=ALU.mult, op1=ALU.add),
              r=[f"xa{par}_{k}", "c_lnc"], w=[f"xa{par}_{k}"], tag="l0b")
        for h2 in range(2):
            sl = slice(h2 * 128, (h2 + 1) * 128)
            tpose(pf[5][:, 384:392], g_lf[:, sl], s_id_f[0:8, 0:8],
                  rb=["g_lf", "c_idf"], wb=[PSB[5]], tag="glt")
            E("vector", lambda e, d=g_lt[h2], p=pf[5][:, 384:392]:
              e.tensor_copy(out=d, in_=p), r=[PSB[5]], w=[f"glt{h2}"], tag="glte")
        for h2 in range(2):
            lt = g_lt[h2]
            lb = f"glt{h2}"
            E("vector", lambda e, s=lt: e.tensor_reduce(out=g_m1, in_=s, axis=AX.X,
                                                        op=ALU.max),
              r=[lb], w=["gm1"], tag="gm1")
            E("vector", lambda e, s=lt: e.tensor_scalar(out=g_d, in0=s, scalar1=g_m1,
                                                        scalar2=None, op0=ALU.subtract),
              r=[lb, "gm1"], w=["gd"], tag="gd")
            E("vector", lambda e: e.tensor_scalar(out=g_sp, in0=g_d, scalar1=0.0,
                                                  scalar2=-1e30, op0=ALU.is_ge,
                                                  op1=ALU.mult),
              r=["gd"], w=["gsp"], tag="gsp")
            E("vector", lambda e, s=lt: e.tensor_tensor(out=g_sp, in0=g_sp, in1=s,
                                                        op=ALU.add),
              r=["gsp", lb], w=["gsp"], tag="gsp2")
            E("vector", lambda e: e.tensor_reduce(out=g_m1, in_=g_sp, axis=AX.X,
                                                  op=ALU.max),
              r=["gsp"], w=["gm1"], tag="gm2")
            E("vector", lambda e, s=lt: e.tensor_scalar(out=g_sp, in0=s, scalar1=g_m1,
                                                        scalar2=None, op0=ALU.subtract),
              r=[lb, "gm1"], w=["gsp"], tag="gd2")
            E("vector", lambda e: e.tensor_scalar(out=g_sp, in0=g_sp, scalar1=0.0,
                                                  scalar2=None, op0=ALU.is_ge),
              r=["gsp"], w=["gsp"], tag="gmask")
            E("scalar", lambda e: e.activation(out=g_ex, in_=g_d, func=AF.Exp),
              r=["gd"], w=["gex"], tag="gex")
            E("vector", lambda e: e.tensor_reduce(out=g_ss, in_=g_ex, axis=AX.X,
                                                  op=ALU.add),
              r=["gex"], w=["gss"], tag="gss")
            E("vector", lambda e: e.reciprocal(out=g_ss, in_=g_ss),
              r=["gss"], w=["gss"], tag="grc")
            E("vector", lambda e: e.tensor_scalar(out=g_ex, in0=g_ex, scalar1=g_ss,
                                                  scalar2=None, op0=ALU.mult),
              r=["gex", "gss"], w=["gex"], tag="gsc")
            E("vector", lambda e, d=g_w[h2]: e.tensor_tensor(out=d, in0=g_ex, in1=g_sp,
                                                             op=ALU.mult),
              r=["gex", "gsp"], w=[f"gw{h2}"], tag="gw")
            tpose(pf[5][0:8, 384:512], g_w[h2], s_id_f,
                  rb=[f"gw{h2}", "c_idf"], wb=[PSB[5]], tag="gwt")
            E("vector", lambda e, d=g_wf[:, h2 * 128:(h2 + 1) * 128], p=pf[5][0:8, 384:512]:
              e.tensor_copy(out=d, in_=p), r=[PSB[5]], w=["g_lf"], tag="gwfe")
        E("gpsimd", lambda e, d=d_w[:, cols], s=g_wf: e.dma_start(out=d, in_=s),
          r=["g_lf"], w=[f"dw{st}"], dma=True, tag="wsp")
        stats_pass(xa, [f"xa{par}_{k}" for k in range(NF)], "s3")
        ln_rows(DM, "r3")
        bcast_row(r_r1, "r_r1", 4, "bc_r3")
        bcast_row(r_mr, "r_mr", 5, "bc_mr3")
        for k in range(NF):
            tm = xtmp[k % 2]
            E("vector", lambda e, d=tm, s=xa[k], p=pf[4][:, 0:ST]:
              e.tensor_tensor(out=d, in0=s, in1=p, op=ALU.mult),
              r=[f"xa{par}_{k}", PSB[4]], w=[f"xtmp{k % 2}"], tag="l1a")
            E("vector", lambda e, d=tm, p=pf[5][:, 0:ST]:
              e.tensor_tensor(out=d, in0=d, in1=p, op=ALU.subtract),
              r=[f"xtmp{k % 2}", PSB[5]], w=[f"xtmp{k % 2}"], tag="l1b")
            E("vector", lambda e, d=xmb[k], s=tm, w1=s_lnc[:, 16 + k:17 + k],
              bb=s_lnc[:, 24 + k:25 + k]:
              e.tensor_scalar(out=d, in0=s, scalar1=w1, scalar2=bb,
                              op0=ALU.mult, op1=ALU.add),
              r=[f"xtmp{k % 2}", "c_lnc"], w=[f"xmb{k}"], tag="l1c")
        for o in range(34):
            o0 = o * 128
            o1 = min(DIP, o0 + 128)
            ow = o1 - o0
            bank = o % 3
            for k in range(NF):
                mm(pf[bank][0:ow, 0:ST], ipW[k][:, o0:o1], xmb[k], k == 0, k == NF - 1,
                   rb=[f"ipW{k}", f"xmb{k}", "warena"], wb=[PSB[bank]], tag="ipm")
            if o < 16:
                E("scalar", lambda e, d=zt[o], p=pf[bank][0:128, 0:ST]:
                  e.activation(out=d, in_=p, func=AF.Silu),
                  r=[PSB[bank]], w=[f"zt{o}"], tag="zev")
            elif o < 33:
                j = o - 16
                E("scalar", lambda e, d=xBCr[j][:, 3:3 + ST], p=pf[bank][0:128, 0:ST]:
                  e.activation(out=d, in_=p, func=AF.Copy),
                  r=[PSB[bank]], w=[f"xBC{j}"], tag="xbcev")
            else:
                E("scalar", lambda e, p=pf[bank][0:32, 0:ST]:
                  e.activation(out=dtraw, in_=p, func=AF.Copy),
                  r=[PSB[bank]], w=["dtraw"], tag="dtev")
        if st == 0 or st == 8:
            for j in range(NCV):
                E("vector", lambda e, d=xBCr[j][:, 0:3]: e.memset(d, 0.0),
                  w=[f"xBC{j}"], tag="cvz")
        for j in range(NCV):
            dst = xst[j] if j < 16 else bct
            dn = f"xst{j}" if j < 16 else "bct"
            cs = xtmp[j % 2]
            E("vector", lambda e, d=cs, s=xBCr[j][:, 0:ST], w1=s_cwc[:, 4 * j:4 * j + 1]:
              e.tensor_scalar(out=d, in0=s, scalar1=w1, scalar2=None, op0=ALU.mult),
              r=[f"xBC{j}", "c_cwc"], w=[f"xtmp{j % 2}"], tag="cv0")
            for tap in (1, 2, 3):
                E("vector", lambda e, d=cs, s=xBCr[j][:, tap:tap + ST],
                  w1=s_cwc[:, 4 * j + tap:4 * j + tap + 1]:
                  e.scalar_tensor_tensor(out=d, in0=s, scalar=w1, in1=d,
                                         op0=ALU.mult, op1=ALU.add),
                  r=[f"xBC{j}", f"xtmp{j % 2}", "c_cwc"], w=[f"xtmp{j % 2}"], tag="cvt")
            E("scalar", lambda e, d=dst, s=cs, bb=s_cbc[:, j:j + 1]:
              e.activation(out=d, in_=s, func=AF.Silu, bias=bb),
              r=[f"xtmp{j % 2}", "c_cbc"], w=[dn], tag="cvs")
            E("gpsimd", lambda e, d=xBCr[j][:, 0:3], s=xBCr[j][:, ST:ST + 3]:
              e.tensor_copy(out=d, in_=s), r=[f"xBC{j}"], w=[f"xBC{j}"], tag="cvh")
        E("sync", lambda e: e.dma_start(out=sCf, in_=bct[64:128, :]),
          r=["bct"], w=["sCf"], dma=True, tag="crel")
        E("scalar", lambda e: e.activation(out=dtt, in_=dtraw, func=AF.Abs, bias=s_dtb),
          r=["dtraw", "c_dtb"], w=["dtt"], tag="dt1")
        E("scalar", lambda e: e.activation(out=dtt, in_=dtt, func=AF.Exp, scale=-1.0),
          r=["dtt", "c_fc"], w=["dtt"], tag="dt2")
        E("scalar", lambda e: e.activation(out=dtt, in_=dtt, func=AF.Ln, bias=1.0),
          r=["dtt", "c_fc"], w=["dtt"], tag="dt3")
        E("scalar", lambda e: e.activation(out=dtf, in_=dtraw, func=AF.Relu, bias=s_dtb),
          r=["dtraw", "c_dtb"], w=["dtf"], tag="dt4")
        E("vector", lambda e: e.tensor_tensor(out=dtf, in0=dtf, in1=dtt, op=ALU.add),
          r=["dtf", "dtt"], w=["dtf"], tag="dt5")
        E("vector", lambda e: e.tensor_scalar(out=afm, in0=dtf, scalar1=s_acol,
                                              scalar2=None, op0=ALU.mult),
          r=["dtf", "acol"], w=["dtraw"], tag="dta")
        for qq in range(2):
            ssd_chunk(st, qq, seq)
        for j in range(NPAIR):
            E("vector", lambda e, d=xst[j], s=zt[j]:
              e.tensor_tensor(out=d, in0=d, in1=s, op=ALU.mult),
              r=[f"xst{j}", f"zt{j}"], w=[f"xst{j}"], tag="gat")
        for j in range(NPAIR):
            sq_ = xsq[j % 2]
            E("scalar", lambda e, d=sq_, s=xst[j]: e.activation(out=d, in_=s,
                                                                func=AF.Square),
              r=[f"xst{j}"], w=[f"xsq{j % 2}"], tag="rmq")
            mm(pf[4][0:1, 0:ST], s_ocb, sq_, j == 0, j == NPAIR - 1,
               rb=["c_ocb", f"xsq{j % 2}"], wb=[PSB[4]], tag="rmm")
        E("scalar", lambda e, p=pf[4][0:1, 0:ST]:
          e.activation(out=r_t, in_=p, func=AF.Ln, scale=1.0 / DI, bias=EPS),
          r=[PSB[4], "c_fc"], w=["r_t"], tag="rml")
        E("scalar", lambda e: e.activation(out=r_rr, in_=r_t, func=AF.Exp, scale=-0.5),
          r=["r_t", "c_fc"], w=["r_r1"], tag="rme")
        bcast_row(r_rr, "r_r1", 5, "bc_rms")
        for j in range(NPAIR):
            E("vector", lambda e, d=zt[j], s=xst[j], w1=s_nwp[:, j:j + 1], p=pf[5][:, 0:ST]:
              e.scalar_tensor_tensor(out=d, in0=s, scalar=w1, in1=p,
                                     op0=ALU.mult, op1=ALU.mult),
              r=[f"xst{j}", "c_nwp", PSB[5]], w=[f"zt{j}"], tag="rma")
        for o in range(NF):
            bank = o % 3
            for k in range(16):
                mm(pf[bank][:, 0:ST], opW[k][:, o * 128:(o + 1) * 128], zt[k],
                   k == 0, k == 15, rb=[f"opW{k}", f"zt{k}", "warena"],
                   wb=[PSB[bank]], tag="opm")
            E("vector", lambda e, d=xa[o], p=pf[bank][:, 0:ST]:
              e.tensor_tensor(out=d, in0=d, in1=p, op=ALU.add),
              r=[f"xa{par}_{o}", PSB[bank]], w=[f"xa{par}_{o}"], tag="opev")
        E("gpsimd", lambda e, d=d_xb[:, cols].rearrange("(k p) t -> p k t", p=128),
          s=xavP[par].rearrange("p (k t) -> p k t", k=NF): e.dma_start(out=d, in_=s),
          r=[f"xa{par}_{k}" for k in range(NF)],
          w=[f"dxb{st}_{k}" for k in range(NF)], dma=True, tag="xbsp")
        stats_pass(xa, [f"xa{par}_{k}" for k in range(NF)], "s4")
        ln_rows(DM, "r4")
        bcast_row(r_r1, "r_r1", 4, "bc_r4")
        bcast_row(r_mr, "r_mr", 5, "bc_mr4")
        for k in range(NF):
            tm = xtmp[k % 2]
            E("vector", lambda e, d=tm, s=xa[k], p=pf[4][:, 0:ST]:
              e.tensor_tensor(out=d, in0=s, in1=p, op=ALU.mult),
              r=[f"xa{par}_{k}", PSB[4]], w=[f"xtmp{k % 2}"], tag="l2a")
            E("vector", lambda e, d=tm, p=pf[5][:, 0:ST]:
              e.tensor_tensor(out=d, in0=d, in1=p, op=ALU.subtract),
              r=[f"xtmp{k % 2}", PSB[5]], w=[f"xtmp{k % 2}"], tag="l2b")
            E("vector", lambda e, d=xmb[k], s=tm, w1=s_lnc[:, 32 + k:33 + k],
              bb=s_lnc[:, 40 + k:41 + k]:
              e.tensor_scalar(out=d, in0=s, scalar1=w1, scalar2=bb,
                              op0=ALU.mult, op1=ALU.add),
              r=[f"xtmp{k % 2}", "c_lnc"], w=[f"xmb{k}"], tag="l2c")
        E("gpsimd", lambda e, d=d_xc[:, cols].rearrange("(k p) t -> p k t", p=128),
          s=xmbv.rearrange("p (k t) -> p k t", k=NF): e.dma_start(out=d, in_=s),
          r=[f"xmb{k}" for k in range(NF)],
          w=[f"dxc{st}_{k}" for k in range(NF)], dma=True, tag="xcsp")

    def p1b():
        STB = 512
        E("vector", lambda e: e.engine_nop(), w=["warena"], deps=B.all_tokens(), tag="wgate1b")
        E("vector", lambda e: e.engine_nop(), w=["aarena"], deps=B.all_tokens(), tag="agate1b")
        E("sync", lambda e, d=f1Wm.rearrange("p (k c) -> p k c", k=NF),
          s=d_f1.rearrange("(k p) c -> p k c", p=128): e.dma_start(out=d, in_=s),
          r=["warena", "w_f1"], w=[f"f1W{k}" for k in range(NF)], dma=True, tag="f1ld")
        E("sync", lambda e, d=f2Wm.rearrange("p (k c) -> p k c", k=32),
          s=d_f2.rearrange("(k p) c -> p k c", p=128): e.dma_start(out=d, in_=s),
          r=["warena", "w_f2"], w=[f"f2W{k}" for k in range(32)], dma=True, tag="f2ld")
        for st in range(8):
            cols = slice(st * STB, (st + 1) * STB)
            sts = (2 * st, 2 * st + 1)
            E("sync", lambda e, d=xcbv.rearrange("p (k t) -> p k t", k=NF),
              s=d_xc[:, cols].rearrange("(k p) t -> p k t", p=128):
              e.dma_start(out=d, in_=s),
              r=["aarena"] + [f"dxc{s_}_{k}" for s_ in sts for k in range(NF)],
              w=[f"xcb{k}" for k in range(NF)], dma=True, tag="xcld")
            E("sync", lambda e, d=xbfv.rearrange("p (k t) -> p k t", k=NF),
              s=d_xb[:, cols].rearrange("(k p) t -> p k t", p=128):
              e.dma_start(out=d, in_=s),
              r=["aarena"] + [f"dxb{s_}_{k}" for s_ in sts for k in range(NF)],
              w=[f"xbf{k}" for k in range(NF)], dma=True, tag="xbld")
            for o in range(32):
                bank = o % 3
                for k in range(NF):
                    mm(pf[bank][:, 0:STB], f1W[k][:, o * 128:(o + 1) * 128], xcb[k],
                       k == 0, k == NF - 1,
                       rb=[f"f1W{k}", f"xcb{k}", "warena"], wb=[PSB[bank]], tag="f1m")
                E("scalar", lambda e, d=g1b[o], p=pf[bank][:, 0:STB],
                  bb=s_b1c[:, o:o + 1]:
                  e.activation(out=d, in_=p, func=AF.Gelu, bias=bb),
                  r=[PSB[bank], "c_b1c"], w=[f"g1b{o}"], tag="gelu")
            for o in range(NF):
                bank = o % 3
                for k in range(32):
                    mm(pf[bank][:, 0:STB], f2W[k][:, o * 128:(o + 1) * 128], g1b[k],
                       k == 0, k == 31,
                       rb=[f"f2W{k}", f"g1b{k}", "warena"], wb=[PSB[bank]], tag="f2m")
                E("vector", lambda e, d=xbf[o], p=pf[bank][:, 0:STB],
                  bb=s_b2c[:, o:o + 1]:
                  e.scalar_tensor_tensor(out=d, in0=p, scalar=bb, in1=d,
                                         op0=ALU.add, op1=ALU.add),
                  r=[PSB[bank], "c_b2c", f"xbf{o}"], w=[f"xbf{o}"], tag="f2ev")
            E("gpsimd", lambda e, d=d_s[:, cols].rearrange("(k p) t -> p k t", p=128),
              s=xbfv.rearrange("p (k t) -> p k t", k=NF): e.dma_start(out=d, in_=s),
              r=[f"xbf{k}" for k in range(NF)],
              w=[f"ds{s_}_{k}" for s_ in sts for k in range(NF)], dma=True, tag="ssp")

    def p2():
        E("vector", lambda e: e.engine_nop(), w=["warena"], deps=B.all_tokens(), tag="wgate2")
        E("vector", lambda e: e.engine_nop(), w=["aarena"], deps=B.all_tokens(), tag="agate2")

        def load_gu(ee):
            E("sync", lambda e, d=guWm.rearrange("p (k c) -> p k c", k=NF),
              s=d_gu[ee].rearrange("(k p) c -> p k c", p=128):
              e.dma_start(out=d, in_=s), r=["warena", f"w_gu{ee}"],
              w=[f"guW{k}" for k in range(NF)], dma=True, tag="guld")

        def load_dn(ee, pp):
            E("sync", lambda e, d=dnWm[pp].rearrange("p (k c) -> p k c", k=NMI),
              s=d_dn[ee].rearrange("(k p) c -> p k c", p=128):
              e.dma_start(out=d, in_=s), r=["warena", f"w_dn{ee}"],
              w=[f"dnW{pp}_{k}" for k in range(NMI)], dma=True, tag="dnld")

        for qt in range(NQ):
            qcols = slice(qt * QT, (qt + 1) * QT)
            wsts = [qt * 4 + i for i in range(4)]
            E("sync", lambda e, d=p2w, s=d_w[:, qcols]: e.dma_start(out=d, in_=s),
              r=["aarena"] + [f"dw{s_}" for s_ in wsts], w=["p2w"], dma=True, tag="p2wld")
            for ee in range(NE):
                if qt == 0 and ee == 0:
                    load_gu(0)
                    load_dn(0, 0)
                for tt2 in range(2):
                    tcols = slice(qt * QT + tt2 * TT, qt * QT + (tt2 + 1) * TT)
                    xsts = [(qt * QT + tt2 * TT) // ST, (qt * QT + tt2 * TT) // ST + 1]
                    E("sync", lambda e, d=p2xnv.rearrange("p (k t) -> p k t", k=NF),
                      s=d_xn[:, tcols].rearrange("(k p) t -> p k t", p=128):
                      e.dma_start(out=d, in_=s),
                      r=["aarena"] + [f"dxn{s_}_{k}" for s_ in xsts for k in range(NF)],
                      w=[f"p2xn{k}" for k in range(NF)], dma=True, tag="p2xld")
                    for j in range(NMI):
                        for k in range(NF):
                            mm(pf[0][:, 0:TT], guW[k][:, j * 128:(j + 1) * 128], p2xn[k],
                               k == 0, k == NF - 1,
                               rb=[f"guW{k}", f"p2xn{k}", "warena"], wb=[PSB[0]],
                               tag="gum")
                        for k in range(NF):
                            mm(pf[1][:, 0:TT], guW[k][:, MI + j * 128:MI + (j + 1) * 128],
                               p2xn[k], k == 0, k == NF - 1,
                               rb=[f"guW{k}", f"p2xn{k}", "warena"], wb=[PSB[1]],
                               tag="guu")
                        E("scalar", lambda e, d=p2h[tt2][j], p=pf[0][:, 0:TT]:
                          e.activation(out=d, in_=p, func=AF.Silu),
                          r=[PSB[0]], w=[f"p2h{tt2}_{j}"], tag="hsl")
                        E("vector", lambda e, d=p2h[tt2][j], p=pf[1][:, 0:TT]:
                          e.tensor_tensor(out=d, in0=d, in1=p, op=ALU.mult),
                          r=[f"p2h{tt2}_{j}", PSB[1]], w=[f"p2h{tt2}_{j}"], tag="hml")
                ne_, nq_ = (ee + 1, qt) if ee < NE - 1 else (0, qt + 1)
                if nq_ < NQ:
                    load_gu(ne_)
                    load_dn(ne_, ne_ % 2)
                pp_ = ee % 2
                for tt2 in range(2):
                    mm(pf[4][:, 0:TT], s_oh8[:, ee * 128:(ee + 1) * 128],
                       p2w[:, tt2 * TT:(tt2 + 1) * TT], True, True,
                       rb=["c_oh8", "p2w"], wb=[PSB[4]], tag="wbm")
                    E("vector", lambda e, p=pf[4][:, 0:TT]: e.tensor_copy(out=p2wbc, in_=p),
                      r=[PSB[4]], w=["p2wbc"], tag="wbe")
                    for o in range(NF):
                        bank = 2 + o % 2
                        for k in range(NMI):
                            mm(pf[bank][:, 0:TT], dnW[pp_][k][:, o * 128:(o + 1) * 128],
                               p2h[tt2][k], k == 0, k == NMI - 1,
                               rb=[f"dnW{pp_}_{k}", f"p2h{tt2}_{k}", "warena"],
                               wb=[PSB[bank]], tag="dnm")
                        asl = slice(tt2 * TT, (tt2 + 1) * TT)
                        if ee == 0:
                            E("vector", lambda e, d=p2acc[o][:, asl], p=pf[bank][:, 0:TT]:
                              e.tensor_tensor(out=d, in0=p, in1=p2wbc, op=ALU.mult),
                              r=[PSB[bank], "p2wbc"], w=[f"p2acc{o}"], tag="cb0")
                        else:
                            E("vector", lambda e, p=pf[bank][:, 0:TT]:
                              e.tensor_tensor(out=p2tmp, in0=p, in1=p2wbc, op=ALU.mult),
                              r=[PSB[bank], "p2wbc"], w=["p2tmp"], tag="cbm")
                            E("gpsimd", lambda e, d=p2acc[o][:, asl]:
                              e.tensor_tensor(out=d, in0=d, in1=p2tmp, op=ALU.add),
                              r=[f"p2acc{o}", "p2tmp"], w=[f"p2acc{o}"], tag="cba")
            E("gpsimd", lambda e, d=d_acc[:, qcols].rearrange("(k p) t -> p k t", p=128),
              s=p2accv.rearrange("p (k t) -> p k t", k=NF): e.dma_start(out=d, in_=s),
              r=[f"p2acc{k}" for k in range(NF)],
              w=[f"dacc{qt}_{k}" for k in range(NF)], dma=True, tag="accsp")

    def p3():
        E("vector", lambda e: e.engine_nop(), w=["warena"], deps=B.all_tokens(), tag="wgate3")
        E("vector", lambda e: e.engine_nop(), w=["aarena"], deps=B.all_tokens(), tag="agate3")
        E("sync", lambda e, d=sgWm.rearrange("p (k c) -> p k c", k=NF),
          s=d_sg.rearrange("(k p) c -> p k c", p=128): e.dma_start(out=d, in_=s),
          r=["warena", "w_sg"], w=[f"sgW{k}" for k in range(NF)], dma=True, tag="sgld")
        E("sync", lambda e, d=suWm.rearrange("p (k c) -> p k c", k=NF),
          s=d_su.rearrange("(k p) c -> p k c", p=128): e.dma_start(out=d, in_=s),
          r=["warena", "w_su"], w=[f"suW{k}" for k in range(NF)], dma=True, tag="suld")
        E("sync", lambda e, d=sdWm.rearrange("p (k c) -> p k c", k=NSI),
          s=d_sd.rearrange("(k p) c -> p k c", p=128): e.dma_start(out=d, in_=s),
          r=["warena", "w_sd"], w=[f"sdW{k}" for k in range(NSI)], dma=True, tag="sdld")
        for st in range(NST):
            cols = slice(st * ST, (st + 1) * ST)
            qt = st // 4
            for (dst, srct, rl, wpfx, tg) in [
                (p3xnv, d_xn, [f"dxn{st}_{k}" for k in range(NF)], "p3xn", "p3xnld"),
                (p3sv, d_s, [f"ds{st}_{k}" for k in range(NF)], "p3s", "p3sld"),
                (p3xv, xfm, [], "p3x", "p3xld"),
                (p3acv, d_acc, [f"dacc{qt}_{k}" for k in range(NF)], "p3ac", "p3acld"),
            ]:
                E("sync", lambda e, d=dst.rearrange("p (k t) -> p k t", k=NF),
                  s=srct[:, cols].rearrange("(k p) t -> p k t", p=128):
                  e.dma_start(out=d, in_=s), r=["aarena"] + rl,
                  w=[f"{wpfx}{k}" for k in range(NF)], dma=True, tag=tg)
            for j in range(NSI):
                b0 = (j % 2) * 2
                b1_ = b0 + 1
                for k in range(NF):
                    mm(pf[b0][:, 0:ST], sgW[k][:, j * 128:(j + 1) * 128], p3xn[k],
                       k == 0, k == NF - 1,
                       rb=[f"sgW{k}", f"p3xn{k}", "warena"], wb=[PSB[b0]], tag="shg")
                for k in range(NF):
                    mm(pf[b1_][:, 0:ST], suW[k][:, j * 128:(j + 1) * 128], p3xn[k],
                       k == 0, k == NF - 1,
                       rb=[f"suW{k}", f"p3xn{k}", "warena"], wb=[PSB[b1_]], tag="shu")
                E("scalar", lambda e, d=p3hs[j], p=pf[b0][:, 0:ST]:
                  e.activation(out=d, in_=p, func=AF.Silu),
                  r=[PSB[b0]], w=[f"p3hs{j}"], tag="shs")
                E("vector", lambda e, d=p3hs[j], p=pf[b1_][:, 0:ST]:
                  e.tensor_tensor(out=d, in0=d, in1=p, op=ALU.mult),
                  r=[f"p3hs{j}", PSB[b1_]], w=[f"p3hs{j}"], tag="shm")
            for k in range(NF):
                mm(pf[4][0:1, 0:ST], s_segb[:, k:k + 1], p3xn[k], k == 0, k == NF - 1,
                   rb=["segb", f"p3xn{k}"], wb=[PSB[4]], tag="swm")
            E("scalar", lambda e, p=pf[4][0:1, 0:ST]:
              e.activation(out=p3sw, in_=p, func=AF.Tanh, scale=0.5),
              r=[PSB[4], "c_fc"], w=["p3sw"], tag="swt")
            E("vector", lambda e: e.tensor_scalar(out=p3sw, in0=p3sw, scalar1=1.0,
                                                  scalar2=0.5, op0=ALU.add, op1=ALU.mult),
              r=["p3sw"], w=["p3sw"], tag="sws")
            mm(pf[5][:, 0:ST], s_o1f, p3sw, True, True,
               rb=["c_o1f", "p3sw"], wb=[PSB[5]], tag="swb")
            E("vector", lambda e, p=pf[5][:, 0:ST]: e.tensor_copy(out=p3swb, in_=p),
              r=[PSB[5]], w=["p3swb"], tag="swbe")
            for o in range(NF):
                bank = o % 2
                for k in range(NSI):
                    mm(pf[bank][:, 0:ST], sdW[k][:, o * 128:(o + 1) * 128], p3hs[k],
                       k == 0, k == NSI - 1,
                       rb=[f"sdW{k}", f"p3hs{k}", "warena"], wb=[PSB[bank]], tag="sdm")
                E("vector", lambda e, d=p3g[o], p=pf[bank][:, 0:ST]:
                  e.tensor_tensor(out=d, in0=p, in1=p3swb, op=ALU.mult),
                  r=[PSB[bank], "p3swb"], w=[f"p3g{o}"], tag="sde")
                E("vector", lambda e, d=p3g[o], s=p3ac[o]:
                  e.tensor_tensor(out=d, in0=d, in1=s, op=ALU.add),
                  r=[f"p3g{o}", f"p3ac{o}"], w=[f"p3g{o}"], tag="sda")
                E("scalar", lambda e, d=p3g[o]: e.activation(out=d, in_=d, func=AF.Silu),
                  r=[f"p3g{o}"], w=[f"p3g{o}"], tag="gsl")
                E("vector", lambda e, d=p3g[o], s=p3s[o]:
                  e.tensor_tensor(out=d, in0=d, in1=s, op=ALU.mult),
                  r=[f"p3g{o}", f"p3s{o}"], w=[f"p3g{o}"], tag="gms")
                E("vector", lambda e, d=p3g[o], s=p3x[o]:
                  e.tensor_tensor(out=d, in0=d, in1=s, op=ALU.add),
                  r=[f"p3g{o}", f"p3x{o}"], w=[f"p3g{o}"], tag="gax")
            E("gpsimd", lambda e, d=ofm[:, cols].rearrange("(k p) t -> p k t", p=128),
              s=p3gv.rearrange("p (k t) -> p k t", k=NF): e.dma_start(out=d, in_=s),
              r=[f"p3g{k}" for k in range(NF)], w=["ofm"], dma=True, tag="outsp")


    for st in range(PREFIX if PREFIX else NST):
        p1a_supertile(st)
        if not PREFIX:
            p0_pump(26)
    if not PREFIX:
        p0_pump(10 ** 9)
    if not PREFIX:
        if PHASES in ("1b", "2", "full"):
            p1b()
        if PHASES in ("2", "full"):
            p2()
        if PHASES == "full":
            p3()

    B.finalize()

    with nc.Block() as block:
        @block.sync
        def _(e):
            B.run_engine("sync", e, sems)

        @block.tensor
        def _(e):
            B.run_engine("tensor", e, sems)

        @block.vector
        def _(e):
            B.run_engine("vector", e, sems)

        @block.scalar
        def _(e):
            B.run_engine("scalar", e, sems)

        @block.gpsimd
        def _(e):
            B.run_engine("gpsimd", e, sems)
            for s in ("dq_sync_0", "dq_sync_1", "dq_scalar_0", "dq_scalar_1",
                      "dq_gpsimd_0", "dq_gpsimd_1"):
                if B.cnt.get(s):
                    e.wait_ge(sems[s], B.cnt[s])

    stack.close()
    return nc


def _host_prep(inputs):
    x = np.asarray(inputs["L0_attn_x"])
    f = np.ascontiguousarray
    d = {}
    d["ipT"] = f(np.asarray(inputs["L0_attn_in_proj_w"]).T)
    d["opT"] = f(np.asarray(inputs["L0_attn_out_proj_w"]).T)
    d["f1T"] = f(np.asarray(inputs["L0_attn_ffn_w1"]).T)
    d["f2T"] = f(np.asarray(inputs["L0_attn_ffn_w2"]).T)
    d["guT"] = f(np.asarray(inputs["L0_ffn_experts_gate_up"]).transpose(0, 2, 1))
    d["dnT"] = f(np.asarray(inputs["L0_ffn_experts_down"]).transpose(0, 2, 1))
    d["sgT"] = f(np.asarray(inputs["L0_ffn_shared_gate_w"]).T)
    d["suT"] = f(np.asarray(inputs["L0_ffn_shared_up_w"]).T)
    d["sdT"] = f(np.asarray(inputs["L0_ffn_shared_down_w"]).T)
    d["gTc"] = f(np.asarray(inputs["L0_ffn_gate_w"]).T)
    d["segc"] = f(np.asarray(inputs["L0_ffn_shared_expert_gate_w"]).reshape(8, 128).T)
    ln = np.zeros((128, 48), np.float32)
    for i, nm in enumerate(["ln0_w", "ln0_b", "ln1_w", "ln1_b", "ln2_w", "ln2_b"]):
        ln[:, i * 8:(i + 1) * 8] = np.asarray(inputs["L0_attn_" + nm]).reshape(8, 128).T
    d["lnc"] = ln
    d["cwc"] = f(np.asarray(inputs["L0_attn_conv_w"]).reshape(17, 128, 4)
                 .transpose(1, 0, 2).reshape(128, 68))
    d["cbc"] = f(np.asarray(inputs["L0_attn_conv_b"]).reshape(17, 128).T)
    d["b1c"] = f(np.asarray(inputs["L0_attn_ffn_b1"]).reshape(32, 128).T)
    d["b2c"] = f(np.asarray(inputs["L0_attn_ffn_b2"]).reshape(8, 128).T)
    d["nwp"] = f(np.asarray(inputs["L0_attn_norm_w"]).reshape(16, 128).T)
    D = np.asarray(inputs["L0_attn_D_param"]).astype(np.float32)
    dp = np.zeros((128, 16), np.float32)
    for c in range(16):
        dp[0:64, c] = D[2 * c]
        dp[64:128, c] = D[2 * c + 1]
    d["dpr"] = dp
    d["dtb"] = f(np.asarray(inputs["L0_attn_dt_bias"]).reshape(32, 1))
    d["alg"] = f(np.asarray(inputs["L0_attn_A_log"]).reshape(32, 1))
    idx = np.arange(128)
    d["c_lt_b"] = (idx[:, None] <= idx[None, :]).astype(BFNP)
    d["c_ut_b"] = (idx[:, None] > idx[None, :]).astype(BFNP)
    d["c_id_b"] = np.eye(128, dtype=BFNP)
    d["c_id_f"] = np.eye(128, dtype=np.float32)
    sel = np.zeros((32, 2048), BFNP)
    for c in range(16):
        for p_ in range(128):
            sel[2 * c + (p_ >= 64), c * 128 + p_] = 1
    d["c_sel"] = sel
    d["c_o1f"] = np.ones((1, 128), np.float32)
    d["c_ocb"] = np.ones((128, 1), BFNP)
    oh = np.zeros((8, 1024), np.float32)
    for e_ in range(8):
        oh[e_, e_ * 128:(e_ + 1) * 128] = 1
    d["c_oh8"] = oh
    fc = np.zeros((128, 8), np.float32)
    for fi, fv in enumerate([1e-5, 1.0, -0.5, -1.0, 0.5, 1.0 / 2048.0, 1.0 / 1024.0, 2.0]):
        fc[:, fi] = fv
    d["c_fc"] = fc
    xcore = []
    for i in range(8):
        xf = np.empty((1024, 4096), np.float32)
        xf[:, :2048] = x[2 * i].T
        xf[:, 2048:] = x[2 * i + 1].T
        xcore.append(xf)
    return d, xcore


_LAST = {}


def kernel(**inputs):
    shared, xcore = _host_prep(inputs)
    nc = build_program()
    in_maps = []
    for i in range(8):
        m = dict(shared)
        m["xfm"] = xcore[i]
        in_maps.append(m)
    res = run_bass_kernel_spmd(nc, in_maps, core_ids=list(range(8)))
    _LAST["res"] = res
    out = np.empty((16, 2048, 1024), np.float32)
    for i in range(8):
        o = res.results[i]["ofm"]
        out[2 * i] = o[:, :2048].T
        out[2 * i + 1] = o[:, 2048:].T
    return out


# version 13 (final)
# speedup vs baseline: 1.2484x; 1.0031x over previous
import numpy as np
import ml_dtypes
import concourse.bass as bass
import concourse.mybir as mybir
from concourse.bass_utils import run_bass_kernel_spmd

F32 = mybir.dt.float32
BF16 = mybir.dt.bfloat16
AF = mybir.ActivationFunctionType
ALU = mybir.AluOpType
AX = mybir.AxisListType
BFNP = ml_dtypes.bfloat16

T = 4096
DM = 1024
NF = 8
DIP = 4256
DI = 2048
NCV = 17
NH = 32
NPAIR = 16
FFN = 4096
ST = 256
NST = 16
Q = 128
NE = 8
MI = 1408
NMI = 11
GU2 = 2816
NGU = 22
SI = 2816
NSI = 22
QT = 1024
NQ = 4
TT = 512
EPS = 1e-5

SB_SZ = 229376

DEBUG = False
PHASES = "full"
PREFIX = 0


class Buf:
    __slots__ = ("w", "rs", "name")

    def __init__(self, name=""):
        self.w = None
        self.rs = []
        self.name = name


class Bld:
    def __init__(self):
        self.ops = []

    def emit(self, eng, fn, reads=(), writes=(), deps=(), dma=False, tag=""):
        i = len(self.ops)
        dl = list(deps)
        for b in reads:
            if b.w is not None:
                dl.append((b.w, True))
        for b in writes:
            if b.w is not None:
                dl.append((b.w, False))
            for t in b.rs:
                dl.append((t, False))
        self.ops.append((eng, fn, dl, dma, tag))
        for b in reads:
            b.rs.append(i)
        for b in writes:
            b.w = i
            b.rs = []
        if not hasattr(self, "last"):
            self.last = {}
        self.last[eng] = i
        return i

    def all_tokens(self):
        return [(i, False) for i in getattr(self, "last", {}).values()]

    def finalize(self):
        semof = {"tensor": "pe", "vector": "ve", "scalar": "sc", "gpsimd": "gp"}
        n = len(self.ops)
        eng_of = [o[0] for o in self.ops]
        is_dma = [o[3] for o in self.ops]
        need = [None] * n
        depended = set()
        for i in range(n):
            eng = eng_of[i]
            nd = []
            for (j, raw) in self.ops[i][2]:
                if j == i:
                    continue
                same = eng_of[j] == eng and not is_dma[j] and not is_dma[i]
                if same and eng == "tensor":
                    continue
                if same and not raw:
                    continue
                nd.append(j)
                depended.add(j)
            need[i] = nd
        G = 4
        val = {}
        cnt = {}
        qidx = {}
        self.fence = {}
        qgrp_end = {}
        per_q_ops = {}
        for i in range(n):
            if is_dma[i]:
                q = eng_of[i]
                per_q_ops.setdefault(q, []).append(i)
        for q, ops in per_q_ops.items():
            g = -1
            gsz = 0
            prev = -10
            for i in ops:
                if i != prev + 1 or gsz >= G:
                    g += 1
                    gsz = 0
                gsz += 1
                prev = i
                s = f"dq_{q}_{g % 2}"
                cnt[s] = cnt.get(s, 0) + 16
                qidx[i] = (q, g, s)
            acc = {}
            for i in ops:
                q_, g, s = qidx[i]
                acc[s] = acc.get(s, 0) + 16
                qgrp_end[(q, g)] = (s, acc[s])
            seen = set()
            for i in ops:
                q_, g, s = qidx[i]
                if g not in seen:
                    seen.add(g)
                    if g >= 2:
                        self.fence[i] = qgrp_end[(q, g - 2)]
        for i in range(n):
            if is_dma[i]:
                q, g, s = qidx[i]
                val[i] = (s, qgrp_end[(q, g)][1], 16)
            elif i in depended:
                s = semof[eng_of[i]]
                cnt[s] = cnt.get(s, 0) + 1
                val[i] = (s, cnt[s], 1)
        wm = {}
        waits = [None] * n
        for i in range(n):
            m = wm.setdefault(eng_of[i], {})
            wl = []
            for j in sorted(set(need[i])):
                s, v, a = val[j]
                if m.get(s, 0) < v:
                    wl.append((s, v))
                    m[s] = v
            waits[i] = wl
        self.waits = waits
        self.val = val
        self.cnt = cnt

    def run_engine(self, eng_name, e, sems):
        for i, (eng, fn, deps, dma, tag) in enumerate(self.ops):
            if eng != eng_name:
                continue
            for (s, v) in self.waits[i]:
                e.wait_ge(sems[s], v)
            if dma and i in self.fence:
                fs, fv = self.fence[i]
                e.wait_ge(sems[fs], fv)
            inst = fn(e)
            if i in self.val:
                s, v, a = self.val[i]
                inst.then_inc(sems[s], 16 if dma else a)


def build_program():
    nc = bass.Bass("TRN2", target_bir_lowering=False, debug=False)
    dt = nc.dram_tensor

    xfm = dt("xfm", [DM, T], F32, kind="ExternalInput").ap()
    ipT = dt("ipT", [DM, DIP], F32, kind="ExternalInput").ap()
    opT = dt("opT", [DI, DM], F32, kind="ExternalInput").ap()
    f1T = dt("f1T", [DM, FFN], F32, kind="ExternalInput").ap()
    f2T = dt("f2T", [FFN, DM], F32, kind="ExternalInput").ap()
    guT = dt("guT", [NE, DM, GU2], F32, kind="ExternalInput").ap()
    dnT = dt("dnT", [NE, MI, DM], F32, kind="ExternalInput").ap()
    sgT = dt("sgT", [DM, SI], F32, kind="ExternalInput").ap()
    suT = dt("suT", [DM, SI], F32, kind="ExternalInput").ap()
    sdT = dt("sdT", [SI, DM], F32, kind="ExternalInput").ap()
    gTc = dt("gTc", [DM, NE], F32, kind="ExternalInput").ap()
    segc = dt("segc", [128, 8], F32, kind="ExternalInput").ap()
    lnc = dt("lnc", [128, 48], F32, kind="ExternalInput").ap()
    cwc = dt("cwc", [128, 68], F32, kind="ExternalInput").ap()
    cbc = dt("cbc", [128, 17], F32, kind="ExternalInput").ap()
    b1c = dt("b1c", [128, 32], F32, kind="ExternalInput").ap()
    b2c = dt("b2c", [128, 8], F32, kind="ExternalInput").ap()
    nwp = dt("nwp", [128, 16], F32, kind="ExternalInput").ap()
    dpr = dt("dpr", [128, 16], F32, kind="ExternalInput").ap()
    dtb = dt("dtb", [32, 1], F32, kind="ExternalInput").ap()
    alg = dt("alg", [32, 1], F32, kind="ExternalInput").ap()
    c_lt_b = dt("c_lt_b", [128, 128], BF16, kind="ExternalInput").ap()
    c_ut_b = dt("c_ut_b", [128, 128], BF16, kind="ExternalInput").ap()
    c_id_b = dt("c_id_b", [128, 128], BF16, kind="ExternalInput").ap()
    c_id_f = dt("c_id_f", [128, 128], F32, kind="ExternalInput").ap()
    c_sel = dt("c_sel", [32, 2048], BF16, kind="ExternalInput").ap()
    c_o1f = dt("c_o1f", [1, 128], F32, kind="ExternalInput").ap()
    c_ocb = dt("c_ocb", [128, 1], BF16, kind="ExternalInput").ap()
    c_oh8 = dt("c_oh8", [8, 1024], F32, kind="ExternalInput").ap()
    c_fc = dt("c_fc", [128, 8], F32, kind="ExternalInput").ap()

    dbgk = "ExternalOutput" if DEBUG else "Internal"
    d_xn = dt("d_xn", [DM, T], BF16, kind=dbgk).ap()
    d_xb = dt("d_xb", [DM, T], F32, kind=dbgk).ap()
    d_xc = dt("d_xc", [DM, T], BF16, kind=dbgk).ap()
    d_w = dt("d_w", [8, T], F32, kind=dbgk).ap()
    d_s = dt("d_s", [DM, T], F32, kind=dbgk).ap()
    d_acc = dt("d_acc", [DM, T], F32, kind=dbgk).ap()
    d_f1 = dt("d_f1", [DM, FFN], BF16, kind="Internal").ap()
    d_f2 = dt("d_f2", [FFN, DM], BF16, kind="Internal").ap()
    d_gu = dt("d_gu", [NE, DM, GU2], BF16, kind="Internal").ap()
    d_dn = dt("d_dn", [NE, MI, DM], BF16, kind="Internal").ap()
    d_sg = dt("d_sg", [DM, SI], BF16, kind="Internal").ap()
    d_su = dt("d_su", [DM, SI], BF16, kind="Internal").ap()
    d_sd = dt("d_sd", [SI, DM], BF16, kind="Internal").ap()
    ofm = dt("ofm", [DM, T], F32, kind="ExternalOutput").ap()
    d_dbg = dt("d_dbg", [8, ST], F32, kind=dbgk).ap()
    d_dbg2 = dt("d_dbg2", [128, ST], F32, kind=dbgk).ap()
    d_dbg3 = dt("d_dbg3", [128, ST], F32, kind=dbgk).ap()

    nalloc = [0]
    off_ = [0]

    def at(shape, dtype, off):
        nalloc[0] += 1
        per = int(np.prod(shape[1:])) * mybir.dt.size(dtype)
        assert off + per <= SB_SZ, f"sbuf overflow {off}+{per}"
        h = nc.alloc_sbuf_tensor_at(f"sb{nalloc[0]}", list(shape), dtype,
                                    offset=off, align_bytes=32)
        return h.ap()

    def take(shape, dtype, n=1):
        per = int(np.prod(shape[1:])) * mybir.dt.size(dtype)
        per = -(-per // 32) * 32
        r = []
        for _ in range(n):
            r.append(at(shape, dtype, off_[0]))
            off_[0] += per
        return r if n > 1 else r[0]

    cursor = SB_SZ - nc.sbuf_bytes_remaining
    off_[0] = 17408
    assert cursor <= off_[0], ("bass internal allocs reach", cursor)
    s_lt_b = take([128, 128], BF16)
    s_ut_b = take([128, 128], BF16)
    s_id_b = take([128, 128], BF16)
    s_id_f = take([128, 128], F32)
    s_sel = take([32, 2048], BF16)
    s_o1f = take([1, 128], F32)
    s_ocb = take([128, 1], BF16)
    s_oh8 = take([8, 1024], F32)
    s_lnc = take([128, 48], F32)
    s_cwc = take([128, 68], F32)
    s_cbc = take([128, 17], F32)
    s_b1c = take([128, 32], F32)
    s_b2c = take([128, 8], F32)
    s_nwp = take([128, 16], F32)
    s_dpr = take([128, 16], F32)
    s_dtb = take([32, 1], F32)
    s_alg = take([32, 1], F32)
    s_acol = take([32, 1], F32)
    s_gtc = take([128, 8], F32, 8)
    s_seg = take([128, 8], F32)
    s_segb = take([128, 8], BF16)
    s_fc = take([128, 8], F32)
    assert off_[0] <= 30208, off_[0]

    WB = 30208
    W_SZ = 135168
    ipW = [at([128, DIP], BF16, WB + k * 8512) for k in range(NF)]
    opW = [at([128, DM], BF16, WB + 68096 + k * 2048) for k in range(16)]
    f1W = [at([128, FFN], BF16, WB + k * 8192) for k in range(NF)]
    f2W = [at([128, DM], BF16, WB + 65536 + k * 2048) for k in range(32)]
    guW = [at([128, GU2], BF16, WB + k * 5632) for k in range(NF)]
    dnW = [[at([128, DM], BF16, WB + 45056 + (p * NMI + k) * 2048)
            for k in range(NMI)] for p in range(2)]
    sgW = [at([128, SI], BF16, WB + k * 5632) for k in range(NF)]
    suW = [at([128, SI], BF16, WB + 45056 + k * 5632) for k in range(NF)]
    sdW = [at([128, DM], BF16, WB + 90112 + k * 2048) for k in range(NSI)]
    f1Wm = at([128, FFN * NF], BF16, WB)
    f2Wm = at([128, DM * 32], BF16, WB + 65536)
    guWm = at([128, GU2 * NF], BF16, WB)
    dnWm = [at([128, DM * NMI], BF16, WB + 45056 + p * NMI * 2048) for p in range(2)]
    sgWm = at([128, SI * NF], BF16, WB)
    suWm = at([128, SI * NF], BF16, WB + 45056)
    sdWm = at([128, DM * NSI], BF16, WB + 90112)
    p1bst = [at([128, 512], F32, WB + 131072 + i * 2048) for i in range(2)]

    AB = WB + W_SZ
    off_[0] = AB
    xavP = take([128, ST * NF], F32, 2)
    xaP = [[xavP[par][:, k * ST:(k + 1) * ST] for k in range(NF)] for par in range(2)]
    xmbv = take([128, ST * NF], BF16)
    xmb = [xmbv[:, k * ST:(k + 1) * ST] for k in range(NF)]
    xsc = take([128, ST], BF16, 2)
    xsq = take([128, ST], BF16, 2)
    xtmp = take([128, ST], F32, 2)
    zt = take([128, ST], BF16, NPAIR)
    xBCr = take([128, ST + 4], BF16, NCV)
    xst = take([128, ST], BF16, NPAIR)
    bct = take([128, ST], BF16)
    sCf = take([64, ST], BF16)
    HT = take([64, DI], BF16, 2)
    xsT = take([128, 128], BF16, NPAIR)
    assert off_[0] <= SB_SZ, off_[0]
    save_off = off_[0]
    off_[0] = WB + 100864
    sBT = take([128, 64], BF16)
    sBh = take([128, 64], BF16, 2)
    sGm = take([128, 128], BF16)
    sL2 = take([128, 128], BF16, 4)
    sM2 = take([128, 128], BF16, 4)
    sRL = take([128, 128], BF16, 4)
    sEb = take([128, 128], BF16, 2)
    sT1 = take([128, 128], BF16, 2)
    sE2 = take([128, 128], F32, 2)
    sdtT = take([128, 32], F32)
    slnd = take([128, 32], F32)
    saT = take([128, 32], F32)
    saTb = take([128, 32], BF16)
    swT = take([128, 32], F32)
    swd = take([128, 32], F32)
    sEf = take([32, 128], BF16)
    sLb = take([64, 32], F32)
    sLr = take([1, 32], F32)
    sLr2 = take([1, 32], F32)
    r_m, r_ms, r_t, r_v, r_r1, r_mr = take([1, ST], F32, 6)
    r_r2 = r_r1
    r_rr = r_r1
    dtraw = take([32, ST], F32)
    dtf = take([32, ST], F32)
    dtt = take([32, ST], F32)
    afm = dtraw
    g_lf = take([8, ST], F32)
    g_lt = take([128, 8], F32, 2)
    g_d = take([128, 8], F32)
    g_m1 = take([128, 1], F32)
    g_sp = take([128, 8], F32)
    g_ex = take([128, 8], F32)
    g_ss = take([128, 1], F32)
    g_w = take([128, 8], F32, 2)
    g_wf = g_lf
    Ast = take([128, 1024], F32, 2)
    AstB = take([128, 1024], BF16, 2)
    xns = take([128, ST * NF], BF16)
    assert off_[0] <= WB + W_SZ, off_[0]
    off_[0] = save_off

    STB = 512
    off_[0] = AB
    xcbv = take([128, STB * NF], BF16)
    xcb = [xcbv[:, k * STB:(k + 1) * STB] for k in range(NF)]
    g1b = take([128, STB], BF16, 32)
    xbfv = take([128, STB * NF], F32)
    xbf = [xbfv[:, k * STB:(k + 1) * STB] for k in range(NF)]
    assert off_[0] <= SB_SZ

    off_[0] = WB + 90112
    p2accv = take([128, QT * NF], F32)
    p2acc = [p2accv[:, k * QT:(k + 1) * QT] for k in range(NF)]
    assert off_[0] <= WB + W_SZ, off_[0]
    off_[0] = AB
    p2xnv = take([128, TT * NF], BF16)
    p2xn = [p2xnv[:, k * TT:(k + 1) * TT] for k in range(NF)]
    p2h = [take([128, TT], BF16, NMI) for _ in range(2)]
    p2wbc = take([128, TT], F32)
    p2tmp = take([128, TT], F32)
    p2w = take([8, QT], F32)
    p2st = take([128, 1408], F32, 4)
    assert off_[0] <= SB_SZ

    off_[0] = AB
    p3hs = take([128, ST], BF16, NSI)
    p3xnv = take([128, ST * NF], BF16)
    p3xn = [p3xnv[:, k * ST:(k + 1) * ST] for k in range(NF)]
    p3sv = take([128, ST * NF], F32)
    p3s = [p3sv[:, k * ST:(k + 1) * ST] for k in range(NF)]
    p3xv = take([128, ST * NF], F32)
    p3x = [p3xv[:, k * ST:(k + 1) * ST] for k in range(NF)]
    p3acv = take([128, ST * NF], F32)
    p3ac = [p3acv[:, k * ST:(k + 1) * ST] for k in range(NF)]
    p3gv = take([128, ST * NF], F32)
    p3g = [p3gv[:, k * ST:(k + 1) * ST] for k in range(NF)]
    p3sw = take([1, ST], F32)
    p3swb = take([128, ST], F32)
    p3st = take([128, 1408], F32, 2)
    assert off_[0] <= SB_SZ

    import contextlib
    stack = contextlib.ExitStack()
    pf = [stack.enter_context(nc.psum_tensor(f"pf{i}", [128, 512], F32)).ap()
          for i in range(6)]
    pb = [stack.enter_context(nc.psum_tensor(f"pb{i}", [128, 1024], BF16)).ap()
          for i in range(2)]
    sem_names = ["pe", "ve", "sc", "gp",
                 "dq_sync_0", "dq_sync_1", "dq_scalar_0", "dq_scalar_1",
                 "dq_gpsimd_0", "dq_gpsimd_1"]
    sems = {s: stack.enter_context(nc.semaphore(s)) for s in sem_names}

    B = Bld()
    bufs = {}

    def bf(x):
        if isinstance(x, Buf):
            return x
        if x not in bufs:
            bufs[x] = Buf(x)
        return bufs[x]

    PSB = [bf(f"ps{i}") for i in range(8)]

    FCVALS = [1e-5, 1.0, -0.5, -1.0, 0.5, 1.0 / 2048.0, 1.0 / 1024.0, 2.0]
    for fi, fv in enumerate(FCVALS):
        nc.const_aps.register(F32, fv, s_fc[:, fi:fi + 1])

    def E(eng, fn, r=(), w=(), deps=(), dma=False, tag=""):
        return B.emit(eng, fn, [bf(x) for x in r], [bf(x) for x in w], deps, dma, tag)

    def mm(out, lhsT, rhs, start, stop, rb=(), wb=(), tp=None, tag="mm"):
        def f(e, o=out, l=lhsT, rr=rhs, s=start, p=stop, t=tp):
            if t is None:
                return e.matmul(out=o, lhsT=l, rhs=rr, start=s, stop=p)
            return e.matmul(out=o, lhsT=l, rhs=rr, start=s, stop=p, tile_position=t)
        return E("tensor", f, r=rb, w=wb, tag=tag)

    def tpose(out, in_, ident, rb, wb, tag="tp"):
        def f(e, o=out, i_=in_, d=ident):
            return e.transpose(out=o, in_=i_, identity=d)
        return E("tensor", f, r=rb, w=wb, tag=tag)

    cl = [(s_lt_b, c_lt_b, "c_lt"), (s_ut_b, c_ut_b, "c_ut"), (s_id_b, c_id_b, "c_idb"),
          (s_id_f, c_id_f, "c_idf"), (s_sel, c_sel, "c_sel"), (s_o1f, c_o1f, "c_o1f"),
          (s_ocb, c_ocb, "c_ocb"), (s_oh8, c_oh8, "c_oh8"), (s_lnc, lnc, "c_lnc"),
          (s_cwc, cwc, "c_cwc"), (s_cbc, cbc, "c_cbc"), (s_b1c, b1c, "c_b1c"),
          (s_b2c, b2c, "c_b2c"), (s_nwp, nwp, "c_nwp"), (s_dpr, dpr, "c_dpr"),
          (s_dtb, dtb, "c_dtb"), (s_alg, alg, "c_alg"), (s_seg, segc, "c_seg"),
          (s_fc, c_fc, "c_fc")]
    for (dst, src, nm) in cl:
        if PREFIX and nm not in ("c_ocb", "c_fc", "c_alg", "c_dtb", "c_lt"):
            continue
        E("sync", lambda e, d=dst, s=src: e.dma_start(out=d, in_=s),
          w=[nm], dma=True, tag="constld")
    for k in range(8):
        if PREFIX:
            break
        E("sync", lambda e, d=s_gtc[k], s=gTc[k * 128:(k + 1) * 128, :]:
          e.dma_start(out=d, in_=s), w=[f"gtc{k}"], dma=True, tag="gtcld")
    E("scalar", lambda e: e.activation(out=s_acol, in_=s_alg, func=AF.Exp),
      r=["c_alg"], w=["acol"], tag="acol")
    E("vector", lambda e: e.tensor_scalar(out=s_acol, in0=s_acol, scalar1=-1.0,
                                          scalar2=None, op0=ALU.mult),
      r=["acol"], w=["acol"], tag="acol2")
    E("vector", lambda e: e.tensor_copy(out=s_segb, in_=s_seg),
      r=["c_seg"], w=["segb"], tag="segb")
    E("vector", lambda e: e.memset(HT[0][:], 0.0), w=["HT0"], tag="ht0")
    E("vector", lambda e: e.memset(HT[1][:], 0.0), w=["HT1"], tag="ht1")

    si = [0]

    def stage_conv(src, dstw, dstname, cols, eng="vector"):
        j = si[0] % 2
        sb = f"Ast{j}"
        E("sync", lambda e, d=Ast[j][:, 0:cols], s=src: e.dma_start(out=d, in_=s),
          w=[sb], dma=True, tag="wld")
        E(eng, lambda e, d=dstw, s=Ast[j][:, 0:cols]: e.tensor_copy(out=d, in_=s),
          r=[sb, "warena"], w=[dstname], tag="wcv")
        si[0] += 1

    if not PREFIX:
        for k in range(NF):
            for c4 in range(5):
                c0, c1 = c4 * 1024, min(DIP, (c4 + 1) * 1024)
                stage_conv(ipT[k * 128:(k + 1) * 128, c0:c1], ipW[k][:, c0:c1],
                           f"ipW{k}", c1 - c0)
        for k in range(16):
            stage_conv(opT[k * 128:(k + 1) * 128, :], opW[k][:, 0:1024], f"opW{k}", 1024)

    p0_jobs = []
    for k in range(NF):
        for c4 in range(4):
            p0_jobs.append((f1T[k * 128:(k + 1) * 128, c4 * 1024:(c4 + 1) * 1024],
                            d_f1[k * 128:(k + 1) * 128, c4 * 1024:(c4 + 1) * 1024],
                            1024, "w_f1"))
    for k in range(32):
        p0_jobs.append((f2T[k * 128:(k + 1) * 128, :],
                        d_f2[k * 128:(k + 1) * 128, :], 1024, "w_f2"))
    for ee in range(NE):
        for k in range(NF):
            for c4 in range(3):
                c0, c1 = c4 * 1024, min(GU2, (c4 + 1) * 1024)
                p0_jobs.append((guT[ee, k * 128:(k + 1) * 128, c0:c1],
                                d_gu[ee, k * 128:(k + 1) * 128, c0:c1],
                                c1 - c0, f"w_gu{ee}"))
        for k in range(NMI):
            p0_jobs.append((dnT[ee, k * 128:(k + 1) * 128, :],
                            d_dn[ee, k * 128:(k + 1) * 128, :], 1024, f"w_dn{ee}"))
    for (srcT, dstT, wname) in [(sgT, d_sg, "w_sg"), (suT, d_su, "w_su")]:
        for k in range(NF):
            for c4 in range(3):
                c0, c1 = c4 * 1024, min(SI, (c4 + 1) * 1024)
                p0_jobs.append((srcT[k * 128:(k + 1) * 128, c0:c1],
                                dstT[k * 128:(k + 1) * 128, c0:c1], c1 - c0, wname))
    for k in range(NSI):
        p0_jobs.append((sdT[k * 128:(k + 1) * 128, :],
                        d_sd[k * 128:(k + 1) * 128, :], 1024, "w_sd"))
    p0_i = [0]

    def p0_pump(n):
        while n > 0 and p0_i[0] < len(p0_jobs):
            batch = p0_jobs[p0_i[0]:p0_i[0] + 1]
            for j, (srcap, dst, cols, wname) in enumerate(batch):
                E("sync", lambda e, d=Ast[j][:, 0:cols], s=srcap: e.dma_start(out=d, in_=s),
                  w=[f"Ast{j}"], dma=True, tag="p0ld")
            for j, (srcap, dst, cols, wname) in enumerate(batch):
                E("gpsimd", lambda e, d=AstB[j][:, 0:cols], s=Ast[j][:, 0:cols]:
                  e.tensor_copy(out=d, in_=s), r=[f"Ast{j}"], w=[f"AstB{j}"], tag="p0cv")
            for j, (srcap, dst, cols, wname) in enumerate(batch):
                E("sync", lambda e, d=dst, s=AstB[j][:, 0:cols]: e.dma_start(out=d, in_=s),
                  r=[f"AstB{j}"], w=[wname], dma=True, tag="p0st")
            p0_i[0] += len(batch)
            n -= len(batch)

    def w_stream(srcs, cve="vector"):
        if not srcs:
            return
        depth = len(srcs[0][4])
        for b0 in range(0, len(srcs), depth):
            batch = srcs[b0:b0 + depth]
            for j, (srcap, dstap, cols, dstname, stg, spfx) in enumerate(batch):
                E("sync", lambda e, d=stg[j][:, 0:cols], s=srcap:
                  e.dma_start(out=d, in_=s), w=[f"{spfx}{j}"], dma=True, tag="wsl")
            for j, (srcap, dstap, cols, dstname, stg, spfx) in enumerate(batch):
                E(cve, lambda e, d=dstap, s=stg[j][:, 0:cols]: e.tensor_copy(out=d, in_=s),
                  r=[f"{spfx}{j}", "warena"], w=[dstname], tag="wsc")

    def ln_rows(denom, tagp):
        E("vector", lambda e, p=pf[4][0:1, 0:ST]:
          e.tensor_scalar(out=r_m, in0=p, scalar1=1.0 / denom, scalar2=None, op0=ALU.mult),
          r=[PSB[4]], w=["r_m"], tag=tagp + "m")
        E("vector", lambda e, p=pf[5][0:1, 0:ST]:
          e.tensor_scalar(out=r_ms, in0=p, scalar1=1.0 / denom, scalar2=None, op0=ALU.mult),
          r=[PSB[5]], w=["r_ms"], tag=tagp + "ms")
        E("vector", lambda e: e.tensor_tensor(out=r_t, in0=r_m, in1=r_m, op=ALU.mult),
          r=["r_m"], w=["r_t"], tag=tagp + "t")
        E("vector", lambda e: e.tensor_tensor(out=r_v, in0=r_ms, in1=r_t, op=ALU.subtract),
          r=["r_ms", "r_t"], w=["r_v"], tag=tagp + "v")
        E("scalar", lambda e: e.activation(out=r_t, in_=r_v, func=AF.Ln, bias=EPS),
          r=["r_v", "c_fc"], w=["r_t"], tag=tagp + "ln")
        E("scalar", lambda e: e.activation(out=r_r1, in_=r_t, func=AF.Exp, scale=-0.5),
          r=["r_t", "c_fc"], w=["r_r1"], tag=tagp + "ex")
        E("vector", lambda e: e.tensor_tensor(out=r_mr, in0=r_m, in1=r_r1, op=ALU.mult),
          r=["r_m", "r_r1"], w=["r_mr"], tag=tagp + "mr")

    def bcast_row(row, rowname, bank, tagp):
        return mm(pf[bank][:, 0:ST], s_o1f, row, True, True,
                  rb=["c_o1f", rowname], wb=[PSB[bank]], tag=tagp)

    def stats_pass(srcs, srcnames, tagp):
        nsrc = len(srcs)
        for k in range(nsrc):
            sc_, sq_ = xsc[k % 2], xsq[k % 2]
            E("vector", lambda e, d=sc_, s=srcs[k]: e.tensor_copy(out=d, in_=s),
              r=[srcnames[k]], w=[f"xsc{k % 2}"], tag=tagp + "c")
            mm(pf[4][0:1, 0:ST], s_ocb, sc_, k == 0, k == nsrc - 1,
               rb=["c_ocb", f"xsc{k % 2}"], wb=[PSB[4]], tag=tagp + "m")
            E("scalar", lambda e, d=sq_, s=srcs[k]: e.activation(out=d, in_=s, func=AF.Square),
              r=[srcnames[k]], w=[f"xsq{k % 2}"], tag=tagp + "q")
            mm(pf[5][0:1, 0:ST], s_ocb, sq_, k == 0, k == nsrc - 1,
               rb=["c_ocb", f"xsq{k % 2}"], wb=[PSB[5]], tag=tagp + "qm")

    def ssd_chunk(st, qq, seq):
        qs = slice(qq * Q, (qq + 1) * Q)
        for j in range(NPAIR):
            bk = 6 + (j % 2)
            sl = slice(((j // 2) % 4) * 128, ((j // 2) % 4) * 128 + 128)
            tpose(pb[j % 2][:, sl], xst[j][:, qs], s_id_b,
                  rb=[f"xst{j}", "c_idb"], wb=[PSB[bk]], tag="xsT")
            if j % 4 < 2:
                E("vector", lambda e, d=xsT[j], s=pb[j % 2][:, sl]:
                  e.tensor_copy(out=d, in_=s), r=[PSB[bk]], w=[f"xsT{j}"], tag="xsTe")
            else:
                E("scalar", lambda e, d=xsT[j], s=pb[j % 2][:, sl]:
                  e.activation(out=d, in_=s, func=AF.Copy),
                  r=[PSB[bk]], w=[f"xsT{j}"], tag="xsTe")
        tpose(pb[0][:, 512:576], bct[0:64, qs], s_id_b[0:64, 0:64],
              rb=["bct", "c_idb"], wb=[PSB[6]], tag="BT")
        E("vector", lambda e, s=pb[0][:, 512:576]: e.tensor_copy(out=sBT, in_=s),
          r=[PSB[6]], w=["sBT"], tag="BTe")
        mm(pf[5][0:128, 0:128], bct[0:64, qs], sCf[:, qs], True, True,
           rb=["bct", "sCf"], wb=[PSB[5]], tag="G")
        E("vector", lambda e, p=pf[5][0:128, 0:128]:
          e.tensor_tensor(out=sGm, in0=p, in1=s_lt_b, op=ALU.mult),
          r=[PSB[5], "c_lt"], w=["sGm"], tag="Gm")
        tpose(pf[5][:, 128:160], dtf[:, qs], s_id_f[0:32, 0:32],
              rb=["dtf", "c_idf"], wb=[PSB[5]], tag="dtT")
        E("vector", lambda e, s=pf[5][:, 128:160]: e.tensor_copy(out=sdtT, in_=s),
          r=[PSB[5]], w=["sdtT"], tag="dtTe")
        E("scalar", lambda e, s=pf[5][:, 128:160]: e.activation(out=slnd, in_=s, func=AF.Ln),
          r=[PSB[5]], w=["slnd"], tag="lnd")
        tpose(pf[5][:, 160:192], afm[:, qs], s_id_f[0:32, 0:32],
              rb=["dtraw", "c_idf"], wb=[PSB[5]], tag="aT")
        E("vector", lambda e, s=pf[5][:, 160:192]: e.tensor_copy(out=saT, in_=s),
          r=[PSB[5]], w=["saT"], tag="aTe")
        E("vector", lambda e: e.tensor_copy(out=saTb, in_=saT),
          r=["saT"], w=["saTb"], tag="aTb")
        mm(pf[5][0:32, 192:320], saTb, s_lt_b, True, True,
           rb=["saTb", "c_lt"], wb=[PSB[5]], tag="sfm")
        E("scalar", lambda e, p=pf[5][0:32, 192:320]:
          e.activation(out=sEf, in_=p, func=AF.Exp), r=[PSB[5]], w=["sEf"], tag="Ef")
        mm(pf[5][:, 320:352], s_ut_b, saTb, True, True,
           rb=["c_ut", "saTb"], wb=[PSB[5]], tag="wtm")
        E("scalar", lambda e, p=pf[5][:, 320:352]:
          e.activation(out=swT, in_=p, func=AF.Exp), r=[PSB[5]], w=["swT"], tag="wte")
        E("vector", lambda e: e.tensor_tensor(out=swd, in0=swT, in1=sdtT, op=ALU.mult),
          r=["swT", "sdtT"], w=["swd"], tag="wtd")
        E("scalar", lambda e: e.activation(out=sLr, in_=saT[0:1, :], func=AF.Exp),
          r=["saT"], w=["sLr"], tag="La")
        E("vector", lambda e: e.tensor_tensor(out=sLr2, in0=sLr, in1=swT[0:1, :], op=ALU.mult),
          r=["sLr", "swT"], w=["sLr2"], tag="Lb")
        mm(pf[5][0:64, 352:384], s_o1f[:, 0:64], sLr2, True, True,
           rb=["c_o1f", "sLr2"], wb=[PSB[5]], tag="Lm")
        E("vector", lambda e, p=pf[5][0:64, 352:384]: e.tensor_copy(out=sLb, in_=p),
          r=[PSB[5]], w=["sLb"], tag="Le")
        for w in range(8):
            hs4 = [4 * w + i for i in range(4)]
            prs = [2 * w, 2 * w + 1]
            for h in hs4:
                E("vector", lambda e, d=sRL[h % 4], sc1=saT[:, h:h + 1]:
                  e.tensor_scalar(out=d, in0=s_lt_b, scalar1=sc1, scalar2=None,
                                  op0=ALU.mult),
                  r=["c_lt", "saT"], w=[f"sRL{h % 4}"], tag="rl")
            for h in hs4:
                sl = slice((h % 4 // 2) * 128, (h % 4 // 2) * 128 + 128)
                mm(pf[h % 2][:, sl], s_ut_b, sRL[h % 4], True, True,
                   rb=["c_ut", f"sRL{h % 4}"], wb=[PSB[h % 2]], tag="dif")
            for h in hs4:
                sl = slice((h % 4 // 2) * 128, (h % 4 // 2) * 128 + 128)
                E("scalar", lambda e, d=sL2[h % 4], pp=pf[h % 2][:, sl],
                  b=slnd[:, h:h + 1]:
                  e.activation(out=d, in_=pp, func=AF.Exp, bias=b),
                  r=[PSB[h % 2], "slnd"], w=[f"sL2{h % 4}"], tag="l2")
            for h in hs4:
                E("gpsimd", lambda e, d=sM2[h % 4], s=sL2[h % 4]:
                  e.tensor_tensor(out=d, in0=s, in1=sGm, op=ALU.mult),
                  r=[f"sL2{h % 4}", "sGm"], w=[f"sM2{h % 4}"], tag="m2")
            for p in prs:
                bI = 4 + p % 2
                mm(pf[bI][:, 128:256], s_sel[:, p * 128:(p + 1) * 128], sEf, True, True,
                   rb=["c_sel", "sEf"], wb=[PSB[bI]], tag="Ebm")
                mm(pf[bI][:, 0:128], HT[seq][:, p * 128:(p + 1) * 128], sCf[:, qs],
                   True, True, rb=[f"HT{seq}", "sCf"], wb=[PSB[bI]], tag="Yi")
                E("scalar", lambda e, d=sEb[p % 2], pp=pf[bI][:, 128:256]:
                  e.activation(out=d, in_=pp, func=AF.Copy),
                  r=[PSB[bI]], w=[f"sEb{p % 2}"], tag="Ebe")
                E("vector", lambda e, d=sT1[p % 2], pp=pf[bI][:, 0:128], eb=sEb[p % 2]:
                  e.tensor_tensor(out=d, in0=pp, in1=eb, op=ALU.mult),
                  r=[PSB[bI], f"sEb{p % 2}"], w=[f"sT1{p % 2}"], tag="t1")
            for h in hs4:
                p = h // 2
                hi = h % 2
                bP = 2 + p % 2
                mm(pf[bP][hi * 64:(hi + 1) * 64, 0:128],
                   xsT[p][:, hi * 64:(hi + 1) * 64], sM2[h % 4], True, True,
                   rb=[f"xsT{p}", f"sM2{h % 4}"], wb=[PSB[bP]],
                   tp=(0, hi * 64), tag="Ym")
            for h in hs4:
                p = h // 2
                hi = h % 2
                bP = 2 + p % 2
                E("vector", lambda e, d=sBh[h % 2], sc1=swd[:, h:h + 1]:
                  e.tensor_scalar(out=d, in0=sBT, scalar1=sc1, scalar2=None,
                                  op0=ALU.mult),
                  r=["sBT", "swd"], w=[f"sBh{h % 2}"], tag="bh")
                mm(pf[bP][0:64, 128 + hi * 64:128 + (hi + 1) * 64], sBh[h % 2],
                   xsT[p][:, hi * 64:(hi + 1) * 64], True, True,
                   rb=[f"sBh{h % 2}", f"xsT{p}"], wb=[PSB[bP]], tag="dH")
                E("vector", lambda e, d=HT[seq][:, h * 64:(h + 1) * 64],
                  sc1=sLb[:, h:h + 1],
                  pp=pf[bP][0:64, 128 + hi * 64:128 + (hi + 1) * 64]:
                  e.scalar_tensor_tensor(out=d, in0=d, scalar=sc1, in1=pp,
                                         op0=ALU.mult, op1=ALU.add),
                  r=[f"HT{seq}", "sLb", PSB[bP]], w=[f"HT{seq}"], tag="Hup")
            for p in prs:
                bP = 2 + p % 2
                E("vector", lambda e, d=sE2[p % 2], t1=sT1[p % 2], pp=pf[bP][:, 0:128]:
                  e.tensor_tensor(out=d, in0=t1, in1=pp, op=ALU.add),
                  r=[f"sT1{p % 2}", PSB[bP]], w=[f"sE2{p % 2}"], tag="e2")
                E("vector", lambda e, d=xst[p][:, qs], sc1=s_dpr[:, p:p + 1],
                  e2=sE2[p % 2]:
                  e.scalar_tensor_tensor(out=d, in0=d, scalar=sc1, in1=e2,
                                         op0=ALU.mult, op1=ALU.add),
                  r=[f"xst{p}", "c_dpr", f"sE2{p % 2}"], w=[f"xst{p}"], tag="e3")

    def emit_xld(st):
        par = st % 2
        cols = slice(st * ST, (st + 1) * ST)
        E("sync", lambda e, d=xavP[par].rearrange("p (k t) -> p k t", k=NF),
          s=xfm[:, cols].rearrange("(k p) t -> p k t", p=128):
          e.dma_start(out=d, in_=s), r=["aarena"],
          w=[f"xa{par}_{k}" for k in range(NF)], dma=True, tag="xld")

    def p1a_supertile(st):
        par = st % 2
        xa = xaP[par]
        cols = slice(st * ST, (st + 1) * ST)
        seq = st // 8
        if st == 0:
            emit_xld(0)
        if st + 1 < NST:
            emit_xld(st + 1)
        stats_pass(xa, [f"xa{par}_{k}" for k in range(NF)], "s1")
        ln_rows(DM, "r1")
        if DEBUG and st == 0:
            for di, (rr, rn) in enumerate([(r_m, "r_m"), (r_ms, "r_ms"), (r_v, "r_v"),
                                           (r_r1, "r_r1"), (r_mr, "r_mr"), (r_t, "r_t")]):
                E("gpsimd", lambda e, d=d_dbg[di:di + 1, :], s=rr:
                  e.dma_start(out=d, in_=s), r=[rn], w=["d_dbg"], dma=True, tag="dbg")
            E("vector", lambda e: e.tensor_copy(out=xtmp[0], in_=xsq[1]),
              r=["xsq1"], w=["xtmp0"], tag="dbgc")
            E("gpsimd", lambda e: e.dma_start(out=d_dbg2, in_=xtmp[0]),
              r=["xtmp0"], w=["d_dbg2"], dma=True, tag="dbg2")
            E("vector", lambda e: e.tensor_copy(out=xtmp[1], in_=xsq[0]),
              r=["xsq0"], w=["xtmp1"], tag="dbgc2")
            E("gpsimd", lambda e: e.dma_start(out=d_dbg3, in_=xtmp[1]),
              r=["xtmp1"], w=["d_dbg3"], dma=True, tag="dbg3")
        if PREFIX:
            E("gpsimd", lambda e: e.dma_start(out=d_dbg2, in_=xa[7]),
              r=["xa7"], w=["d_dbg2"], dma=True, tag="dbgxa")
            E("vector", lambda e: e.tensor_copy(out=xtmp[0], in_=xsq[1]),
              r=["xsq1"], w=["xtmp0"], tag="dbq")
            E("gpsimd", lambda e: e.dma_start(out=d_dbg3, in_=xtmp[0]),
              r=["xtmp0"], w=["d_dbg3"], dma=True, tag="dbq2")
            return
        bcast_row(r_r1, "r_r1", 4, "bc_r1")
        bcast_row(r_mr, "r_mr", 5, "bc_mr1")
        for k in range(NF):
            E("vector", lambda e, d=xa[k], p=pf[4][:, 0:ST]:
              e.tensor_tensor(out=d, in0=d, in1=p, op=ALU.mult),
              r=[f"xa{par}_{k}", PSB[4]], w=[f"xa{par}_{k}"], tag="xn1")
            E("vector", lambda e, d=xa[k], p=pf[5][:, 0:ST]:
              e.tensor_tensor(out=d, in0=d, in1=p, op=ALU.subtract),
              r=[f"xa{par}_{k}", PSB[5]], w=[f"xa{par}_{k}"], tag="xn2")
        for k in range(NF):
            mm(pf[3][0:8, 0:ST], s_gtc[k], xa[k], k == 0, k == NF - 1,
               rb=[f"gtc{k}", f"xa{par}_{k}"], wb=[PSB[3]], tag="gl")
        E("vector", lambda e, p=pf[3][0:8, 0:ST]: e.tensor_copy(out=g_lf, in_=p),
          r=[PSB[3]], w=["g_lf"], tag="glev")
        for k in range(NF):
            E("vector", lambda e, d=xns[:, k * ST:(k + 1) * ST], s=xa[k]:
              e.tensor_copy(out=d, in_=s), r=[f"xa{par}_{k}"], w=[f"xns{k}"], tag="xnb")
        E("gpsimd", lambda e, d=d_xn[:, cols].rearrange("(k p) t -> p k t", p=128),
          s=xns.rearrange("p (k t) -> p k t", k=NF): e.dma_start(out=d, in_=s),
          r=[f"xns{k}" for k in range(NF)],
          w=[f"dxn{st}_{k}" for k in range(NF)], dma=True, tag="xnsp")
        E("vector", lambda e: e.tensor_tensor(out=r_t, in0=r_r1, in1=r_r1, op=ALU.mult),
          r=["r_r1"], w=["r_t"], tag="r2a")
        E("vector", lambda e: e.tensor_tensor(out=r_v, in0=r_v, in1=r_t, op=ALU.mult),
          r=["r_v", "r_t"], w=["r_v"], tag="r2b")
        E("scalar", lambda e: e.activation(out=r_t, in_=r_v, func=AF.Ln, bias=EPS),
          r=["r_v", "c_fc"], w=["r_t"], tag="r2c")
        E("scalar", lambda e: e.activation(out=r_r2, in_=r_t, func=AF.Exp, scale=-0.5),
          r=["r_t", "c_fc"], w=["r_r1"], tag="r2d")
        bcast_row(r_r2, "r_r1", 4, "bc_r2")
        for k in range(NF):
            E("vector", lambda e, d=xa[k], p=pf[4][:, 0:ST]:
              e.tensor_tensor(out=d, in0=d, in1=p, op=ALU.mult),
              r=[f"xa{par}_{k}", PSB[4]], w=[f"xa{par}_{k}"], tag="l0a")
            E("vector", lambda e, d=xa[k], w1=s_lnc[:, k:k + 1], bb=s_lnc[:, 8 + k:9 + k]:
              e.tensor_scalar(out=d, in0=d, scalar1=w1, scalar2=bb,
                              op0=ALU.mult, op1=ALU.add),
              r=[f"xa{par}_{k}", "c_lnc"], w=[f"xa{par}_{k}"], tag="l0b")
        for h2 in range(2):
            sl = slice(h2 * 128, (h2 + 1) * 128)
            tpose(pf[5][:, 384:392], g_lf[:, sl], s_id_f[0:8, 0:8],
                  rb=["g_lf", "c_idf"], wb=[PSB[5]], tag="glt")
            E("vector", lambda e, d=g_lt[h2], p=pf[5][:, 384:392]:
              e.tensor_copy(out=d, in_=p), r=[PSB[5]], w=[f"glt{h2}"], tag="glte")
        for h2 in range(2):
            lt = g_lt[h2]
            lb = f"glt{h2}"
            E("vector", lambda e, s=lt: e.tensor_reduce(out=g_m1, in_=s, axis=AX.X,
                                                        op=ALU.max),
              r=[lb], w=["gm1"], tag="gm1")
            E("vector", lambda e, s=lt: e.tensor_scalar(out=g_d, in0=s, scalar1=g_m1,
                                                        scalar2=None, op0=ALU.subtract),
              r=[lb, "gm1"], w=["gd"], tag="gd")
            E("vector", lambda e: e.tensor_scalar(out=g_sp, in0=g_d, scalar1=0.0,
                                                  scalar2=-1e30, op0=ALU.is_ge,
                                                  op1=ALU.mult),
              r=["gd"], w=["gsp"], tag="gsp")
            E("vector", lambda e, s=lt: e.tensor_tensor(out=g_sp, in0=g_sp, in1=s,
                                                        op=ALU.add),
              r=["gsp", lb], w=["gsp"], tag="gsp2")
            E("vector", lambda e: e.tensor_reduce(out=g_m1, in_=g_sp, axis=AX.X,
                                                  op=ALU.max),
              r=["gsp"], w=["gm1"], tag="gm2")
            E("vector", lambda e, s=lt: e.tensor_scalar(out=g_sp, in0=s, scalar1=g_m1,
                                                        scalar2=None, op0=ALU.subtract),
              r=[lb, "gm1"], w=["gsp"], tag="gd2")
            E("vector", lambda e: e.tensor_scalar(out=g_sp, in0=g_sp, scalar1=0.0,
                                                  scalar2=None, op0=ALU.is_ge),
              r=["gsp"], w=["gsp"], tag="gmask")
            E("scalar", lambda e: e.activation(out=g_ex, in_=g_d, func=AF.Exp),
              r=["gd"], w=["gex"], tag="gex")
            E("vector", lambda e: e.tensor_reduce(out=g_ss, in_=g_ex, axis=AX.X,
                                                  op=ALU.add),
              r=["gex"], w=["gss"], tag="gss")
            E("vector", lambda e: e.reciprocal(out=g_ss, in_=g_ss),
              r=["gss"], w=["gss"], tag="grc")
            E("vector", lambda e: e.tensor_scalar(out=g_ex, in0=g_ex, scalar1=g_ss,
                                                  scalar2=None, op0=ALU.mult),
              r=["gex", "gss"], w=["gex"], tag="gsc")
            E("vector", lambda e, d=g_w[h2]: e.tensor_tensor(out=d, in0=g_ex, in1=g_sp,
                                                             op=ALU.mult),
              r=["gex", "gsp"], w=[f"gw{h2}"], tag="gw")
            tpose(pf[5][0:8, 384:512], g_w[h2], s_id_f,
                  rb=[f"gw{h2}", "c_idf"], wb=[PSB[5]], tag="gwt")
            E("vector", lambda e, d=g_wf[:, h2 * 128:(h2 + 1) * 128], p=pf[5][0:8, 384:512]:
              e.tensor_copy(out=d, in_=p), r=[PSB[5]], w=["g_lf"], tag="gwfe")
        E("gpsimd", lambda e, d=d_w[:, cols], s=g_wf: e.dma_start(out=d, in_=s),
          r=["g_lf"], w=[f"dw{st}"], dma=True, tag="wsp")
        stats_pass(xa, [f"xa{par}_{k}" for k in range(NF)], "s3")
        ln_rows(DM, "r3")
        bcast_row(r_r1, "r_r1", 4, "bc_r3")
        bcast_row(r_mr, "r_mr", 5, "bc_mr3")
        for k in range(NF):
            tm = xtmp[k % 2]
            E("vector", lambda e, d=tm, s=xa[k], p=pf[4][:, 0:ST]:
              e.tensor_tensor(out=d, in0=s, in1=p, op=ALU.mult),
              r=[f"xa{par}_{k}", PSB[4]], w=[f"xtmp{k % 2}"], tag="l1a")
            E("vector", lambda e, d=tm, p=pf[5][:, 0:ST]:
              e.tensor_tensor(out=d, in0=d, in1=p, op=ALU.subtract),
              r=[f"xtmp{k % 2}", PSB[5]], w=[f"xtmp{k % 2}"], tag="l1b")
            E("vector", lambda e, d=xmb[k], s=tm, w1=s_lnc[:, 16 + k:17 + k],
              bb=s_lnc[:, 24 + k:25 + k]:
              e.tensor_scalar(out=d, in0=s, scalar1=w1, scalar2=bb,
                              op0=ALU.mult, op1=ALU.add),
              r=[f"xtmp{k % 2}", "c_lnc"], w=[f"xmb{k}"], tag="l1c")
        for o in range(34):
            o0 = o * 128
            o1 = min(DIP, o0 + 128)
            ow = o1 - o0
            bank = o % 3
            for k in range(NF):
                mm(pf[bank][0:ow, 0:ST], ipW[k][:, o0:o1], xmb[k], k == 0, k == NF - 1,
                   rb=[f"ipW{k}", f"xmb{k}", "warena"], wb=[PSB[bank]], tag="ipm")
            if o < 16:
                E("scalar", lambda e, d=zt[o], p=pf[bank][0:128, 0:ST]:
                  e.activation(out=d, in_=p, func=AF.Silu),
                  r=[PSB[bank]], w=[f"zt{o}"], tag="zev")
            elif o < 33:
                j = o - 16
                E("scalar", lambda e, d=xBCr[j][:, 3:3 + ST], p=pf[bank][0:128, 0:ST]:
                  e.activation(out=d, in_=p, func=AF.Copy),
                  r=[PSB[bank]], w=[f"xBC{j}"], tag="xbcev")
            else:
                E("scalar", lambda e, p=pf[bank][0:32, 0:ST]:
                  e.activation(out=dtraw, in_=p, func=AF.Copy),
                  r=[PSB[bank]], w=["dtraw"], tag="dtev")
        if st == 0 or st == 8:
            for j in range(NCV):
                E("vector", lambda e, d=xBCr[j][:, 0:3]: e.memset(d, 0.0),
                  w=[f"xBC{j}"], tag="cvz")
        for j in range(NCV):
            dst = xst[j] if j < 16 else bct
            dn = f"xst{j}" if j < 16 else "bct"
            cs = xsq[j % 2]
            E("vector", lambda e, d=cs, s=xBCr[j][:, 0:ST], w1=s_cwc[:, 4 * j:4 * j + 1]:
              e.tensor_scalar(out=d, in0=s, scalar1=w1, scalar2=None, op0=ALU.mult),
              r=[f"xBC{j}", "c_cwc"], w=[f"xsq{j % 2}"], tag="cv0")
            for tap in (1, 2, 3):
                E("vector", lambda e, d=cs, s=xBCr[j][:, tap:tap + ST],
                  w1=s_cwc[:, 4 * j + tap:4 * j + tap + 1]:
                  e.scalar_tensor_tensor(out=d, in0=s, scalar=w1, in1=d,
                                         op0=ALU.mult, op1=ALU.add),
                  r=[f"xBC{j}", f"xsq{j % 2}", "c_cwc"], w=[f"xsq{j % 2}"], tag="cvt")
            E("scalar", lambda e, d=dst, s=cs, bb=s_cbc[:, j:j + 1]:
              e.activation(out=d, in_=s, func=AF.Silu, bias=bb),
              r=[f"xsq{j % 2}", "c_cbc"], w=[dn], tag="cvs")
            E("gpsimd", lambda e, d=xBCr[j][:, 0:3], s=xBCr[j][:, ST:ST + 3]:
              e.tensor_copy(out=d, in_=s), r=[f"xBC{j}"], w=[f"xBC{j}"], tag="cvh")
        E("sync", lambda e: e.dma_start(out=sCf, in_=bct[64:128, :]),
          r=["bct"], w=["sCf"], dma=True, tag="crel")
        E("scalar", lambda e: e.activation(out=dtt, in_=dtraw, func=AF.Abs, bias=s_dtb),
          r=["dtraw", "c_dtb"], w=["dtt"], tag="dt1")
        E("scalar", lambda e: e.activation(out=dtt, in_=dtt, func=AF.Exp, scale=-1.0),
          r=["dtt", "c_fc"], w=["dtt"], tag="dt2")
        E("scalar", lambda e: e.activation(out=dtt, in_=dtt, func=AF.Ln, bias=1.0),
          r=["dtt", "c_fc"], w=["dtt"], tag="dt3")
        E("scalar", lambda e: e.activation(out=dtf, in_=dtraw, func=AF.Relu, bias=s_dtb),
          r=["dtraw", "c_dtb"], w=["dtf"], tag="dt4")
        E("vector", lambda e: e.tensor_tensor(out=dtf, in0=dtf, in1=dtt, op=ALU.add),
          r=["dtf", "dtt"], w=["dtf"], tag="dt5")
        E("vector", lambda e: e.tensor_scalar(out=afm, in0=dtf, scalar1=s_acol,
                                              scalar2=None, op0=ALU.mult),
          r=["dtf", "acol"], w=["dtraw"], tag="dta")
        for qq in range(2):
            ssd_chunk(st, qq, seq)
        for j in range(NPAIR):
            E("vector", lambda e, d=xst[j], s=zt[j]:
              e.tensor_tensor(out=d, in0=d, in1=s, op=ALU.mult),
              r=[f"xst{j}", f"zt{j}"], w=[f"xst{j}"], tag="gat")
        for j in range(NPAIR):
            sq_ = xsq[j % 2]
            E("scalar", lambda e, d=sq_, s=xst[j]: e.activation(out=d, in_=s,
                                                                func=AF.Square),
              r=[f"xst{j}"], w=[f"xsq{j % 2}"], tag="rmq")
            mm(pf[4][0:1, 0:ST], s_ocb, sq_, j == 0, j == NPAIR - 1,
               rb=["c_ocb", f"xsq{j % 2}"], wb=[PSB[4]], tag="rmm")
        E("scalar", lambda e, p=pf[4][0:1, 0:ST]:
          e.activation(out=r_t, in_=p, func=AF.Ln, scale=1.0 / DI, bias=EPS),
          r=[PSB[4], "c_fc"], w=["r_t"], tag="rml")
        E("scalar", lambda e: e.activation(out=r_rr, in_=r_t, func=AF.Exp, scale=-0.5),
          r=["r_t", "c_fc"], w=["r_r1"], tag="rme")
        bcast_row(r_rr, "r_r1", 5, "bc_rms")
        for j in range(NPAIR):
            E("vector", lambda e, d=zt[j], s=xst[j], w1=s_nwp[:, j:j + 1], p=pf[5][:, 0:ST]:
              e.scalar_tensor_tensor(out=d, in0=s, scalar=w1, in1=p,
                                     op0=ALU.mult, op1=ALU.mult),
              r=[f"xst{j}", "c_nwp", PSB[5]], w=[f"zt{j}"], tag="rma")
        for o in range(NF):
            bank = o % 3
            for k in range(16):
                mm(pf[bank][:, 0:ST], opW[k][:, o * 128:(o + 1) * 128], zt[k],
                   k == 0, k == 15, rb=[f"opW{k}", f"zt{k}", "warena"],
                   wb=[PSB[bank]], tag="opm")
            E("vector", lambda e, d=xa[o], p=pf[bank][:, 0:ST]:
              e.tensor_tensor(out=d, in0=d, in1=p, op=ALU.add),
              r=[f"xa{par}_{o}", PSB[bank]], w=[f"xa{par}_{o}"], tag="opev")
        E("gpsimd", lambda e, d=d_xb[:, cols].rearrange("(k p) t -> p k t", p=128),
          s=xavP[par].rearrange("p (k t) -> p k t", k=NF): e.dma_start(out=d, in_=s),
          r=[f"xa{par}_{k}" for k in range(NF)],
          w=[f"dxb{st}_{k}" for k in range(NF)], dma=True, tag="xbsp")
        stats_pass(xa, [f"xa{par}_{k}" for k in range(NF)], "s4")
        ln_rows(DM, "r4")
        bcast_row(r_r1, "r_r1", 4, "bc_r4")
        bcast_row(r_mr, "r_mr", 5, "bc_mr4")
        for k in range(NF):
            tm = xtmp[k % 2]
            E("vector", lambda e, d=tm, s=xa[k], p=pf[4][:, 0:ST]:
              e.tensor_tensor(out=d, in0=s, in1=p, op=ALU.mult),
              r=[f"xa{par}_{k}", PSB[4]], w=[f"xtmp{k % 2}"], tag="l2a")
            E("vector", lambda e, d=tm, p=pf[5][:, 0:ST]:
              e.tensor_tensor(out=d, in0=d, in1=p, op=ALU.subtract),
              r=[f"xtmp{k % 2}", PSB[5]], w=[f"xtmp{k % 2}"], tag="l2b")
            E("vector", lambda e, d=xmb[k], s=tm, w1=s_lnc[:, 32 + k:33 + k],
              bb=s_lnc[:, 40 + k:41 + k]:
              e.tensor_scalar(out=d, in0=s, scalar1=w1, scalar2=bb,
                              op0=ALU.mult, op1=ALU.add),
              r=[f"xtmp{k % 2}", "c_lnc"], w=[f"xmb{k}"], tag="l2c")
        E("gpsimd", lambda e, d=d_xc[:, cols].rearrange("(k p) t -> p k t", p=128),
          s=xmbv.rearrange("p (k t) -> p k t", k=NF): e.dma_start(out=d, in_=s),
          r=[f"xmb{k}" for k in range(NF)],
          w=[f"dxc{st}_{k}" for k in range(NF)], dma=True, tag="xcsp")

    def p1b():
        STB = 512
        E("vector", lambda e: e.engine_nop(), w=["warena"], deps=B.all_tokens(), tag="wgate1b")
        E("vector", lambda e: e.engine_nop(), w=["aarena"], deps=B.all_tokens(), tag="agate1b")
        E("sync", lambda e, d=f1Wm.rearrange("p (k c) -> p k c", k=NF),
          s=d_f1.rearrange("(k p) c -> p k c", p=128): e.dma_start(out=d, in_=s),
          r=["warena", "w_f1"], w=[f"f1W{k}" for k in range(NF)], dma=True, tag="f1ld")
        E("sync", lambda e, d=f2Wm.rearrange("p (k c) -> p k c", k=32),
          s=d_f2.rearrange("(k p) c -> p k c", p=128): e.dma_start(out=d, in_=s),
          r=["warena", "w_f2"], w=[f"f2W{k}" for k in range(32)], dma=True, tag="f2ld")
        for st in range(8):
            cols = slice(st * STB, (st + 1) * STB)
            sts = (2 * st, 2 * st + 1)
            E("sync", lambda e, d=xcbv.rearrange("p (k t) -> p k t", k=NF),
              s=d_xc[:, cols].rearrange("(k p) t -> p k t", p=128):
              e.dma_start(out=d, in_=s),
              r=["aarena"] + [f"dxc{s_}_{k}" for s_ in sts for k in range(NF)],
              w=[f"xcb{k}" for k in range(NF)], dma=True, tag="xcld")
            E("sync", lambda e, d=xbfv.rearrange("p (k t) -> p k t", k=NF),
              s=d_xb[:, cols].rearrange("(k p) t -> p k t", p=128):
              e.dma_start(out=d, in_=s),
              r=["aarena"] + [f"dxb{s_}_{k}" for s_ in sts for k in range(NF)],
              w=[f"xbf{k}" for k in range(NF)], dma=True, tag="xbld")
            for o in range(32):
                bank = o % 3
                for k in range(NF):
                    mm(pf[bank][:, 0:STB], f1W[k][:, o * 128:(o + 1) * 128], xcb[k],
                       k == 0, k == NF - 1,
                       rb=[f"f1W{k}", f"xcb{k}", "warena"], wb=[PSB[bank]], tag="f1m")
                E("scalar", lambda e, d=g1b[o], p=pf[bank][:, 0:STB],
                  bb=s_b1c[:, o:o + 1]:
                  e.activation(out=d, in_=p, func=AF.Gelu, bias=bb),
                  r=[PSB[bank], "c_b1c"], w=[f"g1b{o}"], tag="gelu")
            for o in range(NF):
                bank = o % 3
                for k in range(32):
                    mm(pf[bank][:, 0:STB], f2W[k][:, o * 128:(o + 1) * 128], g1b[k],
                       k == 0, k == 31,
                       rb=[f"f2W{k}", f"g1b{k}", "warena"], wb=[PSB[bank]], tag="f2m")
                E("vector", lambda e, d=xbf[o], p=pf[bank][:, 0:STB],
                  bb=s_b2c[:, o:o + 1]:
                  e.scalar_tensor_tensor(out=d, in0=p, scalar=bb, in1=d,
                                         op0=ALU.add, op1=ALU.add),
                  r=[PSB[bank], "c_b2c", f"xbf{o}"], w=[f"xbf{o}"], tag="f2ev")
            E("gpsimd", lambda e, d=d_s[:, cols].rearrange("(k p) t -> p k t", p=128),
              s=xbfv.rearrange("p (k t) -> p k t", k=NF): e.dma_start(out=d, in_=s),
              r=[f"xbf{k}" for k in range(NF)],
              w=[f"ds{s_}_{k}" for s_ in sts for k in range(NF)], dma=True, tag="ssp")

    def p2():
        E("vector", lambda e: e.engine_nop(), w=["warena"], deps=B.all_tokens(), tag="wgate2")
        E("vector", lambda e: e.engine_nop(), w=["aarena"], deps=B.all_tokens(), tag="agate2")

        def load_gu(ee):
            E("sync", lambda e, d=guWm.rearrange("p (k c) -> p k c", k=NF),
              s=d_gu[ee].rearrange("(k p) c -> p k c", p=128):
              e.dma_start(out=d, in_=s), r=["warena", f"w_gu{ee}"],
              w=[f"guW{k}" for k in range(NF)], dma=True, tag="guld")

        def load_dn(ee, pp):
            E("sync", lambda e, d=dnWm[pp].rearrange("p (k c) -> p k c", k=NMI),
              s=d_dn[ee].rearrange("(k p) c -> p k c", p=128):
              e.dma_start(out=d, in_=s), r=["warena", f"w_dn{ee}"],
              w=[f"dnW{pp}_{k}" for k in range(NMI)], dma=True, tag="dnld")

        for qt in range(NQ):
            qcols = slice(qt * QT, (qt + 1) * QT)
            wsts = [qt * 4 + i for i in range(4)]
            E("sync", lambda e, d=p2w, s=d_w[:, qcols]: e.dma_start(out=d, in_=s),
              r=["aarena"] + [f"dw{s_}" for s_ in wsts], w=["p2w"], dma=True, tag="p2wld")
            for ee in range(NE):
                if qt == 0 and ee == 0:
                    load_gu(0)
                    load_dn(0, 0)
                for tt2 in range(2):
                    tcols = slice(qt * QT + tt2 * TT, qt * QT + (tt2 + 1) * TT)
                    xsts = [(qt * QT + tt2 * TT) // ST, (qt * QT + tt2 * TT) // ST + 1]
                    E("sync", lambda e, d=p2xnv.rearrange("p (k t) -> p k t", k=NF),
                      s=d_xn[:, tcols].rearrange("(k p) t -> p k t", p=128):
                      e.dma_start(out=d, in_=s),
                      r=["aarena"] + [f"dxn{s_}_{k}" for s_ in xsts for k in range(NF)],
                      w=[f"p2xn{k}" for k in range(NF)], dma=True, tag="p2xld")
                    for j in range(NMI):
                        for k in range(NF):
                            mm(pf[0][:, 0:TT], guW[k][:, j * 128:(j + 1) * 128], p2xn[k],
                               k == 0, k == NF - 1,
                               rb=[f"guW{k}", f"p2xn{k}", "warena"], wb=[PSB[0]],
                               tag="gum")
                        for k in range(NF):
                            mm(pf[1][:, 0:TT], guW[k][:, MI + j * 128:MI + (j + 1) * 128],
                               p2xn[k], k == 0, k == NF - 1,
                               rb=[f"guW{k}", f"p2xn{k}", "warena"], wb=[PSB[1]],
                               tag="guu")
                        E("scalar", lambda e, d=p2h[tt2][j], p=pf[0][:, 0:TT]:
                          e.activation(out=d, in_=p, func=AF.Silu),
                          r=[PSB[0]], w=[f"p2h{tt2}_{j}"], tag="hsl")
                        E("vector", lambda e, d=p2h[tt2][j], p=pf[1][:, 0:TT]:
                          e.tensor_tensor(out=d, in0=d, in1=p, op=ALU.mult),
                          r=[f"p2h{tt2}_{j}", PSB[1]], w=[f"p2h{tt2}_{j}"], tag="hml")
                ne_, nq_ = (ee + 1, qt) if ee < NE - 1 else (0, qt + 1)
                if nq_ < NQ:
                    load_gu(ne_)
                    load_dn(ne_, ne_ % 2)
                pp_ = ee % 2
                for tt2 in range(2):
                    mm(pf[4][:, 0:TT], s_oh8[:, ee * 128:(ee + 1) * 128],
                       p2w[:, tt2 * TT:(tt2 + 1) * TT], True, True,
                       rb=["c_oh8", "p2w"], wb=[PSB[4]], tag="wbm")
                    E("vector", lambda e, p=pf[4][:, 0:TT]: e.tensor_copy(out=p2wbc, in_=p),
                      r=[PSB[4]], w=["p2wbc"], tag="wbe")
                    for o in range(NF):
                        bank = 2 + o % 2
                        for k in range(NMI):
                            mm(pf[bank][:, 0:TT], dnW[pp_][k][:, o * 128:(o + 1) * 128],
                               p2h[tt2][k], k == 0, k == NMI - 1,
                               rb=[f"dnW{pp_}_{k}", f"p2h{tt2}_{k}", "warena"],
                               wb=[PSB[bank]], tag="dnm")
                        asl = slice(tt2 * TT, (tt2 + 1) * TT)
                        if ee == 0:
                            E("vector", lambda e, d=p2acc[o][:, asl], p=pf[bank][:, 0:TT]:
                              e.tensor_tensor(out=d, in0=p, in1=p2wbc, op=ALU.mult),
                              r=[PSB[bank], "p2wbc"], w=[f"p2acc{o}"], tag="cb0")
                        else:
                            E("vector", lambda e, p=pf[bank][:, 0:TT]:
                              e.tensor_tensor(out=p2tmp, in0=p, in1=p2wbc, op=ALU.mult),
                              r=[PSB[bank], "p2wbc"], w=["p2tmp"], tag="cbm")
                            E("gpsimd", lambda e, d=p2acc[o][:, asl]:
                              e.tensor_tensor(out=d, in0=d, in1=p2tmp, op=ALU.add),
                              r=[f"p2acc{o}", "p2tmp"], w=[f"p2acc{o}"], tag="cba")
            E("gpsimd", lambda e, d=d_acc[:, qcols].rearrange("(k p) t -> p k t", p=128),
              s=p2accv.rearrange("p (k t) -> p k t", k=NF): e.dma_start(out=d, in_=s),
              r=[f"p2acc{k}" for k in range(NF)],
              w=[f"dacc{qt}_{k}" for k in range(NF)], dma=True, tag="accsp")

    def p3():
        E("vector", lambda e: e.engine_nop(), w=["warena"], deps=B.all_tokens(), tag="wgate3")
        E("vector", lambda e: e.engine_nop(), w=["aarena"], deps=B.all_tokens(), tag="agate3")
        E("sync", lambda e, d=sgWm.rearrange("p (k c) -> p k c", k=NF),
          s=d_sg.rearrange("(k p) c -> p k c", p=128): e.dma_start(out=d, in_=s),
          r=["warena", "w_sg"], w=[f"sgW{k}" for k in range(NF)], dma=True, tag="sgld")
        E("sync", lambda e, d=suWm.rearrange("p (k c) -> p k c", k=NF),
          s=d_su.rearrange("(k p) c -> p k c", p=128): e.dma_start(out=d, in_=s),
          r=["warena", "w_su"], w=[f"suW{k}" for k in range(NF)], dma=True, tag="suld")
        E("sync", lambda e, d=sdWm.rearrange("p (k c) -> p k c", k=NSI),
          s=d_sd.rearrange("(k p) c -> p k c", p=128): e.dma_start(out=d, in_=s),
          r=["warena", "w_sd"], w=[f"sdW{k}" for k in range(NSI)], dma=True, tag="sdld")
        for st in range(NST):
            cols = slice(st * ST, (st + 1) * ST)
            qt = st // 4
            for (dst, srct, rl, wpfx, tg) in [
                (p3xnv, d_xn, [f"dxn{st}_{k}" for k in range(NF)], "p3xn", "p3xnld"),
                (p3sv, d_s, [f"ds{st}_{k}" for k in range(NF)], "p3s", "p3sld"),
                (p3xv, xfm, [], "p3x", "p3xld"),
                (p3acv, d_acc, [f"dacc{qt}_{k}" for k in range(NF)], "p3ac", "p3acld"),
            ]:
                E("sync", lambda e, d=dst.rearrange("p (k t) -> p k t", k=NF),
                  s=srct[:, cols].rearrange("(k p) t -> p k t", p=128):
                  e.dma_start(out=d, in_=s), r=["aarena"] + rl,
                  w=[f"{wpfx}{k}" for k in range(NF)], dma=True, tag=tg)
            for j in range(NSI):
                b0 = (j % 2) * 2
                b1_ = b0 + 1
                for k in range(NF):
                    mm(pf[b0][:, 0:ST], sgW[k][:, j * 128:(j + 1) * 128], p3xn[k],
                       k == 0, k == NF - 1,
                       rb=[f"sgW{k}", f"p3xn{k}", "warena"], wb=[PSB[b0]], tag="shg")
                for k in range(NF):
                    mm(pf[b1_][:, 0:ST], suW[k][:, j * 128:(j + 1) * 128], p3xn[k],
                       k == 0, k == NF - 1,
                       rb=[f"suW{k}", f"p3xn{k}", "warena"], wb=[PSB[b1_]], tag="shu")
                E("scalar", lambda e, d=p3hs[j], p=pf[b0][:, 0:ST]:
                  e.activation(out=d, in_=p, func=AF.Silu),
                  r=[PSB[b0]], w=[f"p3hs{j}"], tag="shs")
                E("vector", lambda e, d=p3hs[j], p=pf[b1_][:, 0:ST]:
                  e.tensor_tensor(out=d, in0=d, in1=p, op=ALU.mult),
                  r=[f"p3hs{j}", PSB[b1_]], w=[f"p3hs{j}"], tag="shm")
            for k in range(NF):
                mm(pf[4][0:1, 0:ST], s_segb[:, k:k + 1], p3xn[k], k == 0, k == NF - 1,
                   rb=["segb", f"p3xn{k}"], wb=[PSB[4]], tag="swm")
            E("scalar", lambda e, p=pf[4][0:1, 0:ST]:
              e.activation(out=p3sw, in_=p, func=AF.Tanh, scale=0.5),
              r=[PSB[4], "c_fc"], w=["p3sw"], tag="swt")
            E("vector", lambda e: e.tensor_scalar(out=p3sw, in0=p3sw, scalar1=1.0,
                                                  scalar2=0.5, op0=ALU.add, op1=ALU.mult),
              r=["p3sw"], w=["p3sw"], tag="sws")
            mm(pf[5][:, 0:ST], s_o1f, p3sw, True, True,
               rb=["c_o1f", "p3sw"], wb=[PSB[5]], tag="swb")
            E("vector", lambda e, p=pf[5][:, 0:ST]: e.tensor_copy(out=p3swb, in_=p),
              r=[PSB[5]], w=["p3swb"], tag="swbe")
            for o in range(NF):
                bank = o % 2
                for k in range(NSI):
                    mm(pf[bank][:, 0:ST], sdW[k][:, o * 128:(o + 1) * 128], p3hs[k],
                       k == 0, k == NSI - 1,
                       rb=[f"sdW{k}", f"p3hs{k}", "warena"], wb=[PSB[bank]], tag="sdm")
                E("vector", lambda e, d=p3g[o], p=pf[bank][:, 0:ST]:
                  e.tensor_tensor(out=d, in0=p, in1=p3swb, op=ALU.mult),
                  r=[PSB[bank], "p3swb"], w=[f"p3g{o}"], tag="sde")
                E("vector", lambda e, d=p3g[o], s=p3ac[o]:
                  e.tensor_tensor(out=d, in0=d, in1=s, op=ALU.add),
                  r=[f"p3g{o}", f"p3ac{o}"], w=[f"p3g{o}"], tag="sda")
                E("scalar", lambda e, d=p3g[o]: e.activation(out=d, in_=d, func=AF.Silu),
                  r=[f"p3g{o}"], w=[f"p3g{o}"], tag="gsl")
                E("vector", lambda e, d=p3g[o], s=p3s[o]:
                  e.tensor_tensor(out=d, in0=d, in1=s, op=ALU.mult),
                  r=[f"p3g{o}", f"p3s{o}"], w=[f"p3g{o}"], tag="gms")
                E("vector", lambda e, d=p3g[o], s=p3x[o]:
                  e.tensor_tensor(out=d, in0=d, in1=s, op=ALU.add),
                  r=[f"p3g{o}", f"p3x{o}"], w=[f"p3g{o}"], tag="gax")
            E("gpsimd", lambda e, d=ofm[:, cols].rearrange("(k p) t -> p k t", p=128),
              s=p3gv.rearrange("p (k t) -> p k t", k=NF): e.dma_start(out=d, in_=s),
              r=[f"p3g{k}" for k in range(NF)], w=["ofm"], dma=True, tag="outsp")


    for st in range(PREFIX if PREFIX else NST):
        p1a_supertile(st)
        if not PREFIX:
            p0_pump(26)
    if not PREFIX:
        p0_pump(10 ** 9)
    if not PREFIX:
        if PHASES in ("1b", "2", "full"):
            p1b()
        if PHASES in ("2", "full"):
            p2()
        if PHASES == "full":
            p3()

    B.finalize()

    with nc.Block() as block:
        @block.sync
        def _(e):
            B.run_engine("sync", e, sems)

        @block.tensor
        def _(e):
            B.run_engine("tensor", e, sems)

        @block.vector
        def _(e):
            B.run_engine("vector", e, sems)

        @block.scalar
        def _(e):
            B.run_engine("scalar", e, sems)

        @block.gpsimd
        def _(e):
            B.run_engine("gpsimd", e, sems)
            for s in ("dq_sync_0", "dq_sync_1", "dq_scalar_0", "dq_scalar_1",
                      "dq_gpsimd_0", "dq_gpsimd_1"):
                if B.cnt.get(s):
                    e.wait_ge(sems[s], B.cnt[s])

    stack.close()
    return nc


def _host_prep(inputs):
    x = np.asarray(inputs["L0_attn_x"])
    f = np.ascontiguousarray
    d = {}
    d["ipT"] = f(np.asarray(inputs["L0_attn_in_proj_w"]).T)
    d["opT"] = f(np.asarray(inputs["L0_attn_out_proj_w"]).T)
    d["f1T"] = f(np.asarray(inputs["L0_attn_ffn_w1"]).T)
    d["f2T"] = f(np.asarray(inputs["L0_attn_ffn_w2"]).T)
    d["guT"] = f(np.asarray(inputs["L0_ffn_experts_gate_up"]).transpose(0, 2, 1))
    d["dnT"] = f(np.asarray(inputs["L0_ffn_experts_down"]).transpose(0, 2, 1))
    d["sgT"] = f(np.asarray(inputs["L0_ffn_shared_gate_w"]).T)
    d["suT"] = f(np.asarray(inputs["L0_ffn_shared_up_w"]).T)
    d["sdT"] = f(np.asarray(inputs["L0_ffn_shared_down_w"]).T)
    d["gTc"] = f(np.asarray(inputs["L0_ffn_gate_w"]).T)
    d["segc"] = f(np.asarray(inputs["L0_ffn_shared_expert_gate_w"]).reshape(8, 128).T)
    ln = np.zeros((128, 48), np.float32)
    for i, nm in enumerate(["ln0_w", "ln0_b", "ln1_w", "ln1_b", "ln2_w", "ln2_b"]):
        ln[:, i * 8:(i + 1) * 8] = np.asarray(inputs["L0_attn_" + nm]).reshape(8, 128).T
    d["lnc"] = ln
    d["cwc"] = f(np.asarray(inputs["L0_attn_conv_w"]).reshape(17, 128, 4)
                 .transpose(1, 0, 2).reshape(128, 68))
    d["cbc"] = f(np.asarray(inputs["L0_attn_conv_b"]).reshape(17, 128).T)
    d["b1c"] = f(np.asarray(inputs["L0_attn_ffn_b1"]).reshape(32, 128).T)
    d["b2c"] = f(np.asarray(inputs["L0_attn_ffn_b2"]).reshape(8, 128).T)
    d["nwp"] = f(np.asarray(inputs["L0_attn_norm_w"]).reshape(16, 128).T)
    D = np.asarray(inputs["L0_attn_D_param"]).astype(np.float32)
    dp = np.zeros((128, 16), np.float32)
    for c in range(16):
        dp[0:64, c] = D[2 * c]
        dp[64:128, c] = D[2 * c + 1]
    d["dpr"] = dp
    d["dtb"] = f(np.asarray(inputs["L0_attn_dt_bias"]).reshape(32, 1))
    d["alg"] = f(np.asarray(inputs["L0_attn_A_log"]).reshape(32, 1))
    idx = np.arange(128)
    d["c_lt_b"] = (idx[:, None] <= idx[None, :]).astype(BFNP)
    d["c_ut_b"] = (idx[:, None] > idx[None, :]).astype(BFNP)
    d["c_id_b"] = np.eye(128, dtype=BFNP)
    d["c_id_f"] = np.eye(128, dtype=np.float32)
    sel = np.zeros((32, 2048), BFNP)
    for c in range(16):
        for p_ in range(128):
            sel[2 * c + (p_ >= 64), c * 128 + p_] = 1
    d["c_sel"] = sel
    d["c_o1f"] = np.ones((1, 128), np.float32)
    d["c_ocb"] = np.ones((128, 1), BFNP)
    oh = np.zeros((8, 1024), np.float32)
    for e_ in range(8):
        oh[e_, e_ * 128:(e_ + 1) * 128] = 1
    d["c_oh8"] = oh
    fc = np.zeros((128, 8), np.float32)
    for fi, fv in enumerate([1e-5, 1.0, -0.5, -1.0, 0.5, 1.0 / 2048.0, 1.0 / 1024.0, 2.0]):
        fc[:, fi] = fv
    d["c_fc"] = fc
    xcore = []
    for i in range(8):
        xf = np.empty((1024, 4096), np.float32)
        xf[:, :2048] = x[2 * i].T
        xf[:, 2048:] = x[2 * i + 1].T
        xcore.append(xf)
    return d, xcore


_LAST = {}


def kernel(**inputs):
    shared, xcore = _host_prep(inputs)
    nc = build_program()
    in_maps = []
    for i in range(8):
        m = dict(shared)
        m["xfm"] = xcore[i]
        in_maps.append(m)
    res = run_bass_kernel_spmd(nc, in_maps, core_ids=list(range(8)))
    _LAST["res"] = res
    out = np.empty((16, 2048, 1024), np.float32)
    for i in range(8):
        o = res.results[i]["ofm"]
        out[2 * i] = o[:, :2048].T
        out[2 * i + 1] = o[:, 2048:].T
    return out
